# version 03
# baseline (speedup 1.0000x reference)
import numpy as np
import ml_dtypes
import concourse.bass as bass
import concourse.mybir as mybir

F32 = mybir.dt.float32
BF16 = mybir.dt.bfloat16
AF = mybir.ActivationFunctionType
ALU = mybir.AluOpType
AX = mybir.AxisListType

H, DK, DV, D = 8, 64, 128, 1024
C = 128
EPS_LN = 1e-5
EPS_GN = 1e-5
EPS_RMS = 1e-6
EPS_ACT = 1e-5
KSCALE = DK ** (-0.5)
RND = 12582912.0


def host_constants():
    r = np.arange(128)
    tri_incl = (r[:, None] <= r[None, :]).astype(np.float32)
    tri_excl = (r[:, None] < r[None, :]).astype(np.float32)
    mlow = ((r[:, None] < 64).astype(np.float32)) * np.ones((1, 128), np.float32)
    c = {}
    c["ident"] = np.eye(128, dtype=np.float32)
    c["trim_in"] = (mlow - tri_incl)
    c["trim_iex"] = (tri_excl - mlow)
    c["w64b"] = (-(r < 64).astype(np.float32))[:, None] * np.ones((1, 64), np.float32)
    c["oh127"] = (r == 127).astype(np.float32)[:, None] * np.ones((1, 64), np.float32)
    pstack = np.zeros((16, 8 * 128), np.float32)
    for h in range(8):
        for s in range(16):
            pstack[s, h * 128 + s * 8 + h] = 1.0
    c["pstack"] = pstack
    dm = np.zeros((128, 128), np.float32)
    for h in range(8):
        for rr in range(16):
            for s in range(16):
                if rr <= s - 2:
                    dm[rr * 8 + h, s * 8 + h] = -1.0
    c["gi_mask"] = dm
    p16 = np.zeros((16, 8 * 128), np.float32)
    for i in range(8):
        for k in range(16):
            p16[k, i * 128 + 16 * i + k] = 1.0
    c["p16"] = p16
    psel = np.zeros((128, 8 * 16), np.float32)
    for h in range(8):
        for ss in range(16):
            psel[ss * 8 + h, h * 16 + ss] = 1.0
    c["psel"] = psel
    pslot = np.zeros((8, 16 * 128), np.float32)
    for ss in range(16):
        for h in range(8):
            pslot[h, ss * 128 + ss * 8 + h] = 1.0
    c["pslot"] = pslot
    c["negident"] = -np.eye(128, dtype=np.float32)
    repl = np.zeros((16, 128), np.float32)
    for rr in range(16):
        for h in range(8):
            repl[rr, rr * 8 + h] = 1.0
    c["repl16"] = repl
    c["mask_incl"] = -tri_incl
    return c


def host_prep(inputs):
    g = {}
    tr = lambda a: np.ascontiguousarray(np.asarray(a).T).astype(np.float32)
    g["wqt"] = tr(inputs["L0_attn_Wq"])
    g["wkt"] = tr(inputs["L0_attn_Wk"])
    g["wvt"] = tr(inputs["L0_attn_Wv"])
    g["wat"] = tr(inputs["L0_attn_Wa"])
    g["wgt"] = tr(inputs["L0_attn_Wg"])
    g["wbt"] = tr(inputs["L0_attn_Wb"])
    g["wot"] = tr(inputs["L0_attn_Wo"])
    g["wft"] = tr(inputs["L0_ffn_w_int8"])
    g["ba"] = np.asarray(inputs["L0_attn_ba"]).reshape(1, D).astype(np.float32)
    g["bb"] = np.asarray(inputs["L0_attn_bb"]).reshape(1, 8).astype(np.float32)
    tqk = np.concatenate([
        np.asarray(inputs["L0_attn_qc_w"]).reshape(512, 4),
        np.asarray(inputs["L0_attn_kc_w"]).reshape(512, 4)], axis=0).astype(np.float32)
    g["taps_qk"] = np.ascontiguousarray(tqk.reshape(16, 64, 4).transpose(1, 0, 2).reshape(64, 64))
    tv = np.asarray(inputs["L0_attn_vc_w"]).reshape(1024, 4).astype(np.float32)
    g["taps_v"] = np.ascontiguousarray(tv.reshape(8, 128, 4).transpose(1, 0, 2).reshape(128, 32))
    cqk = np.concatenate([
        np.asarray(inputs["L0_attn_qc_b"]).reshape(512),
        np.asarray(inputs["L0_attn_kc_b"]).reshape(512)]).astype(np.float32)
    g["cb_qk"] = np.ascontiguousarray(cqk.reshape(16, 64).T)
    g["cb_v"] = np.ascontiguousarray(
        np.asarray(inputs["L0_attn_vc_b"]).reshape(8, 128).T.astype(np.float32))
    g["lnw"] = np.asarray(inputs["L0_attn_ln_w"]).reshape(128).astype(np.float32)
    g["lnb"] = np.asarray(inputs["L0_attn_ln_b"]).reshape(128).astype(np.float32)
    g["fbias"] = np.asarray(inputs["L0_ffn_bias"]).reshape(D).astype(np.float32)
    g["wscale"] = np.asarray(inputs["L0_ffn_weight_scale"]).reshape(1).astype(np.float32)
    for k, v in host_constants().items():
        g[k] = v.astype(np.float32)
    return g


def build(T, dumps=()):
    assert T % C == 0
    NOUT = T // C
    nc = bass.Bass("TRN2", target_bir_lowering=False, debug=False)
    dr = {}
    dr["x"] = nc.dram_tensor("x", [T, D], F32, kind="ExternalInput").ap()
    for nm, shp in [("wqt", [D, 512]), ("wkt", [D, 512]), ("wvt", [D, D]),
                    ("wat", [D, D]), ("wgt", [D, D]), ("wbt", [D, 8]),
                    ("wot", [D, D]), ("wft", [D, D]), ("ba", [1, D]),
                    ("bb", [1, 8]), ("taps_qk", [64, 64]), ("taps_v", [128, 32]),
                    ("cb_qk", [64, 16]), ("cb_v", [128, 8]),
                    ("lnw", [128]), ("lnb", [128]), ("fbias", [D]), ("wscale", [1]),
                    ("ident", [128, 128]), ("trim_in", [128, 128]),
                    ("trim_iex", [128, 128]), ("w64b", [128, 64]),
                    ("oh127", [128, 64]), ("pstack", [16, 1024]),
                    ("gi_mask", [128, 128]), ("p16", [16, 1024]),
                    ("psel", [128, 128]), ("pslot", [8, 2048]),
                    ("negident", [128, 128]),
                    ("repl16", [16, 128]), ("mask_incl", [128, 128])]:
        dr[nm] = nc.dram_tensor(nm, shp, F32, kind="ExternalInput").ap()
    dr["out"] = nc.dram_tensor("out", [T, D], F32, kind="ExternalOutput").ap()
    dr["scr_db"] = nc.dram_tensor("scr_db", [128, 1024], BF16, kind="Internal").ap()
    dr["scr_gr"] = nc.dram_tensor("scr_gr", [128, 1024], BF16, kind="Internal").ap()
    dump_dr = {}
    for tag, _p, shp in dumps:
        dump_dr[tag] = nc.dram_tensor("dbg_" + tag, list(shp), F32,
                                      kind="ExternalOutput").ap()

    ctx = []
    def sb(name, shape, dt):
        t = nc.sbuf_tensor(name, list(shape), dt); ctx.append(t); return t.__enter__()
    def ps(name, shape, dt):
        t = nc.psum_tensor(name, list(shape), dt); ctx.append(t); return t.__enter__()

    w_q = sb("w_q", [128, 8 * 512], BF16)
    w_k = sb("w_k", [128, 8 * 512], BF16)
    w_v = sb("w_v", [128, 8 * 1024], BF16)
    w_a = sb("w_a", [128, 8 * 1024], BF16)
    w_g = sb("w_g", [128, 8 * 1024], BF16)
    w_b = sb("w_b", [128, 8 * 8], BF16)
    w_o = sb("w_o", [128, 8 * 1024], BF16)
    w_f = sb("w_f", [128, 8 * 1024], BF16)
    ba_r = sb("ba_r", [1, D], BF16)
    bb_r = sb("bb_r", [1, 8], BF16)
    ones1 = sb("ones1", [1, 128], BF16)
    taps_qk = sb("taps_qkt", [64, 64], F32)
    taps_v = sb("taps_vt", [128, 32], F32)
    cb_qk = sb("cb_qkt", [64, 16], F32)
    cb_v = sb("cb_vt", [128, 8], F32)
    ident = sb("identt", [128, 128], BF16)
    identf = sb("identft", [128, 128], F32)
    trim_in = sb("trim_int", [128, 128], F32)
    trim_iex = sb("trim_iext", [128, 128], F32)
    w64b = sb("w64bt", [128, 64], F32)
    oh127 = sb("oh127t", [128, 64], F32)
    pstack = sb("pstackt", [16, 1024], BF16)
    gi_mask = sb("gi_maskt", [128, 128], F32)
    p16 = sb("p16t", [16, 1024], BF16)
    psel = sb("pselt", [128, 128], BF16)
    pslot = sb("pslott", [8, 2048], BF16)
    negident = sb("negidentt", [128, 128], BF16)
    repl16 = sb("repl16t", [16, 128], BF16)
    mask_incl = sb("mask_inclt", [128, 128], F32)
    lnw_rep = sb("lnw_rep", [128, 1024], F32)
    lnb_rep = sb("lnb_rep", [128, 1024], F32)
    fb_rep = sb("fb_rep", [128, 1024], F32)
    ws_rep = sb("ws_rep", [128, 1], F32)
    cdiag4 = sb("cdiag4", [128, 4 * 128], BF16)

    x_t = sb("x_tt", [128, D], F32)
    scr = sb("scr", [128, D], F32)
    stat = sb("stat", [128, 16], F32)
    xn_tm = sb("xn_tm", [128, D], F32)
    xn_fm = sb("xn_fm", [128, 8 * 128], BF16)
    qraw_tm = sb("qraw_tm", [128, 512], BF16)
    kraw_tm = sb("kraw_tm", [128, 512], BF16)
    vraw_tm = sb("vraw_tm", [128, 1024], BF16)
    bz = sb("bz", [128, 8], F32)
    qr_fm = sb("qr_fm", [64, 8 * 131], BF16)
    kr_fm = sb("kr_fm", [64, 8 * 131], BF16)
    vr_fm = sb("vr_fm", [128, 8 * 131], BF16)
    q_fm = sb("q_fm", [64, 8 * 128], BF16)
    k_fm = sb("k_fm", [64, 8 * 128], BF16)
    v_fm = sb("v_fm", [128, 8 * 128], BF16)
    k_tm = sb("k_tm", [128, 512], BF16)
    v_tm = sb("v_tm", [128, 1024], BF16)
    g_tm = sb("g_tm", [128, 1024], BF16)
    beta = sb("beta", [128, 8], F32)
    gin = sb("gin", [128, 1024], F32)
    igex = sb("igex", [128, 1024], F32)
    vt_w = sb("vt_w", [128, 1024], BF16)
    dbw = sb("dbw", [128, 1024], BF16)
    sigm = sb("sigm", [64, 1024], F32)
    gram = sb("gram", [128, 8 * 128], BF16)
    atil = sb("atil", [128, 8 * 128], BF16)
    s_st = sb("s_st", [64, 8 * 128], F32)
    s_stb = sb("s_stb", [64, 8 * 128], BF16)
    stag = sb("stag", [16, 8 * 128], BF16)
    gpk = sb("gpk", [16, 128], BF16)
    acc_int = sb("acc_int", [128, 128], BF16)
    gi_t = sb("gi_t", [128, 128], BF16)
    u_int = sb("u_int", [128, 128], BF16)
    u_ring = sb("u_ring", [8, 16 * 128], BF16)
    db_hp = sb("db_hp", [8, 16 * 128], BF16)
    gram_sub = sb("gram_sub", [8, 16], BF16)
    uh16 = sb("uh16", [16, 256], BF16)
    t8 = sb("t8", [8, 128], F32)
    u_big = sb("u_big", [128, 8 * 128], BF16)
    o_tm = sb("o_tm", [128, 1024], F32)
    res1 = sb("res1", [128, 1024], F32)
    za = scr
    lp = o_tm
    ew = res1
    og_fm = xn_fm
    n_fm = v_fm
    carry_qk = sb("carry_qk", [64, 16 * 3], BF16)
    carry_v = sb("carry_v", [128, 8 * 3], BF16)

    ps_big = ps("ps_big", [128, 1024], F32)
    ps_a = ps("ps_a", [128, 512], F32)
    ps_ks = ps("ps_ks", [16, 512], F32)
    ps_tr = ps("ps_tr", [128, 1024], BF16)
    ps_int = ps("ps_int", [128, 512], F32)
    ps_ub = ps("ps_ub", [128, 1024], F32)

    sems = {}
    cnt = {}
    for n in ["pe", "dve", "sc", "gp", "ld", "st"]:
        t = nc.semaphore("s_" + n); ctx.append(t)
        sems[n] = t.__enter__(); cnt[n] = 0

    dump_list = list(dumps)

    with nc.Block() as block:
        SEC = {"pe": block.tensor, "dve": block.vector, "sc": block.scalar,
               "gp": block.gpsimd, "sy": block.sync}

        dirty = {"dve": set(), "sc": set(), "gp": set()}

        class HazardGuard:
            def __init__(self, e, eng):
                self._e = e
                self._d = dirty[eng]
            def __getattr__(self, name):
                f = getattr(self._e, name)
                if name in ("wait_ge", "nop", "drain") or not callable(f):
                    return f
                if name == "dma_start":
                    return f
                def g(*a, **k):
                    wr, rd = set(), set()
                    for i, x in enumerate(a):
                        if isinstance(x, bass.AP):
                            (wr if i == 0 else rd).add(x.tensor.name)
                    for kw, x in k.items():
                        if isinstance(x, bass.AP):
                            (wr if kw in ("out", "outs") else rd).add(x.tensor.name)
                    if rd & self._d:
                        self._e.drain()
                        self._d.clear()
                    res = f(*a, **k)
                    self._d |= wr
                    return res
                return g

        def G(eng, fn, waits=(), sem=None, inc=1):
            semn = sem or (eng if eng != "sy" else None)
            wl = [(n, v) for (n, v) in waits if v > 0]
            guard = eng in ("dve", "sc", "gp")
            def body(e):
                for (n, v) in wl:
                    e.wait_ge(sems[n], v)
                last = fn(HazardGuard(e, eng) if guard else e)
                if semn is not None and inc and last is not None:
                    last.then_inc(sems[semn], inc)
            SEC[eng](body)
            if semn is not None and inc:
                cnt[semn] += inc

        def W(*names):
            return [(n, cnt[n]) for n in names]

        def WALL():
            return [(n, cnt[n]) for n in ["pe", "dve", "sc", "gp", "ld"]]

        def DMA(eng, out, in_, waits=(), sem="ld"):
            G(eng, lambda e: e.dma_start(out=out, in_=in_), waits, sem=sem, inc=16)

        def dump(tag, ap_fn, cur_p=None):
            for (t, pp, shp) in dump_list:
                if t == tag and (cur_p is None or pp == cur_p):
                    DMA("gp", dump_dr[tag], ap_fn(), waits=WALL(), sem="st")
                    stc = [("st", cnt["st"])]
                    for eng in ["dve", "sc", "pe", "gp"]:
                        G(eng, lambda e: e.nop(), waits=stc, inc=0)

        for nm, dst, cast in [("wqt", w_q, 1), ("wkt", w_k, 1), ("wvt", w_v, 1),
                              ("wat", w_a, 1), ("wgt", w_g, 1), ("wot", w_o, 1),
                              ("wft", w_f, 1)]:
            ncols = dr[nm].shape[1]
            DMA("gp", dst[:].rearrange("p (c n) -> p c n", n=ncols),
                dr[nm].rearrange("(c p) n -> p c n", p=128))
        DMA("gp", w_b[:].rearrange("p (c n) -> p c n", n=8),
            dr["wbt"].rearrange("(c p) n -> p c n", p=128))
        DMA("gp", ba_r[:], dr["ba"])
        DMA("gp", bb_r[:], dr["bb"])
        DMA("gp", ident[:], dr["ident"])
        DMA("sy", identf[:], dr["ident"])
        DMA("gp", pstack[:], dr["pstack"])
        DMA("gp", p16[:], dr["p16"])
        DMA("gp", psel[:], dr["psel"])
        DMA("gp", pslot[:], dr["pslot"])
        DMA("gp", negident[:], dr["negident"])
        DMA("gp", repl16[:], dr["repl16"])
        DMA("sy", taps_qk[:], dr["taps_qk"])
        DMA("sy", taps_v[:], dr["taps_v"])
        DMA("sy", cb_qk[:], dr["cb_qk"])
        DMA("sy", cb_v[:], dr["cb_v"])
        DMA("sy", trim_in[:], dr["trim_in"])
        DMA("sy", trim_iex[:], dr["trim_iex"])
        DMA("sy", w64b[:], dr["w64b"])
        DMA("sy", oh127[:], dr["oh127"])
        DMA("sy", gi_mask[:], dr["gi_mask"])
        DMA("sy", mask_incl[:], dr["mask_incl"])
        DMA("sy", lnw_rep[:], dr["lnw"][None, None, :].broadcast_to([128, 8, 128]))
        DMA("sy", lnb_rep[:], dr["lnb"][None, None, :].broadcast_to([128, 8, 128]))
        DMA("sy", fb_rep[:], dr["fbias"][None, :].broadcast_to([128, 1024]))
        DMA("sy", ws_rep[:], dr["wscale"][None, :].broadcast_to([128, 1]))

        G("dve", lambda e: e.memset(ones1[:], 1.0))
        G("dve", lambda e: e.memset(s_st[:], 0.0))
        G("dve", lambda e: e.memset(s_stb[:], 0.0))
        G("dve", lambda e: e.memset(carry_qk[:], 0.0))
        G("dve", lambda e: e.memset(carry_v[:], 0.0))

        for p in range(NOUT):
            r0 = p * C
            DMA("sy", x_t[:], dr["x"][r0:r0 + C, :], waits=WALL())

            def ln_stats(e):
                e.reduce_sum(stat[:, 0:1], x_t[:], axis=AX.X)
                return e.reduce_sum(stat[:, 1:2], scr[:], axis=AX.X)
            G("sc", lambda e: e.square(scr[:], x_t[:]), waits=W("ld", "dve"))
            G("dve", ln_stats, waits=W("sc", "ld"))
            def ln_stats2(e):
                e.tensor_scalar(stat[:, 2:3], stat[:, 0:1], 1.0 / D, None, ALU.mult)
                e.tensor_scalar(stat[:, 3:4], stat[:, 1:2], 1.0 / D, None, ALU.mult)
                e.tensor_mul(stat[:, 4:5], stat[:, 2:3], stat[:, 2:3])
                e.tensor_sub(stat[:, 5:6], stat[:, 3:4], stat[:, 4:5])
                return e.tensor_scalar(stat[:, 5:6], stat[:, 5:6], EPS_LN, None, ALU.add)
            G("dve", ln_stats2)
            G("sc", lambda e: e.activation(stat[:, 6:7], stat[:, 5:6], AF.Ln), waits=W("dve"))
            G("sc", lambda e: e.activation(stat[:, 7:8], stat[:, 6:7], AF.Exp, scale=-0.5))
            G("dve", lambda e: e.tensor_scalar(stat[:, 8:9], stat[:, 2:3], -1.0, None, ALU.mult), waits=W("sc"))
            G("dve", lambda e: e.tensor_scalar(xn_tm[:], x_t[:], stat[:, 8:9], stat[:, 7:8], ALU.add, ALU.mult))

            for j in range(8):
                sl = slice(j * 128, (j + 1) * 128)
                G("pe", lambda e, sl=sl, j=j: e.transpose(ps_big[:, j * 128:(j + 1) * 128], xn_tm[:, sl], identf[:]),
                  waits=W("dve", "sc"))
                G("dve", lambda e, sl=sl, j=j: e.tensor_copy(xn_fm[:, sl], ps_big[:, j * 128:(j + 1) * 128]),
                  waits=W("pe"))
            dump("x_t", lambda: x_t[:], p)
            dump("xn_tm", lambda: xn_tm[:], p)
            dump("stat", lambda: stat[:], p)
            dump("xn_fm", lambda: xn_fm[:], p)

            def proj(dst_psum, w, wstride, nlo, nwid, bias_row=None):
                def f(e):
                    last = None
                    for kc in range(8):
                        last = e.matmul(dst_psum,
                                        xn_fm[:, kc * 128:(kc + 1) * 128],
                                        w[:, kc * wstride + nlo: kc * wstride + nlo + nwid],
                                        start=(kc == 0), stop=(kc == 7 and bias_row is None))
                    if bias_row is not None:
                        last = e.matmul(dst_psum, ones1[:], bias_row,
                                        start=False, stop=True)
                    return last
                return f
            for half in range(2):
                G("pe", proj(ps_big[:, half * 512:(half + 1) * 512], w_a, 1024, half * 512, 512,
                             ba_r[:, half * 512:(half + 1) * 512]), waits=W("sc", "dve"))
            G("dve", lambda e: e.tensor_copy(za[:], ps_big[:]), waits=W("pe"))
            G("pe", proj(ps_a[:, 0:8], w_b, 8, 0, 8, bb_r[:]), waits=W("dve"))
            G("dve", lambda e: e.tensor_copy(bz[:], ps_a[:, 0:8]), waits=W("pe"))
            G("pe", proj(ps_a[:, 0:512], w_q, 512, 0, 512), waits=W("dve"))
            G("dve", lambda e: e.tensor_copy(qraw_tm[:], ps_a[:, 0:512]), waits=W("pe"))
            G("pe", proj(ps_a[:, 0:512], w_k, 512, 0, 512), waits=W("dve"))
            G("dve", lambda e: e.tensor_copy(kraw_tm[:], ps_a[:, 0:512]), waits=W("pe"))
            for half in range(2):
                G("pe", proj(ps_big[:, half * 512:(half + 1) * 512], w_v, 1024, half * 512, 512),
                  waits=W("dve"))
            G("dve", lambda e: e.tensor_copy(vraw_tm[:], ps_big[:]), waits=W("pe"))
            for half in range(2):
                G("pe", proj(ps_big[:, half * 512:(half + 1) * 512], w_g, 1024, half * 512, 512),
                  waits=W("dve"))

            def rawtr64(src, dst, carry_t, carry_off):
                for j in range(8):
                    G("pe", lambda e, j=j: e.transpose(
                        ps_tr[0:64, (j % 8) * 128:((j % 8) + 1) * 128],
                        src[:, j * 64:(j + 1) * 64], ident[:]), waits=W("sc", "dve"))
                    G("dve", lambda e, j=j: e.tensor_copy(
                        dst[0:64, j * 131 + 3: j * 131 + 131],
                        ps_tr[0:64, (j % 8) * 128:((j % 8) + 1) * 128]), waits=W("pe"))
                G("dve", lambda e: e.tensor_copy(
                    dst[0:64, :].rearrange("p (s c) -> p s c", c=131)[:, :, 0:3],
                    carry_t[0:64, carry_off * 3:(carry_off + 8) * 3]
                        .rearrange("p (s c) -> p s c", c=3)), waits=W("sc"))
            def rawtr128(src, dst, carry_t):
                for j in range(8):
                    G("pe", lambda e, j=j: e.transpose(
                        ps_tr[:, (j % 8) * 128:((j % 8) + 1) * 128],
                        src[:, j * 128:(j + 1) * 128], ident[:]), waits=W("sc", "dve"))
                    G("dve", lambda e, j=j: e.tensor_copy(
                        dst[:, j * 131 + 3: j * 131 + 131],
                        ps_tr[:, (j % 8) * 128:((j % 8) + 1) * 128]), waits=W("pe"))
                G("dve", lambda e: e.tensor_copy(
                    dst[:].rearrange("p (s c) -> p s c", c=131)[:, :, 0:3],
                    carry_t[:, 0:24].rearrange("p (s c) -> p s c", c=3)), waits=W("sc"))
            rawtr64(qraw_tm, qr_fm, carry_qk, 0)
            rawtr64(kraw_tm, kr_fm, carry_qk, 8)
            rawtr128(vraw_tm, vr_fm, carry_v)

            plan = ([(qr_fm, q_fm, j, j, 64) for j in range(8)] +
                    [(kr_fm, k_fm, j, 8 + j, 64) for j in range(8)] +
                    [(vr_fm, v_fm, j, j, 128) for j in range(8)])
            for (srcf, dstf, j, gslot, pw) in plan:
                tapt = taps_qk if pw == 64 else taps_v
                cbt = cb_qk if pw == 64 else cb_v
                def mkdiag(e, gslot=gslot, pw=pw, tapt=tapt):
                    last = None
                    for t4 in range(4):
                        last = e.tensor_scalar(cdiag4[0:pw, t4 * 128:t4 * 128 + pw],
                                               ident[0:pw, 0:pw],
                                               tapt[0:pw, gslot * 4 + t4: gslot * 4 + t4 + 1],
                                               None, ALU.mult)
                    return last
                G("dve", mkdiag, waits=W("pe"))
                def cmms(e, srcf=srcf, j=j, pw=pw):
                    last = None
                    for t4 in range(4):
                        last = e.matmul(ps_int[0:pw, 0:128],
                                        cdiag4[0:pw, t4 * 128:t4 * 128 + pw],
                                        srcf[0:pw, j * 131 + t4: j * 131 + t4 + 128],
                                        start=(t4 == 0), stop=(t4 == 3))
                    return last
                G("pe", cmms, waits=W("sc", "dve"))
                G("sc", lambda e, dstf=dstf, j=j, gslot=gslot, pw=pw, cbt=cbt: e.activation(
                    dstf[0:pw, j * 128:(j + 1) * 128], ps_int[0:pw, 0:128], AF.Silu,
                    bias=cbt[0:pw, gslot:gslot + 1], scale=1.0), waits=W("pe"))
            G("dve", lambda e: e.tensor_scalar(k_fm[0:64, :], k_fm[0:64, :], KSCALE, None, ALU.mult),
              waits=W("sc"))
            def save_carry(e):
                e.tensor_copy(carry_qk[0:64, 0:24].rearrange("p (s c) -> p s c", c=3),
                              qr_fm[0:64, :].rearrange("p (s c) -> p s c", c=131)[:, :, 128:131])
                e.tensor_copy(carry_qk[0:64, 24:48].rearrange("p (s c) -> p s c", c=3),
                              kr_fm[0:64, :].rearrange("p (s c) -> p s c", c=131)[:, :, 128:131])
                return e.tensor_copy(carry_v[:, 0:24].rearrange("p (s c) -> p s c", c=3),
                              vr_fm[:].rearrange("p (s c) -> p s c", c=131)[:, :, 128:131])
            G("dve", save_carry, waits=W("sc"))
            dump("q_fm", lambda: q_fm[:], p)
            dump("v_fm", lambda: v_fm[:], p)

            G("sc", lambda e: e.activation(o_tm[:], ps_big[:], AF.Tanh, scale=0.5),
              waits=W("pe", "dve"))
            G("dve", lambda e: e.tensor_scalar(g_tm[:], o_tm[:], 1.0, 0.5, ALU.add, ALU.mult),
              waits=W("sc"))
            G("sc", lambda e: e.activation(bz[:, 0:8], bz[:, 0:8], AF.Tanh, scale=0.5),
              waits=W("dve"))
            G("dve", lambda e: e.tensor_scalar(beta[:], bz[:, 0:8], 1.0, 0.5, ALU.add, ALU.mult),
              waits=W("sc"))

            for j in range(8):
                G("pe", lambda e, j=j: e.transpose(ps_tr[:, (j % 8) * 64:(j % 8) * 64 + 64],
                                                   k_fm[0:64, j * 128:(j + 1) * 128], ident[0:64, 0:64]),
                  waits=W("sc", "dve"))
                G("dve", lambda e, j=j: e.tensor_copy(k_tm[:, j * 64:(j + 1) * 64],
                                              ps_tr[:, (j % 8) * 64:(j % 8) * 64 + 64]), waits=W("pe"))
            for j in range(8):
                G("pe", lambda e, j=j: e.transpose(ps_tr[:, (j % 8) * 128:((j % 8) + 1) * 128],
                                                   v_fm[:, j * 128:(j + 1) * 128], ident[:]),
                  waits=W("sc", "dve"))
                G("dve", lambda e, j=j: e.tensor_copy(v_tm[:, j * 128:(j + 1) * 128],
                                              ps_tr[:, (j % 8) * 128:((j % 8) + 1) * 128]),
                  waits=W("pe"))

            G("sc", lambda e: e.activation(ew[:], za[:], AF.Exp, scale=-1.0), waits=W("dve"))
            G("dve", lambda e: e.tensor_scalar(ew[:], ew[:], 1.0, None, ALU.add), waits=W("sc"))
            G("sc", lambda e: e.activation(lp[:], ew[:], AF.Ln), waits=W("dve"))
            G("dve", lambda e: e.tensor_tensor(
                dbw[:].rearrange("p (a b) -> p a b", b=128),
                ew[:].rearrange("p (a b) -> p a b", b=128),
                beta[:].unsqueeze(2).broadcast_to([128, 8, 128]),
                ALU.mult), waits=W("sc"))
            if p > 0:
                def sigmm(e):
                    last = None
                    for half in range(2):
                        last = e.matmul(ps_big[0:64, half * 512:(half + 1) * 512],
                                        w64b[:, 0:64], lp[:, half * 512:(half + 1) * 512],
                                        start=True, stop=True)
                    return last
                G("pe", sigmm, waits=W("sc", "dve"))
                G("sc", lambda e: e.activation(sigm[:], ps_big[0:64, :], AF.Exp), waits=W("pe"))
                G("dve", lambda e: e.tensor_mul(s_st[:], s_st[:], sigm[:]), waits=W("sc"))
                G("dve", lambda e: e.tensor_copy(s_stb[:], s_st[:]))
            def trimm(mat):
                def f(e):
                    last = None
                    for half in range(2):
                        last = e.matmul(ps_big[:, half * 512:(half + 1) * 512],
                                        mat[:], lp[:, half * 512:(half + 1) * 512],
                                        start=True, stop=True)
                    return last
                return f
            G("pe", trimm(trim_in), waits=W("dve", "sc"))
            G("sc", lambda e: e.activation(gin[:], ps_big[:], AF.Exp), waits=W("pe"))
            G("pe", trimm(trim_iex), waits=W("sc"))
            G("sc", lambda e: e.activation(igex[:], ps_big[:], AF.Exp), waits=W("pe"))
            G("dve", lambda e: e.tensor_mul(vt_w[:], v_tm[:], igex[:]), waits=W("sc"))
            dump("gin", lambda: gin[:], p)
            dump("igex", lambda: igex[:], p)

            def hk(hh):
                return k_fm[0:64, hh * 128:(hh + 1) * 128]
            def hq(hh):
                return q_fm[0:64, hh * 128:(hh + 1) * 128]
            for hh in range(8):
                G("pe", lambda e, hh=hh: e.matmul(ps_a[:, 0:128], hk(hh), hk(hh),
                                                  start=True, stop=True), waits=W("dve", "sc"))
                G("dve", lambda e, hh=hh: e.tensor_scalar(
                    gram[:, hh * 128:(hh + 1) * 128], ps_a[:, 0:128], -1.0, None, ALU.mult),
                  waits=W("pe"))
                G("pe", lambda e, hh=hh: e.matmul(ps_a[:, 128:256], hk(hh), hq(hh),
                                                  start=True, stop=True), waits=W("dve"))
                G("dve", lambda e, hh=hh: e.tensor_mul(
                    atil[:, hh * 128:(hh + 1) * 128], ps_a[:, 128:256], mask_incl[:]),
                  waits=W("pe"))
            DMA("gp", dr["scr_gr"], gram[:], waits=W("dve"))
            def qs0(e):
                last = None
                for hh in range(8):
                    last = e.matmul(ps_big[:, hh * 128:(hh + 1) * 128], hq(hh),
                                    s_stb[:, hh * 128:(hh + 1) * 128],
                                    start=(hh % 4 == 0), stop=False)
                return last
            G("pe", qs0, waits=W("dve", "sc"))

            DMA("gp", dr["scr_db"], dbw[:], waits=W("dve"))

            for i16 in range(8):
                t0 = i16 * 16
                DMA("gp", db_hp[:].rearrange("h (s v) -> h s v", v=128),
                    dr["scr_db"][t0:t0 + 16, :].rearrange("s (h v) -> h s v", v=128),
                    waits=W("dve", "ld"))
                if t0 > 0 or True:
                    from concourse.bass import AP as _AP
                    src = _AP(dr["scr_gr"].tensor,
                              (t0 + 1) * 1024 + t0,
                              [[128, 8], [1025, 15]])
                    def _gsub(e, src=src):
                        return e.dma_start(out=gram_sub[:, 1:16], in_=src,
                                           allow_slow_non_contiguous=True)
                    G("gp", _gsub, waits=W("dve", "pe", "ld"), sem="ld", inc=16)
                for half in range(2):
                    hs = range(half * 4, half * 4 + 4)
                    def ksmms(e, hs=hs, i16=i16, t0=t0, half=half):
                        last = None
                        first = True
                        for hh in hs:
                            c0 = (hh % 4) * 128
                            last = e.matmul(ps_ks[0:16, c0:c0 + 128],
                                            hk(hh)[:, t0:t0 + 16],
                                            s_stb[:, hh * 128:(hh + 1) * 128],
                                            start=first, stop=False)
                            first = False
                            if i16 > 0:
                                last = e.matmul(ps_ks[0:16, c0:c0 + 128],
                                                gram[0:t0, hh * 128 + t0: hh * 128 + t0 + 16],
                                                u_big[0:t0, hh * 128:(hh + 1) * 128],
                                                start=False, stop=False)
                        last = e.matmul(ps_ks[0:16, 0:512],
                                        negident[:, t0:t0 + 16],
                                        vt_w[:, half * 512:(half + 1) * 512],
                                        start=False, stop=True)
                        return last
                    G("pe", ksmms, waits=W("dve", "sc"))
                    G("dve", lambda e, half=half: e.tensor_copy(
                        stag[:, half * 512:(half + 1) * 512], ps_ks[0:16, 0:512]),
                      waits=W("pe"))
                def permin(e):
                    last = None
                    for hh in range(8):
                        last = e.matmul(ps_int[:, 128:256], pstack[:, hh * 128:(hh + 1) * 128],
                                        stag[:, hh * 128:(hh + 1) * 128],
                                        start=(hh == 0), stop=(hh == 7))
                    return last
                G("pe", permin, waits=W("dve"))
                G("dve", lambda e: e.tensor_copy(acc_int[:], ps_int[:, 128:256]), waits=W("pe"))
                def gpkmm(e, t0=t0):
                    last = None
                    for hh in range(8):
                        last = e.matmul(ps_ks[0:16, hh * 16:(hh + 1) * 16],
                                        hk(hh)[:, t0:t0 + 16], hk(hh)[:, t0:t0 + 16],
                                        start=True, stop=True)
                    return last
                G("pe", gpkmm, waits=W("dve"))
                G("dve", lambda e: e.tensor_copy(gpk[:], ps_ks[0:16, 0:128]), waits=W("pe"))
                G("pe", lambda e: e.matmul(ps_int[:, 256:384], repl16[:], gpk[:],
                                           start=True, stop=True), waits=W("dve"))
                G("dve", lambda e: e.tensor_mul(
                    gi_t[:].rearrange("p (s h) -> p s h", h=8),
                    ps_int[:, 256:384].rearrange("p (h s) -> p s h", h=8),
                    gi_mask[:].rearrange("p (s h) -> p s h", h=8)), waits=W("pe"))
                G("dve", lambda e: e.memset(u_int[:], 0.0))
                for s16 in range(16):
                    sl = (s16 % 2) * 128
                    def pegrp(e, s16=s16, sl=sl):
                        if s16 >= 1:
                            e.matmul(ps_int[:, 0:128],
                                     pslot[:, (s16 - 1) * 128:s16 * 128],
                                     u_ring[:, (s16 - 1) * 128:s16 * 128],
                                     start=(s16 == 1), stop=False)
                        last = e.matmul(ps_a[0:8, sl:sl + 128],
                                        ident[:, 8 * s16:8 * s16 + 8], acc_int[:],
                                        start=True, stop=(s16 < 2))
                        if s16 >= 2:
                            last = e.matmul(ps_a[0:8, sl:sl + 128],
                                            gi_t[:, 8 * s16:8 * s16 + 8], u_int[:],
                                            start=False, stop=True)
                        return last
                    G("pe", pegrp, waits=W("dve"))
                    def dvegrp(e, s16=s16, sl=sl):
                        if s16 >= 1:
                            e.tensor_copy(u_int[:], ps_int[:, 0:128])
                        if s16 == 0:
                            return e.tensor_mul(
                                u_ring[:, 0:128], ps_a[0:8, sl:sl + 128], db_hp[:, 0:128])
                        e.scalar_tensor_tensor(
                            t8[:], u_ring[:, (s16 - 1) * 128:s16 * 128],
                            gram_sub[:, s16:s16 + 1],
                            ps_a[0:8, sl:sl + 128], ALU.mult, ALU.add)
                        return e.tensor_mul(
                            u_ring[:, s16 * 128:(s16 + 1) * 128],
                            t8[:], db_hp[:, s16 * 128:(s16 + 1) * 128])
                    G("dve", dvegrp, waits=W("pe", "ld"))
                G("pe", lambda e: e.matmul(
                    ps_int[:, 0:128], pslot[:, 15 * 128:16 * 128],
                    u_ring[:, 15 * 128:16 * 128],
                    start=False, stop=True), waits=W("dve"))
                G("dve", lambda e: e.tensor_copy(u_int[:], ps_int[:, 0:128]),
                  waits=W("pe"))
                for hh in range(9):
                    def pe_ub(e, hh=hh, i16=i16):
                        last = None
                        if hh >= 1:
                            last = e.matmul(
                                ps_ub[:, (hh - 1) * 128:hh * 128],
                                p16[:, i16 * 128:(i16 + 1) * 128],
                                uh16[0:16, ((hh - 1) % 2) * 128:((hh - 1) % 2) * 128 + 128],
                                start=(i16 == 0 and (hh - 1) % 4 == 0), stop=(i16 == 7))
                        if hh < 8:
                            last = e.matmul(
                                ps_int[0:16, 128 + (hh % 2) * 128:256 + (hh % 2) * 128],
                                psel[:, hh * 16:(hh + 1) * 16], u_int[:],
                                start=True, stop=True)
                        return last
                    G("pe", pe_ub, waits=W("dve"))
                    if hh < 8:
                        G("dve", lambda e, hh=hh: e.tensor_copy(
                            uh16[0:16, (hh % 2) * 128:(hh % 2) * 128 + 128],
                            ps_int[0:16, 128 + (hh % 2) * 128:256 + (hh % 2) * 128]),
                          waits=W("pe"))
                G("dve", lambda e: e.tensor_copy(u_big[:], ps_ub[:]), waits=W("pe"))

            dump("u_big", lambda: u_big[:], p)
            def ocouple(e):
                last = None
                for hh in range(8):
                    last = e.matmul(ps_big[:, hh * 128:(hh + 1) * 128],
                                    atil[:, hh * 128:(hh + 1) * 128],
                                    u_big[:, hh * 128:(hh + 1) * 128],
                                    start=False, stop=True)
                return last
            G("pe", ocouple, waits=W("dve", "sc"))
            G("dve", lambda e: e.tensor_mul(o_tm[:], gin[:], ps_big[:]), waits=W("pe"))
            dump("o_tm", lambda: o_tm[:], p)

            def repl127(e):
                last = None
                for half in range(2):
                    last = e.matmul(ps_big[0:64, half * 512:(half + 1) * 512],
                                    oh127[:, 0:64], gin[:, half * 512:(half + 1) * 512],
                                    start=True, stop=True)
                return last
            G("pe", repl127, waits=W("dve"))
            G("dve", lambda e: e.tensor_copy(sigm[:], ps_big[0:64, :]), waits=W("pe"))
            for hp in range(2):
                def wmms(e, hp=hp):
                    last = None
                    for hh in range(4 * hp, 4 * hp + 4):
                        last = e.matmul(ps_a[0:64, (hh % 4) * 128:((hh % 4) + 1) * 128],
                                        k_tm[:, hh * 64:(hh + 1) * 64],
                                        u_big[:, hh * 128:(hh + 1) * 128],
                                        start=True, stop=True)
                    return last
                G("pe", wmms, waits=W("dve", "sc"))
                sl = slice(hp * 512, (hp + 1) * 512)
                G("dve", lambda e, sl=sl: e.tensor_sub(s_st[:, sl], s_st[:, sl],
                                                       ps_a[0:64, 0:512]), waits=W("pe"))
            G("dve", lambda e: e.tensor_mul(s_st[:], s_st[:], sigm[:]))
            G("dve", lambda e: e.tensor_copy(s_stb[:], s_st[:]))
            dump("s_st", lambda: s_st[:], p)

            G("sc", lambda e: e.square(scr[:], o_tm[:]), waits=W("dve"))
            def gstats(e):
                e.reduce_sum(stat[:, 0:8], o_tm[:].rearrange("p (a b) -> p a b", b=128), axis=AX.X)
                return e.reduce_sum(stat[:, 8:16], scr[:].rearrange("p (a b) -> p a b", b=128), axis=AX.X)
            G("dve", gstats, waits=W("sc"))
            def gstats2(e):
                e.tensor_scalar(stat[:, 0:8], stat[:, 0:8], 1.0 / DV, None, ALU.mult)
                e.tensor_scalar(stat[:, 8:16], stat[:, 8:16], 1.0 / DV, None, ALU.mult)
                e.tensor_mul(scr[:, 0:8], stat[:, 0:8], stat[:, 0:8])
                e.tensor_sub(stat[:, 8:16], stat[:, 8:16], scr[:, 0:8])
                return e.tensor_scalar(stat[:, 8:16], stat[:, 8:16], EPS_GN, None, ALU.add)
            G("dve", gstats2)
            G("sc", lambda e: e.activation(stat[:, 8:16], stat[:, 8:16], AF.Ln), waits=W("dve"))
            G("sc", lambda e: e.activation(stat[:, 8:16], stat[:, 8:16], AF.Exp, scale=-0.5))
            def onorm(e):
                o3 = o_tm[:].rearrange("p (a b) -> p a b", b=128)
                mu = stat[:, 0:8].unsqueeze(2).broadcast_to([128, 8, 128])
                rs = stat[:, 8:16].unsqueeze(2).broadcast_to([128, 8, 128])
                e.tensor_sub(o3, o3, mu)
                e.tensor_mul(o3, o3, rs)
                e.tensor_mul(o_tm[:], o_tm[:], lnw_rep[:])
                e.tensor_add(o_tm[:], o_tm[:], lnb_rep[:])
                return e.tensor_mul(o_tm[:], o_tm[:], g_tm[:])
            G("dve", onorm, waits=W("sc"))
            for j in range(8):
                G("pe", lambda e, j=j: e.transpose(ps_big[:, j * 128:(j + 1) * 128],
                                                   o_tm[:, j * 128:(j + 1) * 128], identf[:]),
                  waits=W("dve", "sc"))
                G("dve", lambda e, j=j: e.tensor_copy(og_fm[:, j * 128:(j + 1) * 128],
                                              ps_big[:, j * 128:(j + 1) * 128]), waits=W("pe"))
            def womms(e):
                last = None
                for half in range(2):
                    for kc in range(8):
                        last = e.matmul(ps_big[:, half * 512:(half + 1) * 512],
                                        og_fm[:, kc * 128:(kc + 1) * 128],
                                        w_o[:, kc * 1024 + half * 512: kc * 1024 + half * 512 + 512],
                                        start=(kc == 0), stop=(kc == 7))
                return last
            G("pe", womms, waits=W("sc", "dve"))
            G("dve", lambda e: e.tensor_add(res1[:], x_t[:], ps_big[:]), waits=W("pe"))
            G("dve", lambda e: e.tensor_add(res1[:], res1[:], fb_rep[:]))

            G("sc", lambda e: e.square(scr[:], xn_tm[:]), waits=W("dve"))
            G("dve", lambda e: e.reduce_sum(stat[:, 0:1], scr[:], axis=AX.X), waits=W("sc"))
            def rmsst(e):
                e.tensor_scalar(stat[:, 1:2], stat[:, 0:1], 1.0 / D, None, ALU.mult)
                return e.tensor_scalar(stat[:, 1:2], stat[:, 1:2], EPS_RMS, None, ALU.add)
            G("dve", rmsst)
            G("sc", lambda e: e.activation(stat[:, 2:3], stat[:, 1:2], AF.Ln), waits=W("dve"))
            G("sc", lambda e: e.activation(stat[:, 3:4], stat[:, 2:3], AF.Exp, scale=-0.5))
            G("dve", lambda e: e.tensor_scalar(scr[:], xn_tm[:], stat[:, 3:4], None, ALU.mult),
              waits=W("sc"))
            G("dve", lambda e: e.reduce_max(stat[:, 4:5], scr[:], axis=AX.X,
                                            apply_absolute_value=True))
            G("dve", lambda e: e.tensor_scalar(stat[:, 4:5], stat[:, 4:5], EPS_ACT, None, ALU.max))
            G("sc", lambda e: e.activation(stat[:, 5:6], stat[:, 4:5], AF.Ln), waits=W("dve"))
            G("sc", lambda e: e.activation(stat[:, 6:7], stat[:, 5:6], AF.Exp, scale=-1.0))
            G("dve", lambda e: e.tensor_scalar(stat[:, 7:8], stat[:, 6:7], 127.0, None, ALU.mult),
              waits=W("sc"))
            G("dve", lambda e: e.tensor_scalar(scr[:], scr[:], stat[:, 7:8], RND, ALU.mult, ALU.add))
            G("dve", lambda e: e.tensor_scalar(scr[:], scr[:], RND, None, ALU.subtract))
            for j in range(8):
                G("pe", lambda e, j=j: e.transpose(ps_big[:, j * 128:(j + 1) * 128],
                                                   scr[:, j * 128:(j + 1) * 128], identf[:]),
                  waits=W("dve", "sc"))
                G("dve", lambda e, j=j: e.tensor_copy(n_fm[:, j * 128:(j + 1) * 128],
                                              ps_big[:, j * 128:(j + 1) * 128]), waits=W("pe"))
            def ffnmms(e):
                last = None
                for half in range(2):
                    for kc in range(8):
                        last = e.matmul(ps_big[:, half * 512:(half + 1) * 512],
                                        n_fm[:, kc * 128:(kc + 1) * 128],
                                        w_f[:, kc * 1024 + half * 512: kc * 1024 + half * 512 + 512],
                                        start=(kc == 0), stop=(kc == 7))
                return last
            G("pe", ffnmms, waits=W("sc", "dve"))
            def oscale(e):
                e.tensor_mul(stat[:, 9:10], stat[:, 4:5], ws_rep[:])
                return e.tensor_scalar(stat[:, 9:10], stat[:, 9:10], 1.0 / 127.0, None, ALU.mult)
            G("dve", oscale)
            G("dve", lambda e: e.scalar_tensor_tensor(res1[:], ps_big[:], stat[:, 9:10],
                                                      res1[:], ALU.mult, ALU.add), waits=W("pe"))
            DMA("sy", dr["out"][r0:r0 + C, :], res1[:], waits=WALL(), sem="st")
            dump("out_rows", lambda: res1[:], p)

        G("sy", lambda e: e.wait_ge(sems["st"], cnt["st"]) or None, waits=(), sem=None)

    for t in reversed(ctx):
        t.__exit__(None, None, None)
    return nc





_BUILD_CACHE = {}

def kernel(**inputs):
    import numpy as np
    from concourse.bass_utils import run_bass_kernel_spmd
    T = 8192
    x_full = np.asarray(inputs["L0_attn_x"])
    B = x_full.shape[0]
    g = host_prep(inputs)
    if T not in _BUILD_CACHE:
        _BUILD_CACHE[T] = build(T)
    nc = _BUILD_CACHE[T]
    in_maps = []
    for c in range(8):
        im = dict(g)
        im["x"] = np.ascontiguousarray(x_full[c % B])
        in_maps.append(im)
    res = run_bass_kernel_spmd(nc, in_maps, core_ids=list(range(8)))
    out = np.stack([np.asarray(res.results[b]["out"]) for b in range(B)], axis=0)
    return out.astype(np.float32)


# version 04
# speedup vs baseline: 1.1500x; 1.1500x over previous
import numpy as np
import ml_dtypes
import concourse.bass as bass
import concourse.mybir as mybir

F32 = mybir.dt.float32
BF16 = mybir.dt.bfloat16
AF = mybir.ActivationFunctionType
ALU = mybir.AluOpType
AX = mybir.AxisListType

H, DK, DV, D = 8, 64, 128, 1024
C = 128
EPS_LN = 1e-5
EPS_GN = 1e-5
EPS_RMS = 1e-6
EPS_ACT = 1e-5
KSCALE = DK ** (-0.5)
RND = 12582912.0


def host_constants():
    r = np.arange(128)
    tri_incl = (r[:, None] <= r[None, :]).astype(np.float32)
    tri_excl = (r[:, None] < r[None, :]).astype(np.float32)
    mlow = ((r[:, None] < 64).astype(np.float32)) * np.ones((1, 128), np.float32)
    c = {}
    c["ident"] = np.eye(128, dtype=np.float32)
    c["trim_in"] = (mlow - tri_incl)
    c["trim_iex"] = (tri_excl - mlow)
    c["w64b"] = (-(r < 64).astype(np.float32))[:, None] * np.ones((1, 64), np.float32)
    c["oh127"] = (r == 127).astype(np.float32)[:, None] * np.ones((1, 64), np.float32)
    pstack = np.zeros((16, 8 * 128), np.float32)
    for h in range(8):
        for s in range(16):
            pstack[s, h * 128 + s * 8 + h] = 1.0
    c["pstack"] = pstack
    dm = np.zeros((128, 128), np.float32)
    for h in range(8):
        for rr in range(16):
            for s in range(16):
                if rr <= s - 2:
                    dm[rr * 8 + h, s * 8 + h] = -1.0
    c["gi_mask"] = dm
    p16 = np.zeros((16, 8 * 128), np.float32)
    for i in range(8):
        for k in range(16):
            p16[k, i * 128 + 16 * i + k] = 1.0
    c["p16"] = p16
    psel = np.zeros((128, 8 * 16), np.float32)
    for h in range(8):
        for ss in range(16):
            psel[ss * 8 + h, h * 16 + ss] = 1.0
    c["psel"] = psel
    pslot = np.zeros((8, 16 * 128), np.float32)
    for ss in range(16):
        for h in range(8):
            pslot[h, ss * 128 + ss * 8 + h] = 1.0
    c["pslot"] = pslot
    c["negident"] = -np.eye(128, dtype=np.float32)
    repl = np.zeros((16, 128), np.float32)
    for rr in range(16):
        for h in range(8):
            repl[rr, rr * 8 + h] = 1.0
    c["repl16"] = repl
    c["mask_incl"] = -tri_incl
    return c


def host_prep(inputs):
    g = {}
    tr = lambda a: np.ascontiguousarray(np.asarray(a).T).astype(np.float32)
    g["wqt"] = tr(inputs["L0_attn_Wq"])
    g["wkt"] = tr(inputs["L0_attn_Wk"])
    g["wvt"] = tr(inputs["L0_attn_Wv"])
    g["wat"] = tr(inputs["L0_attn_Wa"])
    g["wgt"] = tr(inputs["L0_attn_Wg"])
    g["wbt"] = tr(inputs["L0_attn_Wb"])
    g["wot"] = tr(inputs["L0_attn_Wo"])
    g["wft"] = tr(inputs["L0_ffn_w_int8"])
    g["ba"] = np.asarray(inputs["L0_attn_ba"]).reshape(1, D).astype(np.float32)
    g["bb"] = np.asarray(inputs["L0_attn_bb"]).reshape(1, 8).astype(np.float32)
    tqk = np.concatenate([
        np.asarray(inputs["L0_attn_qc_w"]).reshape(512, 4),
        np.asarray(inputs["L0_attn_kc_w"]).reshape(512, 4)], axis=0).astype(np.float32)
    g["taps_qk"] = np.ascontiguousarray(tqk.reshape(16, 64, 4).transpose(1, 0, 2).reshape(64, 64))
    tv = np.asarray(inputs["L0_attn_vc_w"]).reshape(1024, 4).astype(np.float32)
    g["taps_v"] = np.ascontiguousarray(tv.reshape(8, 128, 4).transpose(1, 0, 2).reshape(128, 32))
    cqk = np.concatenate([
        np.asarray(inputs["L0_attn_qc_b"]).reshape(512),
        np.asarray(inputs["L0_attn_kc_b"]).reshape(512)]).astype(np.float32)
    g["cb_qk"] = np.ascontiguousarray(cqk.reshape(16, 64).T)
    g["cb_v"] = np.ascontiguousarray(
        np.asarray(inputs["L0_attn_vc_b"]).reshape(8, 128).T.astype(np.float32))
    g["lnw"] = np.asarray(inputs["L0_attn_ln_w"]).reshape(128).astype(np.float32)
    g["lnb"] = np.asarray(inputs["L0_attn_ln_b"]).reshape(128).astype(np.float32)
    g["fbias"] = np.asarray(inputs["L0_ffn_bias"]).reshape(D).astype(np.float32)
    g["wscale"] = np.asarray(inputs["L0_ffn_weight_scale"]).reshape(1).astype(np.float32)
    for k, v in host_constants().items():
        g[k] = v.astype(np.float32)
    return g


def build(T, dumps=()):
    assert T % C == 0
    NOUT = T // C
    nc = bass.Bass("TRN2", target_bir_lowering=False, debug=False)
    dr = {}
    dr["x"] = nc.dram_tensor("x", [T, D], F32, kind="ExternalInput").ap()
    for nm, shp in [("wqt", [D, 512]), ("wkt", [D, 512]), ("wvt", [D, D]),
                    ("wat", [D, D]), ("wgt", [D, D]), ("wbt", [D, 8]),
                    ("wot", [D, D]), ("wft", [D, D]), ("ba", [1, D]),
                    ("bb", [1, 8]), ("taps_qk", [64, 64]), ("taps_v", [128, 32]),
                    ("cb_qk", [64, 16]), ("cb_v", [128, 8]),
                    ("lnw", [128]), ("lnb", [128]), ("fbias", [D]), ("wscale", [1]),
                    ("ident", [128, 128]), ("trim_in", [128, 128]),
                    ("trim_iex", [128, 128]), ("w64b", [128, 64]),
                    ("oh127", [128, 64]), ("pstack", [16, 1024]),
                    ("gi_mask", [128, 128]), ("p16", [16, 1024]),
                    ("psel", [128, 128]), ("pslot", [8, 2048]),
                    ("negident", [128, 128]),
                    ("repl16", [16, 128]), ("mask_incl", [128, 128])]:
        dr[nm] = nc.dram_tensor(nm, shp, F32, kind="ExternalInput").ap()
    dr["out"] = nc.dram_tensor("out", [T, D], F32, kind="ExternalOutput").ap()
    dr["scr_db"] = nc.dram_tensor("scr_db", [128, 1024], BF16, kind="Internal").ap()
    dr["scr_gr"] = nc.dram_tensor("scr_gr", [128, 1024], BF16, kind="Internal").ap()
    dump_dr = {}
    for tag, _p, shp in dumps:
        dump_dr[tag] = nc.dram_tensor("dbg_" + tag, list(shp), F32,
                                      kind="ExternalOutput").ap()

    ctx = []
    def sb(name, shape, dt):
        t = nc.sbuf_tensor(name, list(shape), dt); ctx.append(t); return t.__enter__()
    def ps(name, shape, dt):
        t = nc.psum_tensor(name, list(shape), dt); ctx.append(t); return t.__enter__()

    w_q = sb("w_q", [128, 8 * 512], BF16)
    w_k = sb("w_k", [128, 8 * 512], BF16)
    w_v = sb("w_v", [128, 8 * 1024], BF16)
    w_a = sb("w_a", [128, 8 * 1024], BF16)
    w_g = sb("w_g", [128, 8 * 1024], BF16)
    w_b = sb("w_b", [128, 8 * 8], BF16)
    w_o = sb("w_o", [128, 8 * 1024], BF16)
    w_f = sb("w_f", [128, 8 * 1024], BF16)
    ba_r = sb("ba_r", [1, D], BF16)
    bb_r = sb("bb_r", [1, 8], BF16)
    ones1 = sb("ones1", [1, 128], BF16)
    taps_qk = sb("taps_qkt", [64, 64], F32)
    taps_v = sb("taps_vt", [128, 32], F32)
    cb_qk = sb("cb_qkt", [64, 16], F32)
    cb_v = sb("cb_vt", [128, 8], F32)
    ident = sb("identt", [128, 128], BF16)
    identf = sb("identft", [128, 128], F32)
    trim_in = sb("trim_int", [128, 128], F32)
    trim_iex = sb("trim_iext", [128, 128], F32)
    w64b = sb("w64bt", [128, 64], F32)
    oh127 = sb("oh127t", [128, 64], F32)
    pstack = sb("pstackt", [16, 1024], BF16)
    gi_mask = sb("gi_maskt", [128, 128], F32)
    p16 = sb("p16t", [16, 1024], BF16)
    psel = sb("pselt", [128, 128], BF16)
    pslot = sb("pslott", [8, 2048], BF16)
    negident = sb("negidentt", [128, 128], BF16)
    repl16 = sb("repl16t", [16, 128], BF16)
    mask_incl = sb("mask_inclt", [128, 128], F32)
    lnw_rep = sb("lnw_rep", [128, 128], F32)
    lnb_rep = sb("lnb_rep", [128, 128], F32)
    fb_rep = sb("fb_rep", [128, 1024], F32)
    ws_rep = sb("ws_rep", [128, 1], F32)
    scl = sb("scl", [128, 2 * 1024], BF16)

    x_t = sb("x_tt", [128, D], F32)
    scr = sb("scr", [128, D], F32)
    stat = sb("stat", [128, 16], F32)
    xn_tm = sb("xn_tm", [128, D], F32)
    xn_fm = sb("xn_fm", [128, 8 * 128], BF16)
    qraw_tm = sb("qraw_tm", [128, 512], BF16)
    kraw_tm = sb("kraw_tm", [128, 512], BF16)
    vraw_tm = sb("vraw_tm", [128, 1024], BF16)
    bz = sb("bz", [128, 8], F32)
    qr_fm = sb("qr_fm", [64, 8 * 131], BF16)
    kr_fm = sb("kr_fm", [64, 8 * 131], BF16)
    vr_fm = sb("vr_fm", [128, 8 * 131], BF16)
    q_fm = sb("q_fm", [64, 8 * 128], BF16)
    k_fm = sb("k_fm", [64, 8 * 128], BF16)
    v_fm = sb("v_fm", [128, 8 * 128], BF16)
    k_tm = sb("k_tm", [128, 512], BF16)
    v_tm = sb("v_tm", [128, 1024], BF16)
    g_tm = sb("g_tm", [128, 1024], BF16)
    beta = sb("beta", [128, 8], F32)
    gin = sb("gin", [128, 1024], F32)
    igex = sb("igex", [128, 1024], F32)
    vt_w = sb("vt_w", [128, 1024], BF16)
    dbw = sb("dbw", [128, 1024], BF16)
    sigm = sb("sigm", [64, 1024], F32)
    gram = sb("gram", [128, 8 * 128], BF16)
    atil = sb("atil", [128, 8 * 128], BF16)
    s_st = sb("s_st", [64, 8 * 128], F32)
    s_stb = sb("s_stb", [64, 8 * 128], BF16)
    stag = sb("stag", [16, 8 * 128], BF16)
    gpk = sb("gpk", [16, 128], BF16)
    acc_int = sb("acc_int", [128, 128], BF16)
    gi_t = sb("gi_t", [128, 128], BF16)
    u_int = sb("u_int", [128, 128], BF16)
    u_ring = sb("u_ring", [8, 16 * 128], BF16)
    db_hp = sb("db_hp", [8, 16 * 128], BF16)
    gram_sub = sb("gram_sub", [8, 16], BF16)
    uh16 = sb("uh16", [16, 256], BF16)
    t8 = sb("t8", [8, 128], F32)
    u_big = sb("u_big", [128, 8 * 128], BF16)
    o_tm = sb("o_tm", [128, 1024], F32)
    res1 = sb("res1", [128, 1024], F32)
    za = scr
    lp = o_tm
    ew = res1
    og_fm = xn_fm
    n_fm = v_fm
    carry_qk = sb("carry_qk", [64, 16 * 3], BF16)
    carry_v = sb("carry_v", [128, 8 * 3], BF16)

    ps_big = ps("ps_big", [128, 1024], F32)
    ps_a = ps("ps_a", [128, 512], F32)
    ps_ks = ps("ps_ks", [16, 512], F32)
    ps_tr = ps("ps_tr", [128, 1024], BF16)
    ps_int = ps("ps_int", [128, 512], F32)
    ps_ub = ps("ps_ub", [128, 1024], F32)

    sems = {}
    cnt = {}
    for n in ["pe", "dve", "sc", "gp", "ld", "st"]:
        t = nc.semaphore("s_" + n); ctx.append(t)
        sems[n] = t.__enter__(); cnt[n] = 0

    dump_list = list(dumps)

    with nc.Block() as block:
        SEC = {"pe": block.tensor, "dve": block.vector, "sc": block.scalar,
               "gp": block.gpsimd, "sy": block.sync}

        dirty = {"dve": set(), "sc": set(), "gp": set()}

        class HazardGuard:
            def __init__(self, e, eng):
                self._e = e
                self._d = dirty[eng]
            def __getattr__(self, name):
                f = getattr(self._e, name)
                if name in ("wait_ge", "nop", "drain") or not callable(f):
                    return f
                if name == "dma_start":
                    return f
                def g(*a, **k):
                    wr, rd = set(), set()
                    for i, x in enumerate(a):
                        if isinstance(x, bass.AP):
                            (wr if i == 0 else rd).add(x.tensor.name)
                    for kw, x in k.items():
                        if isinstance(x, bass.AP):
                            (wr if kw in ("out", "outs") else rd).add(x.tensor.name)
                    if rd & self._d:
                        self._e.drain()
                        self._d.clear()
                    res = f(*a, **k)
                    self._d |= wr
                    return res
                return g

        def G(eng, fn, waits=(), sem=None, inc=1, stall=False):
            semn = sem or (eng if eng != "sy" else None)
            wl = [(n, v) for (n, v) in waits if v > 0]
            guard = eng in ("dve", "sc", "gp")
            if stall and guard:
                dirty[eng].clear()
            def body(e):
                for (n, v) in wl:
                    e.wait_ge(sems[n], v)
                last = fn(HazardGuard(e, eng) if guard else e)
                if semn is not None and inc and last is not None:
                    last.then_inc(sems[semn], inc)
            SEC[eng](body)
            if semn is not None and inc:
                cnt[semn] += inc

        def W(*names):
            return [(n, cnt[n]) for n in names]

        def WALL():
            return [(n, cnt[n]) for n in ["pe", "dve", "sc", "gp", "ld"]]

        def DMA(eng, out, in_, waits=(), sem="ld"):
            G(eng, lambda e: e.dma_start(out=out, in_=in_), waits, sem=sem, inc=16)

        def dump(tag, ap_fn, cur_p=None):
            for (t, pp, shp) in dump_list:
                if t == tag and (cur_p is None or pp == cur_p):
                    DMA("gp", dump_dr[tag], ap_fn(), waits=WALL(), sem="st")
                    stc = [("st", cnt["st"])]
                    for eng in ["dve", "sc", "pe", "gp"]:
                        G(eng, lambda e: e.nop(), waits=stc, inc=0)

        for nm, dst, cast in [("wqt", w_q, 1), ("wkt", w_k, 1), ("wvt", w_v, 1),
                              ("wat", w_a, 1), ("wgt", w_g, 1), ("wot", w_o, 1),
                              ("wft", w_f, 1)]:
            ncols = dr[nm].shape[1]
            DMA("gp", dst[:].rearrange("p (c n) -> p c n", n=ncols),
                dr[nm].rearrange("(c p) n -> p c n", p=128))
        DMA("gp", w_b[:].rearrange("p (c n) -> p c n", n=8),
            dr["wbt"].rearrange("(c p) n -> p c n", p=128))
        DMA("gp", ba_r[:], dr["ba"])
        DMA("gp", bb_r[:], dr["bb"])
        DMA("gp", ident[:], dr["ident"])
        DMA("sy", identf[:], dr["ident"])
        DMA("gp", pstack[:], dr["pstack"])
        DMA("gp", p16[:], dr["p16"])
        DMA("gp", psel[:], dr["psel"])
        DMA("gp", pslot[:], dr["pslot"])
        DMA("gp", negident[:], dr["negident"])
        DMA("gp", repl16[:], dr["repl16"])
        DMA("sy", taps_qk[:], dr["taps_qk"])
        DMA("sy", taps_v[:], dr["taps_v"])
        DMA("sy", cb_qk[:], dr["cb_qk"])
        DMA("sy", cb_v[:], dr["cb_v"])
        DMA("sy", trim_in[:], dr["trim_in"])
        DMA("sy", trim_iex[:], dr["trim_iex"])
        DMA("sy", w64b[:], dr["w64b"])
        DMA("sy", oh127[:], dr["oh127"])
        DMA("sy", gi_mask[:], dr["gi_mask"])
        DMA("sy", mask_incl[:], dr["mask_incl"])
        DMA("sy", lnw_rep[:], dr["lnw"][None, :].broadcast_to([128, 128]))
        DMA("sy", lnb_rep[:], dr["lnb"][None, :].broadcast_to([128, 128]))
        DMA("sy", fb_rep[:], dr["fbias"][None, :].broadcast_to([128, 1024]))
        DMA("sy", ws_rep[:], dr["wscale"][None, :].broadcast_to([128, 1]))

        G("dve", lambda e: e.memset(ones1[:], 1.0))
        G("dve", lambda e: e.memset(s_st[:], 0.0))
        G("dve", lambda e: e.memset(s_stb[:], 0.0))
        G("dve", lambda e: e.memset(carry_qk[:], 0.0))
        G("dve", lambda e: e.memset(carry_v[:], 0.0))

        for p in range(NOUT):
            r0 = p * C
            DMA("sy", x_t[:], dr["x"][r0:r0 + C, :], waits=WALL())

            def ln_stats(e):
                e.reduce_sum(stat[:, 0:1], x_t[:], axis=AX.X)
                return e.reduce_sum(stat[:, 1:2], scr[:], axis=AX.X)
            G("sc", lambda e: e.square(scr[:], x_t[:]), waits=W("ld", "dve"))
            G("dve", ln_stats, waits=W("sc", "ld"))
            def ln_stats2(e):
                e.tensor_scalar(stat[:, 2:3], stat[:, 0:1], 1.0 / D, None, ALU.mult)
                e.tensor_scalar(stat[:, 3:4], stat[:, 1:2], 1.0 / D, None, ALU.mult)
                e.tensor_mul(stat[:, 4:5], stat[:, 2:3], stat[:, 2:3])
                e.tensor_sub(stat[:, 5:6], stat[:, 3:4], stat[:, 4:5])
                return e.tensor_scalar(stat[:, 5:6], stat[:, 5:6], EPS_LN, None, ALU.add)
            G("dve", ln_stats2)
            G("sc", lambda e: e.activation(stat[:, 6:7], stat[:, 5:6], AF.Ln), waits=W("dve"))
            G("sc", lambda e: e.activation(stat[:, 7:8], stat[:, 6:7], AF.Exp, scale=-0.5))
            G("dve", lambda e: e.tensor_scalar(stat[:, 8:9], stat[:, 2:3], -1.0, None, ALU.mult), waits=W("sc"))
            G("dve", lambda e: e.tensor_scalar(xn_tm[:], x_t[:], stat[:, 8:9], stat[:, 7:8], ALU.add, ALU.mult))

            def xntr(e):
                last = None
                for j in range(8):
                    last = e.transpose(ps_big[:, j * 128:(j + 1) * 128],
                                       xn_tm[:, j * 128:(j + 1) * 128], identf[:])
                return last
            G("pe", xntr, waits=W("dve", "sc"))
            G("dve", lambda e: e.tensor_copy(xn_fm[:], ps_big[:]), waits=W("pe"), stall=True)
            dump("x_t", lambda: x_t[:], p)
            dump("xn_tm", lambda: xn_tm[:], p)
            dump("stat", lambda: stat[:], p)
            dump("xn_fm", lambda: xn_fm[:], p)

            def proj(dst_psum, w, wstride, nlo, nwid, bias_row=None):
                def f(e):
                    last = None
                    for kc in range(8):
                        last = e.matmul(dst_psum,
                                        xn_fm[:, kc * 128:(kc + 1) * 128],
                                        w[:, kc * wstride + nlo: kc * wstride + nlo + nwid],
                                        start=(kc == 0), stop=(kc == 7 and bias_row is None))
                    if bias_row is not None:
                        last = e.matmul(dst_psum, ones1[:], bias_row,
                                        start=False, stop=True)
                    return last
                return f
            def proj1(e):
                for half in range(2):
                    proj(ps_big[:, half * 512:(half + 1) * 512], w_a, 1024, half * 512, 512,
                         ba_r[:, half * 512:(half + 1) * 512])(e)
                proj(ps_a[:, 0:512], w_q, 512, 0, 512)(e)
                return proj(ps_ub[:, 0:512], w_k, 512, 0, 512)(e)
            G("pe", proj1, waits=W("sc", "dve"))
            def evac1(e):
                e.tensor_copy(za[:], ps_big[:])
                e.tensor_copy(qraw_tm[:], ps_a[:, 0:512])
                return e.tensor_copy(kraw_tm[:], ps_ub[:, 0:512])
            G("dve", evac1, waits=W("pe"), stall=True)
            def proj2(e):
                for half in range(2):
                    proj(ps_big[:, half * 512:(half + 1) * 512], w_v, 1024, half * 512, 512)(e)
                return proj(ps_a[:, 0:8], w_b, 8, 0, 8, bb_r[:])(e)
            G("pe", proj2, waits=W("dve"))
            def evac2(e):
                e.tensor_copy(vraw_tm[:], ps_big[:])
                return e.tensor_copy(bz[:], ps_a[:, 0:8])
            G("dve", evac2, waits=W("pe"), stall=True)

            def rawtr(src, dst, carry_t, carry_off, pw):
                def trs(e):
                    last = None
                    for j in range(8):
                        last = e.transpose(
                            ps_tr[0:pw, j * 128:(j + 1) * 128],
                            src[:, j * pw:(j + 1) * pw], ident[:])
                    return last
                G("pe", trs, waits=W("sc", "dve"))
                def evac(e):
                    e.tensor_copy(
                        dst[0:pw, :].rearrange("p (s c) -> p s c", c=131)[:, :, 3:131],
                        ps_tr[0:pw, :].rearrange("p (s c) -> p s c", c=128))
                    return e.tensor_copy(
                        dst[0:pw, :].rearrange("p (s c) -> p s c", c=131)[:, :, 0:3],
                        carry_t[0:pw, carry_off * 3:(carry_off + 8) * 3]
                            .rearrange("p (s c) -> p s c", c=3))
                G("dve", evac, waits=W("pe"), stall=True)
            rawtr(qraw_tm, qr_fm, carry_qk, 0, 64)
            rawtr(kraw_tm, kr_fm, carry_qk, 8, 64)
            rawtr(vraw_tm, vr_fm, carry_v, 0, 128)

            def cscale(srcf, tapt, gbase, pw, t4, reg, cbt):
                def f(e):
                    dst3 = scl[0:pw, reg * 1024:(reg + 1) * 1024].rearrange(
                        "p (s c) -> p s c", c=128)
                    src3 = srcf[0:pw, :].rearrange("p (s c) -> p s c", c=131)[:, :, t4:t4 + 128]
                    tap = tapt[0:pw, gbase * 4 + t4: gbase * 4 + t4 + 29: 4] \
                        .unsqueeze(2).broadcast_to([pw, 8, 128])
                    last = e.tensor_tensor(dst3, src3, tap, ALU.mult)
                    if t4 == 0:
                        cb3 = cbt[0:pw, gbase:gbase + 8].unsqueeze(2).broadcast_to([pw, 8, 128])
                        last = e.tensor_tensor(dst3, dst3, cb3, ALU.add)
                    return last
                return f
            def cmm(psr, pw, t4, reg):
                def f(e):
                    for half in range(2):
                        last = e.matmul(psr[0:pw, half * 512:(half + 1) * 512],
                                        ident[0:pw, 0:pw],
                                        scl[0:pw, reg * 1024 + half * 512: reg * 1024 + (half + 1) * 512],
                                        start=(t4 == 0), stop=(t4 == 3))
                    return last
                return f
            def conv_tensor(srcf, dstf, tapt, gbase, pw, psr, pre_waits, cbt):
                pe_after = []
                for t4 in range(4):
                    wd = ([("pe", pe_after[t4 - 2])] if t4 >= 2 else list(pre_waits))
                    G("dve", cscale(srcf, tapt, gbase, pw, t4, t4 % 2, cbt), waits=wd)
                    G("pe", cmm(psr, pw, t4, t4 % 2),
                      waits=W("dve") + (list(pre_waits) if t4 == 0 else []))
                    pe_after.append(cnt["pe"])
                G("sc", lambda e: e.activation(dstf[0:pw, 0:1024], psr[0:pw, 0:1024], AF.Silu),
                  waits=W("pe"))
            conv_tensor(qr_fm, q_fm, taps_qk, 0, 64, ps_big, [], cb_qk)
            conv_tensor(kr_fm, k_fm, taps_qk, 8, 64, ps_ub, W("pe"), cb_qk)
            conv_tensor(vr_fm, v_fm, taps_v, 0, 128, ps_big, W("pe", "sc"), cb_v)
            G("dve", lambda e: e.tensor_scalar(k_fm[0:64, :], k_fm[0:64, :], KSCALE, None, ALU.mult),
              waits=W("sc"))
            def save_carry(e):
                e.tensor_copy(carry_qk[0:64, 0:24].rearrange("p (s c) -> p s c", c=3),
                              qr_fm[0:64, :].rearrange("p (s c) -> p s c", c=131)[:, :, 128:131])
                e.tensor_copy(carry_qk[0:64, 24:48].rearrange("p (s c) -> p s c", c=3),
                              kr_fm[0:64, :].rearrange("p (s c) -> p s c", c=131)[:, :, 128:131])
                return e.tensor_copy(carry_v[:, 0:24].rearrange("p (s c) -> p s c", c=3),
                              vr_fm[:].rearrange("p (s c) -> p s c", c=131)[:, :, 128:131])
            G("dve", save_carry, waits=W("sc"))
            dump("q_fm", lambda: q_fm[:], p)
            dump("v_fm", lambda: v_fm[:], p)

            for half in range(2):
                G("pe", proj(ps_big[:, half * 512:(half + 1) * 512], w_g, 1024, half * 512, 512),
                  waits=W("dve", "sc"))
            G("sc", lambda e: e.activation(o_tm[:], ps_big[:], AF.Tanh, scale=0.5),
              waits=W("pe", "dve"))
            G("dve", lambda e: e.tensor_scalar(g_tm[:], o_tm[:], 1.0, 0.5, ALU.add, ALU.mult),
              waits=W("sc"))
            G("sc", lambda e: e.activation(bz[:, 0:8], bz[:, 0:8], AF.Tanh, scale=0.5),
              waits=W("dve"))
            G("dve", lambda e: e.tensor_scalar(beta[:], bz[:, 0:8], 1.0, 0.5, ALU.add, ALU.mult),
              waits=W("sc"))

            def ktrs(e):
                last = None
                for j in range(8):
                    last = e.transpose(ps_tr[:, j * 64:(j + 1) * 64],
                                       k_fm[0:64, j * 128:(j + 1) * 128], ident[0:64, 0:64])
                return last
            G("pe", ktrs, waits=W("sc", "dve"))
            G("dve", lambda e: e.tensor_copy(k_tm[:], ps_tr[:, 0:512]), waits=W("pe"), stall=True)
            def vtrs(e):
                last = None
                for j in range(8):
                    last = e.transpose(ps_tr[:, j * 128:(j + 1) * 128],
                                       v_fm[:, j * 128:(j + 1) * 128], ident[:])
                return last
            G("pe", vtrs, waits=W("dve"))
            G("dve", lambda e: e.tensor_copy(v_tm[:], ps_tr[:]), waits=W("pe"), stall=True)

            G("sc", lambda e: e.activation(ew[:], za[:], AF.Exp, scale=-1.0), waits=W("dve"))
            G("dve", lambda e: e.tensor_scalar(ew[:], ew[:], 1.0, None, ALU.add), waits=W("sc"))
            G("sc", lambda e: e.activation(lp[:], ew[:], AF.Ln), waits=W("dve"))
            G("dve", lambda e: e.tensor_tensor(
                dbw[:].rearrange("p (a b) -> p a b", b=128),
                ew[:].rearrange("p (a b) -> p a b", b=128),
                beta[:].unsqueeze(2).broadcast_to([128, 8, 128]),
                ALU.mult), waits=W("sc"))
            if p > 0:
                def sigmm(e):
                    last = None
                    for half in range(2):
                        last = e.matmul(ps_big[0:64, half * 512:(half + 1) * 512],
                                        w64b[:, 0:64], lp[:, half * 512:(half + 1) * 512],
                                        start=True, stop=True)
                    return last
                G("pe", sigmm, waits=W("sc", "dve"))
                G("sc", lambda e: e.activation(sigm[:], ps_big[0:64, :], AF.Exp), waits=W("pe"))
                G("dve", lambda e: e.tensor_mul(s_st[:], s_st[:], sigm[:]), waits=W("sc"))
                G("dve", lambda e: e.tensor_copy(s_stb[:], s_st[:]))
            def trimm(mat):
                def f(e):
                    last = None
                    for half in range(2):
                        last = e.matmul(ps_big[:, half * 512:(half + 1) * 512],
                                        mat[:], lp[:, half * 512:(half + 1) * 512],
                                        start=True, stop=True)
                    return last
                return f
            G("pe", trimm(trim_in), waits=W("dve", "sc"))
            G("sc", lambda e: e.activation(gin[:], ps_big[:], AF.Exp), waits=W("pe"))
            G("pe", trimm(trim_iex), waits=W("sc"))
            G("sc", lambda e: e.activation(igex[:], ps_big[:], AF.Exp), waits=W("pe"))
            G("dve", lambda e: e.tensor_mul(vt_w[:], v_tm[:], igex[:]), waits=W("sc"))
            dump("gin", lambda: gin[:], p)
            dump("igex", lambda: igex[:], p)

            def hk(hh):
                return k_fm[0:64, hh * 128:(hh + 1) * 128]
            def hq(hh):
                return q_fm[0:64, hh * 128:(hh + 1) * 128]
            for hh in range(8):
                G("pe", lambda e, hh=hh: e.matmul(ps_a[:, 0:128], hk(hh), hk(hh),
                                                  start=True, stop=True), waits=W("dve", "sc"))
                G("dve", lambda e, hh=hh: e.tensor_scalar(
                    gram[:, hh * 128:(hh + 1) * 128], ps_a[:, 0:128], -1.0, None, ALU.mult),
                  waits=W("pe"))
                G("pe", lambda e, hh=hh: e.matmul(ps_a[:, 128:256], hk(hh), hq(hh),
                                                  start=True, stop=True), waits=W("dve"))
                G("dve", lambda e, hh=hh: e.tensor_mul(
                    atil[:, hh * 128:(hh + 1) * 128], ps_a[:, 128:256], mask_incl[:]),
                  waits=W("pe"))
            DMA("gp", dr["scr_gr"], gram[:], waits=W("dve"))
            def qs0(e):
                last = None
                for hh in range(8):
                    last = e.matmul(ps_big[:, hh * 128:(hh + 1) * 128], hq(hh),
                                    s_stb[:, hh * 128:(hh + 1) * 128],
                                    start=(hh % 4 == 0), stop=False)
                return last
            G("pe", qs0, waits=W("dve", "sc"))

            DMA("gp", dr["scr_db"], dbw[:], waits=W("dve"))

            for i16 in range(8):
                t0 = i16 * 16
                DMA("gp", db_hp[:].rearrange("h (s v) -> h s v", v=128),
                    dr["scr_db"][t0:t0 + 16, :].rearrange("s (h v) -> h s v", v=128),
                    waits=W("dve", "ld"))
                if t0 > 0 or True:
                    from concourse.bass import AP as _AP
                    src = _AP(dr["scr_gr"].tensor,
                              (t0 + 1) * 1024 + t0,
                              [[128, 8], [1025, 15]])
                    def _gsub(e, src=src):
                        return e.dma_start(out=gram_sub[:, 1:16], in_=src,
                                           allow_slow_non_contiguous=True)
                    G("gp", _gsub, waits=W("dve", "pe", "ld"), sem="ld", inc=16)
                for half in range(2):
                    hs = range(half * 4, half * 4 + 4)
                    def ksmms(e, hs=hs, i16=i16, t0=t0, half=half):
                        last = None
                        first = True
                        for hh in hs:
                            c0 = (hh % 4) * 128
                            last = e.matmul(ps_ks[0:16, c0:c0 + 128],
                                            hk(hh)[:, t0:t0 + 16],
                                            s_stb[:, hh * 128:(hh + 1) * 128],
                                            start=first, stop=False)
                            first = False
                            if i16 > 0:
                                last = e.matmul(ps_ks[0:16, c0:c0 + 128],
                                                gram[0:t0, hh * 128 + t0: hh * 128 + t0 + 16],
                                                u_big[0:t0, hh * 128:(hh + 1) * 128],
                                                start=False, stop=False)
                        last = e.matmul(ps_ks[0:16, 0:512],
                                        negident[:, t0:t0 + 16],
                                        vt_w[:, half * 512:(half + 1) * 512],
                                        start=False, stop=True)
                        return last
                    G("pe", ksmms, waits=W("dve", "sc"))
                    G("dve", lambda e, half=half: e.tensor_copy(
                        stag[:, half * 512:(half + 1) * 512], ps_ks[0:16, 0:512]),
                      waits=W("pe"))
                def permin(e):
                    last = None
                    for hh in range(8):
                        last = e.matmul(ps_int[:, 128:256], pstack[:, hh * 128:(hh + 1) * 128],
                                        stag[:, hh * 128:(hh + 1) * 128],
                                        start=(hh == 0), stop=(hh == 7))
                    return last
                G("pe", permin, waits=W("dve"))
                G("dve", lambda e: e.tensor_copy(acc_int[:], ps_int[:, 128:256]), waits=W("pe"))
                def gpkmm(e, t0=t0):
                    last = None
                    for hh in range(8):
                        last = e.matmul(ps_ks[0:16, hh * 16:(hh + 1) * 16],
                                        hk(hh)[:, t0:t0 + 16], hk(hh)[:, t0:t0 + 16],
                                        start=True, stop=True)
                    return last
                G("pe", gpkmm, waits=W("dve"))
                G("dve", lambda e: e.tensor_copy(gpk[:], ps_ks[0:16, 0:128]), waits=W("pe"))
                G("pe", lambda e: e.matmul(ps_int[:, 256:384], repl16[:], gpk[:],
                                           start=True, stop=True), waits=W("dve"))
                G("dve", lambda e: e.tensor_mul(
                    gi_t[:].rearrange("p (s h) -> p s h", h=8),
                    ps_int[:, 256:384].rearrange("p (h s) -> p s h", h=8),
                    gi_mask[:].rearrange("p (s h) -> p s h", h=8)), waits=W("pe"))
                G("dve", lambda e: e.memset(u_int[:], 0.0))
                for s16 in range(16):
                    sl = (s16 % 2) * 128
                    def pegrp(e, s16=s16, sl=sl):
                        if s16 >= 1:
                            e.matmul(ps_int[:, 0:128],
                                     pslot[:, (s16 - 1) * 128:s16 * 128],
                                     u_ring[:, (s16 - 1) * 128:s16 * 128],
                                     start=(s16 == 1), stop=False)
                        last = e.matmul(ps_a[0:8, sl:sl + 128],
                                        ident[:, 8 * s16:8 * s16 + 8], acc_int[:],
                                        start=True, stop=(s16 < 2))
                        if s16 >= 2:
                            last = e.matmul(ps_a[0:8, sl:sl + 128],
                                            gi_t[:, 8 * s16:8 * s16 + 8], u_int[:],
                                            start=False, stop=True)
                        return last
                    G("pe", pegrp, waits=W("dve"))
                    def dvegrp(e, s16=s16, sl=sl):
                        if s16 >= 1:
                            e.tensor_copy(u_int[:], ps_int[:, 0:128])
                        if s16 == 0:
                            return e.tensor_mul(
                                u_ring[:, 0:128], ps_a[0:8, sl:sl + 128], db_hp[:, 0:128])
                        e.scalar_tensor_tensor(
                            t8[:], u_ring[:, (s16 - 1) * 128:s16 * 128],
                            gram_sub[:, s16:s16 + 1],
                            ps_a[0:8, sl:sl + 128], ALU.mult, ALU.add)
                        return e.tensor_mul(
                            u_ring[:, s16 * 128:(s16 + 1) * 128],
                            t8[:], db_hp[:, s16 * 128:(s16 + 1) * 128])
                    G("dve", dvegrp, waits=W("pe", "ld"), stall=True)
                G("pe", lambda e: e.matmul(
                    ps_int[:, 0:128], pslot[:, 15 * 128:16 * 128],
                    u_ring[:, 15 * 128:16 * 128],
                    start=False, stop=True), waits=W("dve"))
                G("dve", lambda e: e.tensor_copy(u_int[:], ps_int[:, 0:128]),
                  waits=W("pe"))
                for hh in range(9):
                    def pe_ub(e, hh=hh, i16=i16):
                        last = None
                        if hh >= 1:
                            last = e.matmul(
                                ps_ub[:, (hh - 1) * 128:hh * 128],
                                p16[:, i16 * 128:(i16 + 1) * 128],
                                uh16[0:16, ((hh - 1) % 2) * 128:((hh - 1) % 2) * 128 + 128],
                                start=(i16 == 0 and (hh - 1) % 4 == 0), stop=(i16 == 7))
                        if hh < 8:
                            last = e.matmul(
                                ps_int[0:16, 128 + (hh % 2) * 128:256 + (hh % 2) * 128],
                                psel[:, hh * 16:(hh + 1) * 16], u_int[:],
                                start=True, stop=True)
                        return last
                    G("pe", pe_ub, waits=W("dve"))
                    if hh < 8:
                        G("dve", lambda e, hh=hh: e.tensor_copy(
                            uh16[0:16, (hh % 2) * 128:(hh % 2) * 128 + 128],
                            ps_int[0:16, 128 + (hh % 2) * 128:256 + (hh % 2) * 128]),
                          waits=W("pe"))
                G("dve", lambda e: e.tensor_copy(u_big[:], ps_ub[:]), waits=W("pe"))

            dump("u_big", lambda: u_big[:], p)
            def ocouple(e):
                last = None
                for hh in range(8):
                    last = e.matmul(ps_big[:, hh * 128:(hh + 1) * 128],
                                    atil[:, hh * 128:(hh + 1) * 128],
                                    u_big[:, hh * 128:(hh + 1) * 128],
                                    start=False, stop=True)
                return last
            G("pe", ocouple, waits=W("dve", "sc"))
            G("dve", lambda e: e.tensor_mul(o_tm[:], gin[:], ps_big[:]), waits=W("pe"))
            dump("o_tm", lambda: o_tm[:], p)

            def repl127(e):
                last = None
                for half in range(2):
                    last = e.matmul(ps_big[0:64, half * 512:(half + 1) * 512],
                                    oh127[:, 0:64], gin[:, half * 512:(half + 1) * 512],
                                    start=True, stop=True)
                return last
            G("pe", repl127, waits=W("dve"))
            G("dve", lambda e: e.tensor_copy(sigm[:], ps_big[0:64, :]), waits=W("pe"))
            for hp in range(2):
                def wmms(e, hp=hp):
                    last = None
                    for hh in range(4 * hp, 4 * hp + 4):
                        last = e.matmul(ps_a[0:64, (hh % 4) * 128:((hh % 4) + 1) * 128],
                                        k_tm[:, hh * 64:(hh + 1) * 64],
                                        u_big[:, hh * 128:(hh + 1) * 128],
                                        start=True, stop=True)
                    return last
                G("pe", wmms, waits=W("dve", "sc"))
                sl = slice(hp * 512, (hp + 1) * 512)
                G("dve", lambda e, sl=sl: e.tensor_sub(s_st[:, sl], s_st[:, sl],
                                                       ps_a[0:64, 0:512]), waits=W("pe"))
            G("dve", lambda e: e.tensor_mul(s_st[:], s_st[:], sigm[:]))
            G("dve", lambda e: e.tensor_copy(s_stb[:], s_st[:]))
            dump("s_st", lambda: s_st[:], p)

            G("sc", lambda e: e.square(scr[:], o_tm[:]), waits=W("dve"))
            def gstats(e):
                e.reduce_sum(stat[:, 0:8], o_tm[:].rearrange("p (a b) -> p a b", b=128), axis=AX.X)
                return e.reduce_sum(stat[:, 8:16], scr[:].rearrange("p (a b) -> p a b", b=128), axis=AX.X)
            G("dve", gstats, waits=W("sc"))
            def gstats2(e):
                e.tensor_scalar(stat[:, 0:8], stat[:, 0:8], 1.0 / DV, None, ALU.mult)
                e.tensor_scalar(stat[:, 8:16], stat[:, 8:16], 1.0 / DV, None, ALU.mult)
                e.tensor_mul(scr[:, 0:8], stat[:, 0:8], stat[:, 0:8])
                e.tensor_sub(stat[:, 8:16], stat[:, 8:16], scr[:, 0:8])
                return e.tensor_scalar(stat[:, 8:16], stat[:, 8:16], EPS_GN, None, ALU.add)
            G("dve", gstats2)
            G("sc", lambda e: e.activation(stat[:, 8:16], stat[:, 8:16], AF.Ln), waits=W("dve"))
            G("sc", lambda e: e.activation(stat[:, 8:16], stat[:, 8:16], AF.Exp, scale=-0.5))
            def onorm(e):
                o3 = o_tm[:].rearrange("p (a b) -> p a b", b=128)
                mu = stat[:, 0:8].unsqueeze(2).broadcast_to([128, 8, 128])
                rs = stat[:, 8:16].unsqueeze(2).broadcast_to([128, 8, 128])
                e.tensor_sub(o3, o3, mu)
                e.tensor_mul(o3, o3, rs)
                e.tensor_mul(o_tm[:].rearrange("p (a b) -> p a b", b=128),
                             o_tm[:].rearrange("p (a b) -> p a b", b=128),
                             lnw_rep[:].unsqueeze(1).broadcast_to([128, 8, 128]))
                e.tensor_add(o_tm[:].rearrange("p (a b) -> p a b", b=128),
                             o_tm[:].rearrange("p (a b) -> p a b", b=128),
                             lnb_rep[:].unsqueeze(1).broadcast_to([128, 8, 128]))
                return e.tensor_mul(o_tm[:], o_tm[:], g_tm[:])
            G("dve", onorm, waits=W("sc"))
            for j in range(8):
                G("pe", lambda e, j=j: e.transpose(ps_big[:, j * 128:(j + 1) * 128],
                                                   o_tm[:, j * 128:(j + 1) * 128], identf[:]),
                  waits=W("dve", "sc"))
                G("dve", lambda e, j=j: e.tensor_copy(og_fm[:, j * 128:(j + 1) * 128],
                                              ps_big[:, j * 128:(j + 1) * 128]), waits=W("pe"))
            def womms(e):
                last = None
                for half in range(2):
                    for kc in range(8):
                        last = e.matmul(ps_big[:, half * 512:(half + 1) * 512],
                                        og_fm[:, kc * 128:(kc + 1) * 128],
                                        w_o[:, kc * 1024 + half * 512: kc * 1024 + half * 512 + 512],
                                        start=(kc == 0), stop=(kc == 7))
                return last
            G("pe", womms, waits=W("sc", "dve"))
            G("dve", lambda e: e.tensor_add(res1[:], x_t[:], ps_big[:]), waits=W("pe"))
            G("dve", lambda e: e.tensor_add(res1[:], res1[:], fb_rep[:]))

            G("sc", lambda e: e.square(scr[:], xn_tm[:]), waits=W("dve"))
            G("dve", lambda e: e.reduce_sum(stat[:, 0:1], scr[:], axis=AX.X), waits=W("sc"))
            def rmsst(e):
                e.tensor_scalar(stat[:, 1:2], stat[:, 0:1], 1.0 / D, None, ALU.mult)
                return e.tensor_scalar(stat[:, 1:2], stat[:, 1:2], EPS_RMS, None, ALU.add)
            G("dve", rmsst)
            G("sc", lambda e: e.activation(stat[:, 2:3], stat[:, 1:2], AF.Ln), waits=W("dve"))
            G("sc", lambda e: e.activation(stat[:, 3:4], stat[:, 2:3], AF.Exp, scale=-0.5))
            G("dve", lambda e: e.tensor_scalar(scr[:], xn_tm[:], stat[:, 3:4], None, ALU.mult),
              waits=W("sc"))
            G("dve", lambda e: e.reduce_max(stat[:, 4:5], scr[:], axis=AX.X,
                                            apply_absolute_value=True))
            G("dve", lambda e: e.tensor_scalar(stat[:, 4:5], stat[:, 4:5], EPS_ACT, None, ALU.max))
            G("sc", lambda e: e.activation(stat[:, 5:6], stat[:, 4:5], AF.Ln), waits=W("dve"))
            G("sc", lambda e: e.activation(stat[:, 6:7], stat[:, 5:6], AF.Exp, scale=-1.0))
            G("dve", lambda e: e.tensor_scalar(stat[:, 7:8], stat[:, 6:7], 127.0, None, ALU.mult),
              waits=W("sc"))
            G("dve", lambda e: e.tensor_scalar(scr[:], scr[:], stat[:, 7:8], RND, ALU.mult, ALU.add))
            G("dve", lambda e: e.tensor_scalar(scr[:], scr[:], RND, None, ALU.subtract))
            for j in range(8):
                G("pe", lambda e, j=j: e.transpose(ps_big[:, j * 128:(j + 1) * 128],
                                                   scr[:, j * 128:(j + 1) * 128], identf[:]),
                  waits=W("dve", "sc"))
                G("dve", lambda e, j=j: e.tensor_copy(n_fm[:, j * 128:(j + 1) * 128],
                                              ps_big[:, j * 128:(j + 1) * 128]), waits=W("pe"))
            def ffnmms(e):
                last = None
                for half in range(2):
                    for kc in range(8):
                        last = e.matmul(ps_big[:, half * 512:(half + 1) * 512],
                                        n_fm[:, kc * 128:(kc + 1) * 128],
                                        w_f[:, kc * 1024 + half * 512: kc * 1024 + half * 512 + 512],
                                        start=(kc == 0), stop=(kc == 7))
                return last
            G("pe", ffnmms, waits=W("sc", "dve"))
            def oscale(e):
                e.tensor_mul(stat[:, 9:10], stat[:, 4:5], ws_rep[:])
                return e.tensor_scalar(stat[:, 9:10], stat[:, 9:10], 1.0 / 127.0, None, ALU.mult)
            G("dve", oscale)
            G("dve", lambda e: e.scalar_tensor_tensor(res1[:], ps_big[:], stat[:, 9:10],
                                                      res1[:], ALU.mult, ALU.add), waits=W("pe"))
            DMA("sy", dr["out"][r0:r0 + C, :], res1[:], waits=WALL(), sem="st")
            dump("out_rows", lambda: res1[:], p)

        G("sy", lambda e: e.wait_ge(sems["st"], cnt["st"]) or None, waits=(), sem=None)

    for t in reversed(ctx):
        t.__exit__(None, None, None)
    return nc





_BUILD_CACHE = {}

def kernel(**inputs):
    import numpy as np
    from concourse.bass_utils import run_bass_kernel_spmd
    T = 8192
    x_full = np.asarray(inputs["L0_attn_x"])
    B = x_full.shape[0]
    g = host_prep(inputs)
    if T not in _BUILD_CACHE:
        _BUILD_CACHE[T] = build(T)
    nc = _BUILD_CACHE[T]
    in_maps = []
    for c in range(8):
        im = dict(g)
        im["x"] = np.ascontiguousarray(x_full[c % B])
        in_maps.append(im)
    res = run_bass_kernel_spmd(nc, in_maps, core_ids=list(range(8)))
    out = np.stack([np.asarray(res.results[b]["out"]) for b in range(B)], axis=0)
    return out.astype(np.float32)


# version 05
# speedup vs baseline: 1.1958x; 1.0398x over previous
import numpy as np
import ml_dtypes
import concourse.bass as bass
import concourse.mybir as mybir

F32 = mybir.dt.float32
BF16 = mybir.dt.bfloat16
AF = mybir.ActivationFunctionType
ALU = mybir.AluOpType
AX = mybir.AxisListType

H, DK, DV, D = 8, 64, 128, 1024
C = 128
EPS_LN = 1e-5
EPS_GN = 1e-5
EPS_RMS = 1e-6
EPS_ACT = 1e-5
KSCALE = DK ** (-0.5)
RND = 12582912.0


def host_constants():
    r = np.arange(128)
    tri_incl = (r[:, None] <= r[None, :]).astype(np.float32)
    tri_excl = (r[:, None] < r[None, :]).astype(np.float32)
    mlow = ((r[:, None] < 64).astype(np.float32)) * np.ones((1, 128), np.float32)
    c = {}
    c["ident"] = np.eye(128, dtype=np.float32)
    c["trim_in"] = (mlow - tri_incl)
    c["trim_iex"] = (tri_excl - mlow)
    c["w64b"] = (-(r < 64).astype(np.float32))[:, None] * np.ones((1, 64), np.float32)
    c["oh127"] = (r == 127).astype(np.float32)[:, None] * np.ones((1, 64), np.float32)
    pstack = np.zeros((16, 8 * 128), np.float32)
    for h in range(8):
        for s in range(16):
            pstack[s, h * 128 + s * 8 + h] = 1.0
    c["pstack"] = pstack
    dm = np.zeros((128, 128), np.float32)
    for h in range(8):
        for rr in range(16):
            for s in range(16):
                if rr <= s - 2:
                    dm[rr * 8 + h, s * 8 + h] = -1.0
    c["gi_mask"] = dm
    p16 = np.zeros((16, 8 * 128), np.float32)
    for i in range(8):
        for k in range(16):
            p16[k, i * 128 + 16 * i + k] = 1.0
    c["p16"] = p16
    psel = np.zeros((128, 8 * 16), np.float32)
    for h in range(8):
        for ss in range(16):
            psel[ss * 8 + h, h * 16 + ss] = 1.0
    c["psel"] = psel
    pslot = np.zeros((8, 16 * 128), np.float32)
    for ss in range(16):
        for h in range(8):
            pslot[h, ss * 128 + ss * 8 + h] = 1.0
    c["pslot"] = pslot
    c["negident"] = -np.eye(128, dtype=np.float32)
    repl = np.zeros((16, 128), np.float32)
    for rr in range(16):
        for h in range(8):
            repl[rr, rr * 8 + h] = 1.0
    c["repl16"] = repl
    c["mask_incl"] = -tri_incl
    return c


def host_prep(inputs):
    g = {}
    tr = lambda a: np.ascontiguousarray(np.asarray(a).T).astype(np.float32)
    g["wqt"] = tr(inputs["L0_attn_Wq"])
    g["wkt"] = tr(inputs["L0_attn_Wk"])
    g["wvt"] = tr(inputs["L0_attn_Wv"])
    g["wat"] = tr(inputs["L0_attn_Wa"])
    g["wgt"] = tr(inputs["L0_attn_Wg"])
    g["wbt"] = tr(inputs["L0_attn_Wb"])
    g["wot"] = tr(inputs["L0_attn_Wo"])
    g["wft"] = tr(inputs["L0_ffn_w_int8"])
    g["ba"] = np.asarray(inputs["L0_attn_ba"]).reshape(1, D).astype(np.float32)
    g["bb"] = np.asarray(inputs["L0_attn_bb"]).reshape(1, 8).astype(np.float32)
    tqk = np.concatenate([
        np.asarray(inputs["L0_attn_qc_w"]).reshape(512, 4),
        np.asarray(inputs["L0_attn_kc_w"]).reshape(512, 4)], axis=0).astype(np.float32)
    g["taps_qk"] = np.ascontiguousarray(tqk.reshape(16, 64, 4).transpose(1, 0, 2).reshape(64, 64))
    tv = np.asarray(inputs["L0_attn_vc_w"]).reshape(1024, 4).astype(np.float32)
    g["taps_v"] = np.ascontiguousarray(tv.reshape(8, 128, 4).transpose(1, 0, 2).reshape(128, 32))
    cqk = np.concatenate([
        np.asarray(inputs["L0_attn_qc_b"]).reshape(512),
        np.asarray(inputs["L0_attn_kc_b"]).reshape(512)]).astype(np.float32)
    g["cb_qk"] = np.ascontiguousarray(cqk.reshape(16, 64).T)
    g["cb_v"] = np.ascontiguousarray(
        np.asarray(inputs["L0_attn_vc_b"]).reshape(8, 128).T.astype(np.float32))
    g["lnw"] = np.asarray(inputs["L0_attn_ln_w"]).reshape(128).astype(np.float32)
    g["lnb"] = np.asarray(inputs["L0_attn_ln_b"]).reshape(128).astype(np.float32)
    g["fbias"] = np.asarray(inputs["L0_ffn_bias"]).reshape(D).astype(np.float32)
    g["wscale"] = np.asarray(inputs["L0_ffn_weight_scale"]).reshape(1).astype(np.float32)
    for k, v in host_constants().items():
        g[k] = v.astype(np.float32)
    return g


def build(T, dumps=()):
    assert T % C == 0
    NOUT = T // C
    nc = bass.Bass("TRN2", target_bir_lowering=False, debug=False)
    dr = {}
    dr["x"] = nc.dram_tensor("x", [T, D], F32, kind="ExternalInput").ap()
    for nm, shp in [("wqt", [D, 512]), ("wkt", [D, 512]), ("wvt", [D, D]),
                    ("wat", [D, D]), ("wgt", [D, D]), ("wbt", [D, 8]),
                    ("wot", [D, D]), ("wft", [D, D]), ("ba", [1, D]),
                    ("bb", [1, 8]), ("taps_qk", [64, 64]), ("taps_v", [128, 32]),
                    ("cb_qk", [64, 16]), ("cb_v", [128, 8]),
                    ("lnw", [128]), ("lnb", [128]), ("fbias", [D]), ("wscale", [1]),
                    ("ident", [128, 128]), ("trim_in", [128, 128]),
                    ("trim_iex", [128, 128]), ("w64b", [128, 64]),
                    ("oh127", [128, 64]), ("pstack", [16, 1024]),
                    ("gi_mask", [128, 128]), ("p16", [16, 1024]),
                    ("psel", [128, 128]), ("pslot", [8, 2048]),
                    ("negident", [128, 128]),
                    ("repl16", [16, 128]), ("mask_incl", [128, 128])]:
        dr[nm] = nc.dram_tensor(nm, shp, F32, kind="ExternalInput").ap()
    dr["out"] = nc.dram_tensor("out", [T, D], F32, kind="ExternalOutput").ap()
    dr["scr_db"] = nc.dram_tensor("scr_db", [128, 1024], BF16, kind="Internal").ap()
    dr["scr_gr"] = nc.dram_tensor("scr_gr", [128, 1024], BF16, kind="Internal").ap()
    dump_dr = {}
    for tag, _p, shp in dumps:
        dump_dr[tag] = nc.dram_tensor("dbg_" + tag, list(shp), F32,
                                      kind="ExternalOutput").ap()

    ctx = []
    def sb(name, shape, dt):
        t = nc.sbuf_tensor(name, list(shape), dt); ctx.append(t); return t.__enter__()
    def ps(name, shape, dt):
        t = nc.psum_tensor(name, list(shape), dt); ctx.append(t); return t.__enter__()

    w_q = sb("w_q", [128, 8 * 512], BF16)
    w_k = sb("w_k", [128, 8 * 512], BF16)
    w_v = sb("w_v", [128, 8 * 1024], BF16)
    w_a = sb("w_a", [128, 8 * 1024], BF16)
    w_g = sb("w_g", [128, 8 * 1024], BF16)
    w_b = sb("w_b", [128, 8 * 8], BF16)
    w_o = sb("w_o", [128, 8 * 1024], BF16)
    w_f = sb("w_f", [128, 8 * 1024], BF16)
    ba_r = sb("ba_r", [1, D], BF16)
    bb_r = sb("bb_r", [1, 8], BF16)
    ones1 = sb("ones1", [1, 128], BF16)
    taps_qk = sb("taps_qkt", [64, 64], F32)
    taps_v = sb("taps_vt", [128, 32], F32)
    cb_qk = sb("cb_qkt", [64, 16], F32)
    cb_v = sb("cb_vt", [128, 8], F32)
    ident = sb("identt", [128, 128], BF16)
    identf = sb("identft", [128, 128], F32)
    trim_in = sb("trim_int", [128, 128], F32)
    trim_iex = sb("trim_iext", [128, 128], F32)
    w64b = sb("w64bt", [128, 64], F32)
    oh127 = sb("oh127t", [128, 64], F32)
    pstack = sb("pstackt", [16, 1024], BF16)
    gi_mask = sb("gi_maskt", [128, 128], F32)
    p16 = sb("p16t", [16, 1024], BF16)
    psel = sb("pselt", [128, 128], BF16)
    pslot = sb("pslott", [8, 2048], BF16)
    negident = sb("negidentt", [128, 128], BF16)
    repl16 = sb("repl16t", [16, 128], BF16)
    mask_incl = sb("mask_inclt", [128, 128], F32)
    lnw_rep = sb("lnw_rep", [128, 128], F32)
    lnb_rep = sb("lnb_rep", [128, 128], F32)
    fbr = sb("fbr", [1, 1024], BF16)
    ws_rep = sb("ws_rep", [128, 1], F32)
    scl = sb("scl", [128, 2 * 1024], BF16)

    x_t = sb("x_tt", [128, D], F32)
    scr = sb("scr", [128, D], F32)
    stat = sb("stat", [128, 24], F32)
    xn_tm = sb("xn_tm", [128, D], F32)
    xn_fm = sb("xn_fm", [128, 8 * 128], BF16)
    qraw_tm = sb("qraw_tm", [128, 512], BF16)
    kraw_tm = sb("kraw_tm", [128, 512], BF16)
    vraw_tm = sb("vraw_tm", [128, 1024], BF16)
    bz = sb("bz", [128, 8], F32)
    qr_fm = sb("qr_fm", [64, 8 * 131], BF16)
    kr_fm = sb("kr_fm", [64, 8 * 131], BF16)
    vr_fm = sb("vr_fm", [128, 8 * 131], BF16)
    q_fm = sb("q_fm", [64, 8 * 128], BF16)
    k_fm = sb("k_fm", [64, 8 * 128], BF16)
    v_fm = sb("v_fm", [128, 8 * 128], BF16)
    k_tm = sb("k_tm", [128, 512], BF16)
    v_tm = sb("v_tm", [128, 1024], BF16)
    g_tm = sb("g_tm", [128, 1024], BF16)
    beta = sb("beta", [128, 8], F32)
    gin = sb("gin", [128, 1024], F32)
    igex = sb("igex", [128, 1024], F32)
    vt_w = sb("vt_w", [128, 1024], BF16)
    dbw = sb("dbw", [128, 1024], BF16)
    sigm = sb("sigm", [64, 1024], F32)
    gram = sb("gram", [128, 8 * 128], BF16)
    atil = sb("atil", [128, 8 * 128], BF16)
    s_st = sb("s_st", [64, 8 * 128], F32)
    s_stb = sb("s_stb", [64, 8 * 128], BF16)
    stag = sb("stag", [16, 8 * 128], BF16)
    gpk = sb("gpk", [16, 128], BF16)
    acc_int = sb("acc_int", [128, 128], BF16)
    gi_t = sb("gi_t", [128, 128], BF16)
    u_int = sb("u_int", [128, 128], BF16)
    u_ring = sb("u_ring", [8, 16 * 128], BF16)
    db_hp = sb("db_hp", [8, 16 * 128], BF16)
    gram_sub = sb("gram_sub", [8, 16], BF16)
    uh16 = sb("uh16", [16, 256], BF16)
    t8 = sb("t8", [8, 128], F32)
    u_big = sb("u_big", [128, 8 * 128], BF16)
    o_tm = sb("o_tm", [128, 1024], F32)
    res1 = sb("res1", [128, 1024], F32)
    za = scr
    lp = o_tm
    ew = res1
    og_fm = xn_fm
    n_fm = v_fm
    carry_qk = sb("carry_qk", [64, 16 * 3], BF16)
    carry_v = sb("carry_v", [128, 8 * 3], BF16)

    ps_big = ps("ps_big", [128, 1024], F32)
    ps_a = ps("ps_a", [128, 512], F32)
    ps_ks = ps("ps_ks", [16, 512], F32)
    ps_tr = ps("ps_tr", [128, 1024], BF16)
    ps_int = ps("ps_int", [128, 512], F32)
    ps_ub = ps("ps_ub", [128, 1024], F32)

    sems = {}
    cnt = {}
    for n in ["pe", "dve", "sc", "gp", "ld", "st"]:
        t = nc.semaphore("s_" + n); ctx.append(t)
        sems[n] = t.__enter__(); cnt[n] = 0

    dump_list = list(dumps)

    with nc.Block() as block:
        SEC = {"pe": block.tensor, "dve": block.vector, "sc": block.scalar,
               "gp": block.gpsimd, "sy": block.sync}

        dirty = {"dve": set(), "sc": set(), "gp": set()}

        class HazardGuard:
            def __init__(self, e, eng):
                self._e = e
                self._d = dirty[eng]
            def __getattr__(self, name):
                f = getattr(self._e, name)
                if name in ("wait_ge", "nop", "drain") or not callable(f):
                    return f
                if name == "dma_start":
                    return f
                def g(*a, **k):
                    wr, rd = set(), set()
                    for i, x in enumerate(a):
                        if isinstance(x, bass.AP):
                            (wr if i == 0 else rd).add(x.tensor.name)
                    for kw, x in k.items():
                        if isinstance(x, bass.AP):
                            (wr if kw in ("out", "outs") else rd).add(x.tensor.name)
                    if rd & self._d:
                        self._e.drain()
                        self._d.clear()
                    res = f(*a, **k)
                    self._d |= wr
                    return res
                return g

        def G(eng, fn, waits=(), sem=None, inc=1, stall=False):
            semn = sem or (eng if eng != "sy" else None)
            wl = [(n, v) for (n, v) in waits if v > 0]
            guard = eng in ("dve", "sc", "gp")
            if stall and guard:
                dirty[eng].clear()
            def body(e):
                for (n, v) in wl:
                    e.wait_ge(sems[n], v)
                last = fn(HazardGuard(e, eng) if guard else e)
                if semn is not None and inc and last is not None:
                    last.then_inc(sems[semn], inc)
            SEC[eng](body)
            if semn is not None and inc:
                cnt[semn] += inc

        def W(*names):
            return [(n, cnt[n]) for n in names]

        def WALL():
            return [(n, cnt[n]) for n in ["pe", "dve", "sc", "gp", "ld"]]

        def DMA(eng, out, in_, waits=(), sem="ld"):
            G(eng, lambda e: e.dma_start(out=out, in_=in_), waits, sem=sem, inc=16)

        def dump(tag, ap_fn, cur_p=None):
            for (t, pp, shp) in dump_list:
                if t == tag and (cur_p is None or pp == cur_p):
                    DMA("gp", dump_dr[tag], ap_fn(), waits=WALL(), sem="st")
                    stc = [("st", cnt["st"])]
                    for eng in ["dve", "sc", "pe", "gp"]:
                        G(eng, lambda e: e.nop(), waits=stc, inc=0)

        for nm, dst, cast in [("wqt", w_q, 1), ("wkt", w_k, 1), ("wvt", w_v, 1),
                              ("wat", w_a, 1), ("wgt", w_g, 1), ("wot", w_o, 1),
                              ("wft", w_f, 1)]:
            ncols = dr[nm].shape[1]
            DMA("gp", dst[:].rearrange("p (c n) -> p c n", n=ncols),
                dr[nm].rearrange("(c p) n -> p c n", p=128))
        DMA("gp", w_b[:].rearrange("p (c n) -> p c n", n=8),
            dr["wbt"].rearrange("(c p) n -> p c n", p=128))
        DMA("gp", ba_r[:], dr["ba"])
        DMA("gp", bb_r[:], dr["bb"])
        DMA("gp", ident[:], dr["ident"])
        DMA("sy", identf[:], dr["ident"])
        DMA("gp", pstack[:], dr["pstack"])
        DMA("gp", p16[:], dr["p16"])
        DMA("gp", psel[:], dr["psel"])
        DMA("gp", pslot[:], dr["pslot"])
        DMA("gp", negident[:], dr["negident"])
        DMA("gp", repl16[:], dr["repl16"])
        DMA("sy", taps_qk[:], dr["taps_qk"])
        DMA("sy", taps_v[:], dr["taps_v"])
        DMA("sy", cb_qk[:], dr["cb_qk"])
        DMA("sy", cb_v[:], dr["cb_v"])
        DMA("sy", trim_in[:], dr["trim_in"])
        DMA("sy", trim_iex[:], dr["trim_iex"])
        DMA("sy", w64b[:], dr["w64b"])
        DMA("sy", oh127[:], dr["oh127"])
        DMA("sy", gi_mask[:], dr["gi_mask"])
        DMA("sy", mask_incl[:], dr["mask_incl"])
        DMA("sy", lnw_rep[:], dr["lnw"][None, :].broadcast_to([128, 128]))
        DMA("sy", lnb_rep[:], dr["lnb"][None, :].broadcast_to([128, 128]))
        DMA("gp", fbr[:], dr["fbias"][None, :])
        DMA("sy", ws_rep[:], dr["wscale"][None, :].broadcast_to([128, 1]))

        G("dve", lambda e: e.memset(ones1[:], 1.0))
        G("dve", lambda e: e.memset(s_st[:], 0.0))
        G("dve", lambda e: e.memset(s_stb[:], 0.0))
        G("dve", lambda e: e.memset(carry_qk[:], 0.0))
        G("dve", lambda e: e.memset(carry_v[:], 0.0))

        for p in range(NOUT):
            r0 = p * C
            DMA("sy", x_t[:], dr["x"][r0:r0 + C, :], waits=WALL())

            def ln_stats(e):
                e.reduce_sum(stat[:, 0:1], x_t[:], axis=AX.X)
                return e.reduce_sum(stat[:, 1:2], scr[:], axis=AX.X)
            G("sc", lambda e: e.square(scr[:], x_t[:]), waits=W("ld", "dve"))
            G("dve", ln_stats, waits=W("sc", "ld"))
            def ln_stats2(e):
                e.tensor_scalar(stat[:, 2:3], stat[:, 0:1], 1.0 / D, None, ALU.mult)
                e.tensor_scalar(stat[:, 3:4], stat[:, 1:2], 1.0 / D, None, ALU.mult)
                e.tensor_mul(stat[:, 4:5], stat[:, 2:3], stat[:, 2:3])
                e.tensor_sub(stat[:, 5:6], stat[:, 3:4], stat[:, 4:5])
                return e.tensor_scalar(stat[:, 5:6], stat[:, 5:6], EPS_LN, None, ALU.add)
            G("dve", ln_stats2)
            G("sc", lambda e: e.activation(stat[:, 6:7], stat[:, 5:6], AF.Ln), waits=W("dve"))
            G("sc", lambda e: e.activation(stat[:, 7:8], stat[:, 6:7], AF.Exp, scale=-0.5))
            G("dve", lambda e: e.tensor_scalar(stat[:, 8:9], stat[:, 2:3], -1.0, None, ALU.mult), waits=W("sc"))
            G("dve", lambda e: e.tensor_scalar(xn_tm[:], x_t[:], stat[:, 8:9], stat[:, 7:8], ALU.add, ALU.mult))

            def xntr(e):
                last = None
                for j in range(8):
                    last = e.transpose(ps_big[:, j * 128:(j + 1) * 128],
                                       xn_tm[:, j * 128:(j + 1) * 128], identf[:])
                return last
            G("pe", xntr, waits=W("dve", "sc"))
            G("dve", lambda e: e.tensor_copy(xn_fm[:], ps_big[:]), waits=W("pe"), stall=True)
            dump("x_t", lambda: x_t[:], p)
            dump("xn_tm", lambda: xn_tm[:], p)
            dump("stat", lambda: stat[:], p)
            dump("xn_fm", lambda: xn_fm[:], p)

            def proj(dst_psum, w, wstride, nlo, nwid, bias_row=None):
                def f(e):
                    last = None
                    for kc in range(8):
                        last = e.matmul(dst_psum,
                                        xn_fm[:, kc * 128:(kc + 1) * 128],
                                        w[:, kc * wstride + nlo: kc * wstride + nlo + nwid],
                                        start=(kc == 0), stop=(kc == 7 and bias_row is None))
                    if bias_row is not None:
                        last = e.matmul(dst_psum, ones1[:], bias_row,
                                        start=False, stop=True)
                    return last
                return f
            def proj1(e):
                for half in range(2):
                    proj(ps_big[:, half * 512:(half + 1) * 512], w_a, 1024, half * 512, 512,
                         ba_r[:, half * 512:(half + 1) * 512])(e)
                proj(ps_a[:, 0:512], w_q, 512, 0, 512)(e)
                return proj(ps_ub[:, 0:512], w_k, 512, 0, 512)(e)
            G("pe", proj1, waits=W("sc", "dve"))
            def evac1(e):
                e.tensor_copy(za[:], ps_big[:])
                e.tensor_copy(qraw_tm[:], ps_a[:, 0:512])
                return e.tensor_copy(kraw_tm[:], ps_ub[:, 0:512])
            G("dve", evac1, waits=W("pe"), stall=True)
            def proj2(e):
                for half in range(2):
                    proj(ps_big[:, half * 512:(half + 1) * 512], w_v, 1024, half * 512, 512)(e)
                return proj(ps_a[:, 0:8], w_b, 8, 0, 8, bb_r[:])(e)
            G("pe", proj2, waits=W("dve"))
            def evac2(e):
                e.tensor_copy(vraw_tm[:], ps_big[:])
                return e.tensor_copy(bz[:], ps_a[:, 0:8])
            G("dve", evac2, waits=W("pe"), stall=True)

            def rawtr(src, dst, carry_t, carry_off, pw):
                def trs(e):
                    last = None
                    for j in range(8):
                        last = e.transpose(
                            ps_tr[0:pw, j * 128:(j + 1) * 128],
                            src[:, j * pw:(j + 1) * pw], ident[:])
                    return last
                G("pe", trs, waits=W("sc", "dve"))
                def evac(e):
                    e.tensor_copy(
                        dst[0:pw, :].rearrange("p (s c) -> p s c", c=131)[:, :, 3:131],
                        ps_tr[0:pw, :].rearrange("p (s c) -> p s c", c=128))
                    return e.tensor_copy(
                        dst[0:pw, :].rearrange("p (s c) -> p s c", c=131)[:, :, 0:3],
                        carry_t[0:pw, carry_off * 3:(carry_off + 8) * 3]
                            .rearrange("p (s c) -> p s c", c=3))
                G("dve", evac, waits=W("pe"), stall=True)
            rawtr(qraw_tm, qr_fm, carry_qk, 0, 64)
            rawtr(kraw_tm, kr_fm, carry_qk, 8, 64)
            rawtr(vraw_tm, vr_fm, carry_v, 0, 128)

            def cscale(srcf, tapt, gbase, pw, t4, reg, cbt):
                def f(e):
                    dst3 = scl[0:pw, reg * 1024:(reg + 1) * 1024].rearrange(
                        "p (s c) -> p s c", c=128)
                    src3 = srcf[0:pw, :].rearrange("p (s c) -> p s c", c=131)[:, :, t4:t4 + 128]
                    tap = tapt[0:pw, gbase * 4 + t4: gbase * 4 + t4 + 29: 4] \
                        .unsqueeze(2).broadcast_to([pw, 8, 128])
                    last = e.tensor_tensor(dst3, src3, tap, ALU.mult)
                    if t4 == 0:
                        cb3 = cbt[0:pw, gbase:gbase + 8].unsqueeze(2).broadcast_to([pw, 8, 128])
                        last = e.tensor_tensor(dst3, dst3, cb3, ALU.add)
                    return last
                return f
            def cmm(psr, pw, t4, reg):
                def f(e):
                    for half in range(2):
                        last = e.matmul(psr[0:pw, half * 512:(half + 1) * 512],
                                        ident[0:pw, 0:pw],
                                        scl[0:pw, reg * 1024 + half * 512: reg * 1024 + (half + 1) * 512],
                                        start=(t4 == 0), stop=(t4 == 3))
                    return last
                return f
            def conv_tensor(srcf, dstf, tapt, gbase, pw, psr, pre_waits, cbt):
                pe_after = []
                for t4 in range(4):
                    wd = ([("pe", pe_after[t4 - 2])] if t4 >= 2 else list(pre_waits))
                    G("dve", cscale(srcf, tapt, gbase, pw, t4, t4 % 2, cbt), waits=wd)
                    G("pe", cmm(psr, pw, t4, t4 % 2),
                      waits=W("dve") + (list(pre_waits) if t4 == 0 else []))
                    pe_after.append(cnt["pe"])
                G("sc", lambda e: e.activation(dstf[0:pw, 0:1024], psr[0:pw, 0:1024], AF.Silu),
                  waits=W("pe"))
            conv_tensor(qr_fm, q_fm, taps_qk, 0, 64, ps_big, [], cb_qk)
            conv_tensor(kr_fm, k_fm, taps_qk, 8, 64, ps_ub, W("pe"), cb_qk)
            conv_tensor(vr_fm, v_fm, taps_v, 0, 128, ps_big, W("pe", "sc"), cb_v)
            G("dve", lambda e: e.tensor_scalar(k_fm[0:64, :], k_fm[0:64, :], KSCALE, None, ALU.mult),
              waits=W("sc"))
            def save_carry(e):
                e.tensor_copy(carry_qk[0:64, 0:24].rearrange("p (s c) -> p s c", c=3),
                              qr_fm[0:64, :].rearrange("p (s c) -> p s c", c=131)[:, :, 128:131])
                e.tensor_copy(carry_qk[0:64, 24:48].rearrange("p (s c) -> p s c", c=3),
                              kr_fm[0:64, :].rearrange("p (s c) -> p s c", c=131)[:, :, 128:131])
                return e.tensor_copy(carry_v[:, 0:24].rearrange("p (s c) -> p s c", c=3),
                              vr_fm[:].rearrange("p (s c) -> p s c", c=131)[:, :, 128:131])
            G("dve", save_carry, waits=W("sc"))
            dump("q_fm", lambda: q_fm[:], p)
            dump("v_fm", lambda: v_fm[:], p)

            for half in range(2):
                G("pe", proj(ps_big[:, half * 512:(half + 1) * 512], w_g, 1024, half * 512, 512),
                  waits=W("dve", "sc"))
            G("sc", lambda e: e.activation(o_tm[:], ps_big[:], AF.Tanh, scale=0.5),
              waits=W("pe", "dve"))
            G("dve", lambda e: e.tensor_scalar(g_tm[:], o_tm[:], 1.0, 0.5, ALU.add, ALU.mult),
              waits=W("sc"))
            G("sc", lambda e: e.activation(bz[:, 0:8], bz[:, 0:8], AF.Tanh, scale=0.5),
              waits=W("dve"))
            G("dve", lambda e: e.tensor_scalar(beta[:], bz[:, 0:8], 1.0, 0.5, ALU.add, ALU.mult),
              waits=W("sc"))

            def ktrs(e):
                last = None
                for j in range(8):
                    last = e.transpose(ps_tr[:, j * 64:(j + 1) * 64],
                                       k_fm[0:64, j * 128:(j + 1) * 128], ident[0:64, 0:64])
                return last
            G("pe", ktrs, waits=W("sc", "dve"))
            G("dve", lambda e: e.tensor_copy(k_tm[:], ps_tr[:, 0:512]), waits=W("pe"), stall=True)
            def vtrs(e):
                last = None
                for j in range(8):
                    last = e.transpose(ps_tr[:, j * 128:(j + 1) * 128],
                                       v_fm[:, j * 128:(j + 1) * 128], ident[:])
                return last
            G("pe", vtrs, waits=W("dve"))
            G("dve", lambda e: e.tensor_copy(v_tm[:], ps_tr[:]), waits=W("pe"), stall=True)

            G("sc", lambda e: e.activation(ew[:], za[:], AF.Exp, scale=-1.0), waits=W("dve"))
            G("dve", lambda e: e.tensor_scalar(ew[:], ew[:], 1.0, None, ALU.add), waits=W("sc"))
            G("sc", lambda e: e.activation(lp[:], ew[:], AF.Ln), waits=W("dve"))
            G("dve", lambda e: e.tensor_tensor(
                dbw[:].rearrange("p (a b) -> p a b", b=128),
                ew[:].rearrange("p (a b) -> p a b", b=128),
                beta[:].unsqueeze(2).broadcast_to([128, 8, 128]),
                ALU.mult), waits=W("sc"))
            if p > 0:
                def sigmm(e):
                    last = None
                    for half in range(2):
                        last = e.matmul(ps_big[0:64, half * 512:(half + 1) * 512],
                                        w64b[:, 0:64], lp[:, half * 512:(half + 1) * 512],
                                        start=True, stop=True)
                    return last
                G("pe", sigmm, waits=W("sc", "dve"))
                G("sc", lambda e: e.activation(sigm[:], ps_big[0:64, :], AF.Exp), waits=W("pe"))
                G("dve", lambda e: e.tensor_mul(s_st[:], s_st[:], sigm[:]), waits=W("sc"))
                G("dve", lambda e: e.tensor_copy(s_stb[:], s_st[:]))
            def trimm(mat):
                def f(e):
                    last = None
                    for half in range(2):
                        last = e.matmul(ps_big[:, half * 512:(half + 1) * 512],
                                        mat[:], lp[:, half * 512:(half + 1) * 512],
                                        start=True, stop=True)
                    return last
                return f
            G("pe", trimm(trim_in), waits=W("dve", "sc"))
            G("sc", lambda e: e.activation(gin[:], ps_big[:], AF.Exp), waits=W("pe"))
            G("pe", trimm(trim_iex), waits=W("sc"))
            G("sc", lambda e: e.activation(igex[:], ps_big[:], AF.Exp), waits=W("pe"))
            G("dve", lambda e: e.tensor_mul(vt_w[:], v_tm[:], igex[:]), waits=W("sc"))
            dump("gin", lambda: gin[:], p)
            dump("igex", lambda: igex[:], p)

            DMA("gp", dr["scr_db"], dbw[:], waits=W("dve"))

            G("sc", lambda e: e.square(scr[:], xn_tm[:]), waits=W("dve"))
            G("dve", lambda e: e.reduce_sum(stat[:, 16:17], scr[:], axis=AX.X), waits=W("sc"))
            def rmsst(e):
                e.tensor_scalar(stat[:, 17:18], stat[:, 16:17], 1.0 / D, None, ALU.mult)
                return e.tensor_scalar(stat[:, 17:18], stat[:, 17:18], EPS_RMS, None, ALU.add)
            G("dve", rmsst)
            G("sc", lambda e: e.activation(stat[:, 18:19], stat[:, 17:18], AF.Ln), waits=W("dve"))
            G("sc", lambda e: e.activation(stat[:, 19:20], stat[:, 18:19], AF.Exp, scale=-0.5))
            G("dve", lambda e: e.tensor_scalar(scr[:], xn_tm[:], stat[:, 19:20], None, ALU.mult),
              waits=W("sc"))
            G("dve", lambda e: e.reduce_max(stat[:, 20:21], scr[:], axis=AX.X,
                                            apply_absolute_value=True))
            G("dve", lambda e: e.tensor_scalar(stat[:, 20:21], stat[:, 20:21], EPS_ACT, None, ALU.max))
            G("sc", lambda e: e.activation(stat[:, 21:22], stat[:, 20:21], AF.Ln), waits=W("dve"))
            G("sc", lambda e: e.activation(stat[:, 22:23], stat[:, 21:22], AF.Exp, scale=-1.0))
            G("dve", lambda e: e.tensor_scalar(stat[:, 23:24], stat[:, 22:23], 127.0, None, ALU.mult),
              waits=W("sc"))
            G("dve", lambda e: e.tensor_scalar(scr[:], scr[:], stat[:, 23:24], RND, ALU.mult, ALU.add))
            G("dve", lambda e: e.tensor_scalar(scr[:], scr[:], RND, None, ALU.subtract))
            def oscale(e):
                e.tensor_mul(stat[:, 22:23], stat[:, 20:21], ws_rep[:])
                return e.tensor_scalar(stat[:, 22:23], stat[:, 22:23], 1.0 / 127.0, None, ALU.mult)
            G("dve", oscale)
            G("dve", lambda e: e.tensor_copy(dbw[:], scr[:]), waits=W("ld"))
            def ntrs(e):
                last = None
                for j in range(8):
                    last = e.transpose(ps_tr[:, j * 128:(j + 1) * 128],
                                       dbw[:, j * 128:(j + 1) * 128], ident[:])
                return last
            G("pe", ntrs, waits=W("dve", "sc"))
            G("dve", lambda e: e.tensor_copy(n_fm[:], ps_tr[:]), waits=W("pe"), stall=True)

            def hk(hh):
                return k_fm[0:64, hh * 128:(hh + 1) * 128]
            def hq(hh):
                return q_fm[0:64, hh * 128:(hh + 1) * 128]
            def gramms(e):
                last = None
                for hh in range(8):
                    last = e.matmul(ps_big[:, hh * 128:(hh + 1) * 128], hk(hh), hk(hh),
                                    start=True, stop=True)
                return last
            G("pe", gramms, waits=W("dve", "sc"))
            G("dve", lambda e: e.tensor_scalar(gram[:], ps_big[:], -1.0, None, ALU.mult),
              waits=W("pe"), stall=True)
            def atilms(e):
                last = None
                for hh in range(8):
                    last = e.matmul(ps_big[:, hh * 128:(hh + 1) * 128], hk(hh), hq(hh),
                                    start=True, stop=True)
                return last
            G("pe", atilms, waits=W("dve"))
            G("dve", lambda e: e.tensor_mul(
                atil[:].rearrange("p (h b) -> p h b", b=128),
                ps_big[:].rearrange("p (h b) -> p h b", b=128),
                mask_incl[:].unsqueeze(1).broadcast_to([128, 8, 128])),
              waits=W("pe"), stall=True)
            DMA("gp", dr["scr_gr"], gram[:], waits=W("dve"))
            def qs0(e):
                last = None
                for hh in range(8):
                    last = e.matmul(ps_big[:, hh * 128:(hh + 1) * 128], hq(hh),
                                    s_stb[:, hh * 128:(hh + 1) * 128],
                                    start=(hh % 4 == 0), stop=False)
                return last
            G("pe", qs0, waits=W("dve", "sc"))

            for i16 in range(8):
                t0 = i16 * 16
                DMA("gp", db_hp[:].rearrange("h (s v) -> h s v", v=128),
                    dr["scr_db"][t0:t0 + 16, :].rearrange("s (h v) -> h s v", v=128),
                    waits=W("dve", "ld"))
                if t0 > 0 or True:
                    from concourse.bass import AP as _AP
                    src = _AP(dr["scr_gr"].tensor,
                              (t0 + 1) * 1024 + t0,
                              [[128, 8], [1025, 15]])
                    def _gsub(e, src=src):
                        return e.dma_start(out=gram_sub[:, 1:16], in_=src,
                                           allow_slow_non_contiguous=True)
                    G("gp", _gsub, waits=W("dve", "pe", "ld"), sem="ld", inc=16)
                for half in range(2):
                    hs = range(half * 4, half * 4 + 4)
                    def ksmms(e, hs=hs, i16=i16, t0=t0, half=half):
                        last = None
                        first = True
                        for hh in hs:
                            c0 = (hh % 4) * 128
                            last = e.matmul(ps_ks[0:16, c0:c0 + 128],
                                            hk(hh)[:, t0:t0 + 16],
                                            s_stb[:, hh * 128:(hh + 1) * 128],
                                            start=first, stop=False)
                            first = False
                            if i16 > 0:
                                last = e.matmul(ps_ks[0:16, c0:c0 + 128],
                                                gram[0:t0, hh * 128 + t0: hh * 128 + t0 + 16],
                                                u_big[0:t0, hh * 128:(hh + 1) * 128],
                                                start=False, stop=False)
                        last = e.matmul(ps_ks[0:16, 0:512],
                                        negident[:, t0:t0 + 16],
                                        vt_w[:, half * 512:(half + 1) * 512],
                                        start=False, stop=True)
                        return last
                    G("pe", ksmms, waits=W("dve", "sc"))
                    G("dve", lambda e, half=half: e.tensor_copy(
                        stag[:, half * 512:(half + 1) * 512], ps_ks[0:16, 0:512]),
                      waits=W("pe"))
                def permin(e):
                    last = None
                    for hh in range(8):
                        last = e.matmul(ps_int[:, 128:256], pstack[:, hh * 128:(hh + 1) * 128],
                                        stag[:, hh * 128:(hh + 1) * 128],
                                        start=(hh == 0), stop=(hh == 7))
                    return last
                G("pe", permin, waits=W("dve"))
                G("dve", lambda e: e.tensor_copy(acc_int[:], ps_int[:, 128:256]), waits=W("pe"))
                def gpkmm(e, t0=t0):
                    last = None
                    for hh in range(8):
                        last = e.matmul(ps_ks[0:16, hh * 16:(hh + 1) * 16],
                                        hk(hh)[:, t0:t0 + 16], hk(hh)[:, t0:t0 + 16],
                                        start=True, stop=True)
                    return last
                G("pe", gpkmm, waits=W("dve"))
                G("dve", lambda e: e.tensor_copy(gpk[:], ps_ks[0:16, 0:128]), waits=W("pe"))
                G("pe", lambda e: e.matmul(ps_int[:, 256:384], repl16[:], gpk[:],
                                           start=True, stop=True), waits=W("dve"))
                G("dve", lambda e: e.tensor_mul(
                    gi_t[:].rearrange("p (s h) -> p s h", h=8),
                    ps_int[:, 256:384].rearrange("p (h s) -> p s h", h=8),
                    gi_mask[:].rearrange("p (s h) -> p s h", h=8)), waits=W("pe"))
                G("dve", lambda e: e.memset(u_int[:], 0.0))
                for s16 in range(16):
                    sl = (s16 % 2) * 128
                    def pegrp(e, s16=s16, sl=sl):
                        if s16 >= 1:
                            e.matmul(ps_int[:, 0:128],
                                     pslot[:, (s16 - 1) * 128:s16 * 128],
                                     u_ring[:, (s16 - 1) * 128:s16 * 128],
                                     start=(s16 == 1), stop=False)
                        last = e.matmul(ps_a[0:8, sl:sl + 128],
                                        ident[:, 8 * s16:8 * s16 + 8], acc_int[:],
                                        start=True, stop=(s16 < 2))
                        if s16 >= 2:
                            last = e.matmul(ps_a[0:8, sl:sl + 128],
                                            gi_t[:, 8 * s16:8 * s16 + 8], u_int[:],
                                            start=False, stop=True)
                        return last
                    G("pe", pegrp, waits=W("dve"))
                    def dvegrp(e, s16=s16, sl=sl):
                        if s16 >= 1:
                            e.tensor_copy(u_int[:], ps_int[:, 0:128])
                        if s16 == 0:
                            return e.tensor_mul(
                                u_ring[:, 0:128], ps_a[0:8, sl:sl + 128], db_hp[:, 0:128])
                        e.scalar_tensor_tensor(
                            t8[:], u_ring[:, (s16 - 1) * 128:s16 * 128],
                            gram_sub[:, s16:s16 + 1],
                            ps_a[0:8, sl:sl + 128], ALU.mult, ALU.add)
                        return e.tensor_mul(
                            u_ring[:, s16 * 128:(s16 + 1) * 128],
                            t8[:], db_hp[:, s16 * 128:(s16 + 1) * 128])
                    G("dve", dvegrp, waits=W("pe", "ld"), stall=True)
                G("pe", lambda e: e.matmul(
                    ps_int[:, 0:128], pslot[:, 15 * 128:16 * 128],
                    u_ring[:, 15 * 128:16 * 128],
                    start=False, stop=True), waits=W("dve"))
                G("dve", lambda e: e.tensor_copy(u_int[:], ps_int[:, 0:128]),
                  waits=W("pe"))
                for hh in range(9):
                    def pe_ub(e, hh=hh, i16=i16):
                        last = None
                        if hh >= 1:
                            last = e.matmul(
                                ps_ub[:, (hh - 1) * 128:hh * 128],
                                p16[:, i16 * 128:(i16 + 1) * 128],
                                uh16[0:16, ((hh - 1) % 2) * 128:((hh - 1) % 2) * 128 + 128],
                                start=(i16 == 0 and (hh - 1) % 4 == 0), stop=(i16 == 7))
                        if hh < 8:
                            last = e.matmul(
                                ps_int[0:16, 128 + (hh % 2) * 128:256 + (hh % 2) * 128],
                                psel[:, hh * 16:(hh + 1) * 16], u_int[:],
                                start=True, stop=True)
                        return last
                    G("pe", pe_ub, waits=W("dve"))
                    if hh < 8:
                        G("dve", lambda e, hh=hh: e.tensor_copy(
                            uh16[0:16, (hh % 2) * 128:(hh % 2) * 128 + 128],
                            ps_int[0:16, 128 + (hh % 2) * 128:256 + (hh % 2) * 128]),
                          waits=W("pe"))
                G("dve", lambda e: e.tensor_copy(u_big[:], ps_ub[:]), waits=W("pe"))

            dump("u_big", lambda: u_big[:], p)
            def ocouple(e):
                last = None
                for hh in range(8):
                    last = e.matmul(ps_big[:, hh * 128:(hh + 1) * 128],
                                    atil[:, hh * 128:(hh + 1) * 128],
                                    u_big[:, hh * 128:(hh + 1) * 128],
                                    start=False, stop=True)
                return last
            G("pe", ocouple, waits=W("dve", "sc"))
            G("dve", lambda e: e.tensor_mul(o_tm[:], gin[:], ps_big[:]), waits=W("pe"))
            dump("o_tm", lambda: o_tm[:], p)

            def repl127(e):
                last = None
                for half in range(2):
                    last = e.matmul(ps_big[0:64, half * 512:(half + 1) * 512],
                                    oh127[:, 0:64], gin[:, half * 512:(half + 1) * 512],
                                    start=True, stop=True)
                return last
            G("pe", repl127, waits=W("dve"))
            G("dve", lambda e: e.tensor_copy(sigm[:], ps_big[0:64, :]), waits=W("pe"))
            for hp in range(2):
                def wmms(e, hp=hp):
                    last = None
                    for hh in range(4 * hp, 4 * hp + 4):
                        last = e.matmul(ps_a[0:64, (hh % 4) * 128:((hh % 4) + 1) * 128],
                                        k_tm[:, hh * 64:(hh + 1) * 64],
                                        u_big[:, hh * 128:(hh + 1) * 128],
                                        start=True, stop=True)
                    return last
                G("pe", wmms, waits=W("dve", "sc"))
                sl = slice(hp * 512, (hp + 1) * 512)
                G("dve", lambda e, sl=sl: e.tensor_sub(s_st[:, sl], s_st[:, sl],
                                                       ps_a[0:64, 0:512]), waits=W("pe"))
            G("dve", lambda e: e.tensor_mul(s_st[:], s_st[:], sigm[:]))
            G("dve", lambda e: e.tensor_copy(s_stb[:], s_st[:]))
            dump("s_st", lambda: s_st[:], p)

            G("sc", lambda e: e.square(scr[:], o_tm[:]), waits=W("dve"))
            def gstats(e):
                e.reduce_sum(stat[:, 0:8], o_tm[:].rearrange("p (a b) -> p a b", b=128), axis=AX.X)
                return e.reduce_sum(stat[:, 8:16], scr[:].rearrange("p (a b) -> p a b", b=128), axis=AX.X)
            G("dve", gstats, waits=W("sc"))
            def gstats2(e):
                e.tensor_scalar(stat[:, 0:8], stat[:, 0:8], 1.0 / DV, None, ALU.mult)
                e.tensor_scalar(stat[:, 8:16], stat[:, 8:16], 1.0 / DV, None, ALU.mult)
                e.tensor_mul(scr[:, 0:8], stat[:, 0:8], stat[:, 0:8])
                e.tensor_sub(stat[:, 8:16], stat[:, 8:16], scr[:, 0:8])
                return e.tensor_scalar(stat[:, 8:16], stat[:, 8:16], EPS_GN, None, ALU.add)
            G("dve", gstats2)
            G("sc", lambda e: e.activation(stat[:, 8:16], stat[:, 8:16], AF.Ln), waits=W("dve"))
            G("sc", lambda e: e.activation(stat[:, 8:16], stat[:, 8:16], AF.Exp, scale=-0.5))
            def onorm(e):
                o3 = o_tm[:].rearrange("p (a b) -> p a b", b=128)
                mu = stat[:, 0:8].unsqueeze(2).broadcast_to([128, 8, 128])
                rs = stat[:, 8:16].unsqueeze(2).broadcast_to([128, 8, 128])
                e.tensor_sub(o3, o3, mu)
                e.tensor_mul(o3, o3, rs)
                e.tensor_mul(o_tm[:].rearrange("p (a b) -> p a b", b=128),
                             o_tm[:].rearrange("p (a b) -> p a b", b=128),
                             lnw_rep[:].unsqueeze(1).broadcast_to([128, 8, 128]))
                e.tensor_add(o_tm[:].rearrange("p (a b) -> p a b", b=128),
                             o_tm[:].rearrange("p (a b) -> p a b", b=128),
                             lnb_rep[:].unsqueeze(1).broadcast_to([128, 8, 128]))
                return e.tensor_mul(o_tm[:], o_tm[:], g_tm[:])
            G("dve", onorm, waits=W("sc"))
            def ogtrs(e):
                last = None
                for j in range(8):
                    last = e.transpose(ps_big[:, j * 128:(j + 1) * 128],
                                       o_tm[:, j * 128:(j + 1) * 128], identf[:])
                return last
            G("pe", ogtrs, waits=W("dve", "sc"))
            G("dve", lambda e: e.tensor_copy(og_fm[:], ps_big[:]), waits=W("pe"), stall=True)
            def wfmms(e):
                last = None
                for half in range(2):
                    dst = ps_big[:, half * 512:(half + 1) * 512]
                    for kc in range(8):
                        last = e.matmul(dst, og_fm[:, kc * 128:(kc + 1) * 128],
                                        w_o[:, kc * 1024 + half * 512: kc * 1024 + half * 512 + 512],
                                        start=(kc == 0), stop=False)
                    last = e.matmul(dst, ones1[:], fbr[:, half * 512:(half + 1) * 512],
                                    start=False, stop=True)
                    dstf = ps_ub[:, half * 512:(half + 1) * 512]
                    for kc in range(8):
                        last = e.matmul(dstf, n_fm[:, kc * 128:(kc + 1) * 128],
                                        w_f[:, kc * 1024 + half * 512: kc * 1024 + half * 512 + 512],
                                        start=(kc == 0), stop=(kc == 7))
                return last
            G("pe", wfmms, waits=W("sc", "dve"))
            def resfin(e):
                e.tensor_add(res1[:], x_t[:], ps_big[:])
                return e.scalar_tensor_tensor(res1[:], ps_ub[:], stat[:, 22:23],
                                              res1[:], ALU.mult, ALU.add)
            G("dve", resfin, waits=W("pe"), stall=True)
            DMA("sy", dr["out"][r0:r0 + C, :], res1[:], waits=WALL(), sem="st")
            dump("out_rows", lambda: res1[:], p)

        G("sy", lambda e: e.wait_ge(sems["st"], cnt["st"]) or None, waits=(), sem=None)

    for t in reversed(ctx):
        t.__exit__(None, None, None)
    return nc





_BUILD_CACHE = {}

def kernel(**inputs):
    import numpy as np
    from concourse.bass_utils import run_bass_kernel_spmd
    T = 8192
    x_full = np.asarray(inputs["L0_attn_x"])
    B = x_full.shape[0]
    g = host_prep(inputs)
    if T not in _BUILD_CACHE:
        _BUILD_CACHE[T] = build(T)
    nc = _BUILD_CACHE[T]
    in_maps = []
    for c in range(8):
        im = dict(g)
        im["x"] = np.ascontiguousarray(x_full[c % B])
        in_maps.append(im)
    res = run_bass_kernel_spmd(nc, in_maps, core_ids=list(range(8)))
    out = np.stack([np.asarray(res.results[b]["out"]) for b in range(B)], axis=0)
    return out.astype(np.float32)


# version 08
# speedup vs baseline: 1.7722x; 1.4820x over previous
import numpy as np
import ml_dtypes
import concourse.bass as bass
import concourse.mybir as mybir

F32 = mybir.dt.float32
BF16 = mybir.dt.bfloat16
AF = mybir.ActivationFunctionType
ALU = mybir.AluOpType
AX = mybir.AxisListType

H, DK, DV, D = 8, 64, 128, 1024
C = 128
EPS_LN = 1e-5
EPS_GN = 1e-5
EPS_RMS = 1e-6
EPS_ACT = 1e-5
KSCALE = DK ** (-0.5)
RND = 12582912.0


def host_constants():
    r = np.arange(128)
    tri_incl = (r[:, None] <= r[None, :]).astype(np.float32)
    tri_excl = (r[:, None] < r[None, :]).astype(np.float32)
    mlow = ((r[:, None] < 64).astype(np.float32)) * np.ones((1, 128), np.float32)
    c = {}
    c["ident"] = np.eye(128, dtype=np.float32)
    c["trim_in"] = (mlow - tri_incl)
    c["trim_iex"] = (tri_excl - mlow)
    c["w64b"] = (-(r < 64).astype(np.float32))[:, None] * np.ones((1, 64), np.float32)
    c["oh127"] = (r == 127).astype(np.float32)[:, None] * np.ones((1, 64), np.float32)
    pstack = np.zeros((16, 8 * 128), np.float32)
    for h in range(8):
        for s in range(16):
            pstack[s, h * 128 + s * 8 + h] = 1.0
    c["pstack"] = pstack
    bm = np.zeros((128, 16 * 128), np.float32)
    for h in range(8):
        for rr in range(16):
            for s in range(rr + 1, 16):
                bm[rr * 8 + h, rr * 128 + s * 8 + h] = -1.0
    c["bmask"] = bm
    c["negident"] = -np.eye(128, dtype=np.float32)
    c["mask_incl"] = -tri_incl
    return c


def host_prep(inputs):
    g = {}
    tr = lambda a: np.ascontiguousarray(np.asarray(a).T).astype(np.float32)
    g["wqt"] = tr(inputs["L0_attn_Wq"])
    g["wkt"] = tr(inputs["L0_attn_Wk"])
    g["wvt"] = tr(inputs["L0_attn_Wv"])
    g["wat"] = tr(inputs["L0_attn_Wa"])
    g["wgt"] = tr(inputs["L0_attn_Wg"])
    g["wbt"] = tr(inputs["L0_attn_Wb"])
    g["wot"] = tr(inputs["L0_attn_Wo"])
    g["wft"] = tr(inputs["L0_ffn_w_int8"])
    g["ba"] = np.asarray(inputs["L0_attn_ba"]).reshape(1, D).astype(np.float32)
    g["bb"] = np.asarray(inputs["L0_attn_bb"]).reshape(1, 8).astype(np.float32)
    tqk = np.concatenate([
        np.asarray(inputs["L0_attn_qc_w"]).reshape(512, 4),
        np.asarray(inputs["L0_attn_kc_w"]).reshape(512, 4)], axis=0).astype(np.float32)
    g["taps_qk"] = np.ascontiguousarray(tqk.reshape(16, 64, 4).transpose(1, 0, 2).reshape(64, 64))
    tv = np.asarray(inputs["L0_attn_vc_w"]).reshape(1024, 4).astype(np.float32)
    g["taps_v"] = np.ascontiguousarray(tv.reshape(8, 128, 4).transpose(1, 0, 2).reshape(128, 32))
    cqk = np.concatenate([
        np.asarray(inputs["L0_attn_qc_b"]).reshape(512),
        np.asarray(inputs["L0_attn_kc_b"]).reshape(512)]).astype(np.float32)
    g["cb_qk"] = np.ascontiguousarray(cqk.reshape(16, 64).T)
    g["cb_v"] = np.ascontiguousarray(
        np.asarray(inputs["L0_attn_vc_b"]).reshape(8, 128).T.astype(np.float32))
    g["lnw"] = np.asarray(inputs["L0_attn_ln_w"]).reshape(128).astype(np.float32)
    g["lnb"] = np.asarray(inputs["L0_attn_ln_b"]).reshape(128).astype(np.float32)
    g["fbias"] = np.asarray(inputs["L0_ffn_bias"]).reshape(D).astype(np.float32)
    g["wscale"] = np.asarray(inputs["L0_ffn_weight_scale"]).reshape(1).astype(np.float32)
    for k, v in host_constants().items():
        g[k] = v.astype(np.float32)
    return g


def build(T, dumps=()):
    assert T % C == 0
    NOUT = T // C
    nc = bass.Bass("TRN2", target_bir_lowering=False, debug=False)
    dr = {}
    dr["x"] = nc.dram_tensor("x", [T, D], F32, kind="ExternalInput").ap()
    for nm, shp in [("wqt", [D, 512]), ("wkt", [D, 512]), ("wvt", [D, D]),
                    ("wat", [D, D]), ("wgt", [D, D]), ("wbt", [D, 8]),
                    ("wot", [D, D]), ("wft", [D, D]), ("ba", [1, D]),
                    ("bb", [1, 8]), ("taps_qk", [64, 64]), ("taps_v", [128, 32]),
                    ("cb_qk", [64, 16]), ("cb_v", [128, 8]),
                    ("lnw", [128]), ("lnb", [128]), ("fbias", [D]), ("wscale", [1]),
                    ("ident", [128, 128]), ("trim_in", [128, 128]),
                    ("trim_iex", [128, 128]), ("w64b", [128, 64]),
                    ("oh127", [128, 64]), ("pstack", [16, 1024]),
                    ("bmask", [128, 2048]),
                    ("negident", [128, 128]),
                    ("mask_incl", [128, 128])]:
        dr[nm] = nc.dram_tensor(nm, shp, F32, kind="ExternalInput").ap()
    dr["out"] = nc.dram_tensor("out", [T, D], F32, kind="ExternalOutput").ap()
    dr["scr_db"] = nc.dram_tensor("scr_db", [128, 1024], BF16, kind="Internal").ap()
    dr["scr_ub"] = nc.dram_tensor("scr_ub", [128, 1024], BF16, kind="Internal").ap()
    dump_dr = {}
    for tag, _p, shp in dumps:
        dump_dr[tag] = nc.dram_tensor("dbg_" + tag, list(shp), F32,
                                      kind="ExternalOutput").ap()

    ctx = []
    def sb(name, shape, dt):
        t = nc.sbuf_tensor(name, list(shape), dt); ctx.append(t); return t.__enter__()
    def ps(name, shape, dt):
        t = nc.psum_tensor(name, list(shape), dt); ctx.append(t); return t.__enter__()

    w_q = sb("w_q", [128, 8 * 512], BF16)
    w_k = sb("w_k", [128, 8 * 512], BF16)
    w_v = sb("w_v", [128, 8 * 1024], BF16)
    w_a = sb("w_a", [128, 8 * 1024], BF16)
    w_g = sb("w_g", [128, 8 * 1024], BF16)
    w_b = sb("w_b", [128, 8 * 8], BF16)
    w_o = sb("w_o", [128, 8 * 1024], BF16)
    w_f = sb("w_f", [128, 8 * 1024], BF16)
    ba_r = sb("ba_r", [1, D], BF16)
    bb_r = sb("bb_r", [1, 8], BF16)
    ones1 = sb("ones1", [1, 128], BF16)
    taps_qk = sb("taps_qkt", [64, 64], F32)
    taps_v = sb("taps_vt", [128, 32], F32)
    cb_qk = sb("cb_qkt", [64, 16], F32)
    cb_v = sb("cb_vt", [128, 8], F32)
    ident = sb("identt", [128, 128], BF16)
    identf = sb("identft", [128, 128], F32)
    trim_in = sb("trim_int", [128, 128], F32)
    trim_iex = sb("trim_iext", [128, 128], F32)
    w64b = sb("w64bt", [128, 64], F32)
    oh127 = sb("oh127t", [128, 64], F32)
    pstack = sb("pstackt", [16, 1024], BF16)
    bmask = sb("bmaskt", [128, 2048], BF16)
    negident = sb("negidentt", [128, 128], BF16)
    mask_incl = sb("mask_inclt", [128, 128], F32)
    lnw_rep = sb("lnw_rep", [128, 128], F32)
    lnb_rep = sb("lnb_rep", [128, 128], F32)
    fbr = sb("fbr", [1, 1024], BF16)
    ws_rep = sb("ws_rep", [128, 1], F32)
    scl = sb("scl", [128, 2 * 1024], BF16)

    x_t = sb("x_tt", [128, D], F32)
    scr = sb("scr", [128, D], F32)
    stat = sb("stat", [128, 24], F32)
    xn_tm = sb("xn_tm", [128, D], F32)
    xn_fm = sb("xn_fm", [128, 8 * 128], BF16)
    qraw_tm = sb("qraw_tm", [128, 512], BF16)
    kraw_tm = sb("kraw_tm", [128, 512], BF16)
    vraw_tm = sb("vraw_tm", [128, 1024], BF16)
    bz = sb("bz", [128, 8], F32)
    qr_fm = sb("qr_fm", [64, 8 * 131], BF16)
    kr_fm = sb("kr_fm", [64, 8 * 131], BF16)
    vr_fm = sb("vr_fm", [128, 8 * 131], BF16)
    q_fm = sb("q_fm", [64, 8 * 128], BF16)
    k_fm = sb("k_fm", [64, 8 * 128], BF16)
    v_fm = sb("v_fm", [128, 8 * 128], BF16)
    k_tm = sb("k_tm", [128, 512], BF16)
    v_tm = sb("v_tm", [128, 1024], BF16)
    g_tm = sb("g_tm", [128, 1024], BF16)
    beta = sb("beta", [128, 8], F32)
    gin = sb("gin", [128, 1024], F32)
    igex = sb("igex", [128, 1024], F32)
    vt_w = sb("vt_w", [128, 1024], BF16)
    dbw = sb("dbw", [128, 1024], BF16)
    sigm = sb("sigm", [64, 1024], F32)
    gram = sb("gram", [128, 8 * 128], BF16)
    atil = sb("atil", [128, 8 * 128], BF16)
    s_st = sb("s_st", [64, 8 * 128], F32)
    s_stb = sb("s_stb", [64, 8 * 128], BF16)
    stag = sb("stag", [16, 8 * 128], BF16)
    c_steps = sb("c_steps", [128, 16 * 128], BF16)
    k_slab = sb("k_slab", [64, 1024], BF16)
    u_slab = sb("u_slab", [128, 128], BF16)
    db_sh = sb("db_sh", [128, 8 * 128], BF16)
    u_big = sb("u_big", [128, 8 * 128], BF16)
    o_tm = sb("o_tm", [128, 1024], F32)
    res1 = sb("res1", [128, 1024], F32)
    za = scr
    lp = o_tm
    ew = res1
    og_fm = xn_fm
    n_fm = v_fm
    carry_qk = sb("carry_qk", [64, 16 * 3], BF16)
    carry_v = sb("carry_v", [128, 8 * 3], BF16)

    ps_big = ps("ps_big", [128, 1024], F32)
    ps_a = ps("ps_a", [128, 512], F32)
    ps_ks = ps("ps_ks", [16, 512], F32)
    ps_tr = ps("ps_tr", [128, 1024], BF16)
    ps_int = ps("ps_int", [128, 512], F32)
    ps_ub = ps("ps_ub", [128, 1024], F32)

    sems = {}
    cnt = {}
    for n in ["pe", "dve", "sc", "gp", "ld", "st"]:
        t = nc.semaphore("s_" + n); ctx.append(t)
        sems[n] = t.__enter__(); cnt[n] = 0

    dump_list = list(dumps)

    with nc.Block() as block:
        SEC = {"pe": block.tensor, "dve": block.vector, "sc": block.scalar,
               "gp": block.gpsimd, "sy": block.sync}

        dirty = {"dve": set(), "sc": set(), "gp": set()}

        class HazardGuard:
            def __init__(self, e, eng):
                self._e = e
                self._d = dirty[eng]
            def __getattr__(self, name):
                f = getattr(self._e, name)
                if name in ("wait_ge", "nop", "drain") or not callable(f):
                    return f
                if name == "dma_start":
                    return f
                def g(*a, **k):
                    wr, rd = set(), set()
                    for i, x in enumerate(a):
                        if isinstance(x, bass.AP):
                            (wr if i == 0 else rd).add(x.tensor.name)
                    for kw, x in k.items():
                        if isinstance(x, bass.AP):
                            (wr if kw in ("out", "outs") else rd).add(x.tensor.name)
                    if rd & self._d:
                        self._e.drain()
                        self._d.clear()
                    res = f(*a, **k)
                    self._d |= wr
                    return res
                return g

        def G(eng, fn, waits=(), sem=None, inc=1, stall=False):
            semn = sem or (eng if eng != "sy" else None)
            wl = [(n, v) for (n, v) in waits if v > 0]
            guard = eng in ("dve", "sc", "gp")
            if stall and guard:
                dirty[eng].clear()
            def body(e):
                for (n, v) in wl:
                    e.wait_ge(sems[n], v)
                last = fn(HazardGuard(e, eng) if guard else e)
                if semn is not None and inc and last is not None:
                    last.then_inc(sems[semn], inc)
            SEC[eng](body)
            if semn is not None and inc:
                cnt[semn] += inc

        def W(*names):
            return [(n, cnt[n]) for n in names]

        def WALL():
            return [(n, cnt[n]) for n in ["pe", "dve", "sc", "gp", "ld"]]

        def DMA(eng, out, in_, waits=(), sem="ld"):
            G(eng, lambda e: e.dma_start(out=out, in_=in_), waits, sem=sem, inc=16)

        def dump(tag, ap_fn, cur_p=None):
            for (t, pp, shp) in dump_list:
                if t == tag and (cur_p is None or pp == cur_p):
                    DMA("gp", dump_dr[tag], ap_fn(), waits=WALL(), sem="st")
                    stc = [("st", cnt["st"])]
                    for eng in ["dve", "sc", "pe", "gp"]:
                        G(eng, lambda e: e.nop(), waits=stc, inc=0)

        for nm, dst, cast in [("wqt", w_q, 1), ("wkt", w_k, 1), ("wvt", w_v, 1),
                              ("wat", w_a, 1), ("wgt", w_g, 1), ("wot", w_o, 1),
                              ("wft", w_f, 1)]:
            ncols = dr[nm].shape[1]
            DMA("gp", dst[:].rearrange("p (c n) -> p c n", n=ncols),
                dr[nm].rearrange("(c p) n -> p c n", p=128))
        DMA("gp", w_b[:].rearrange("p (c n) -> p c n", n=8),
            dr["wbt"].rearrange("(c p) n -> p c n", p=128))
        DMA("gp", ba_r[:], dr["ba"])
        DMA("gp", bb_r[:], dr["bb"])
        DMA("gp", ident[:], dr["ident"])
        DMA("sy", identf[:], dr["ident"])
        DMA("gp", pstack[:], dr["pstack"])
        DMA("gp", bmask[:], dr["bmask"])
        DMA("gp", negident[:], dr["negident"])
        DMA("sy", taps_qk[:], dr["taps_qk"])
        DMA("sy", taps_v[:], dr["taps_v"])
        DMA("sy", cb_qk[:], dr["cb_qk"])
        DMA("sy", cb_v[:], dr["cb_v"])
        DMA("sy", trim_in[:], dr["trim_in"])
        DMA("sy", trim_iex[:], dr["trim_iex"])
        DMA("sy", w64b[:], dr["w64b"])
        DMA("sy", oh127[:], dr["oh127"])
        DMA("sy", mask_incl[:], dr["mask_incl"])
        DMA("sy", lnw_rep[:], dr["lnw"][None, :].broadcast_to([128, 128]))
        DMA("sy", lnb_rep[:], dr["lnb"][None, :].broadcast_to([128, 128]))
        DMA("gp", fbr[:], dr["fbias"][None, :])
        DMA("sy", ws_rep[:], dr["wscale"][None, :].broadcast_to([128, 1]))

        G("dve", lambda e: e.memset(ones1[:], 1.0))
        G("dve", lambda e: e.memset(s_st[:], 0.0))
        G("dve", lambda e: e.memset(s_stb[:], 0.0))
        G("dve", lambda e: e.memset(carry_qk[:], 0.0))
        G("dve", lambda e: e.memset(carry_v[:], 0.0))

        for p in range(NOUT):
            r0 = p * C
            DMA("sy", x_t[:], dr["x"][r0:r0 + C, :], waits=W("dve"))

            def ln_stats(e):
                e.reduce_sum(stat[:, 0:1], x_t[:], axis=AX.X)
                return e.reduce_sum(stat[:, 1:2], scr[:], axis=AX.X)
            G("sc", lambda e: e.square(scr[:], x_t[:]), waits=W("ld", "dve"))
            G("dve", ln_stats, waits=W("sc", "ld"))
            def ln_stats2(e):
                e.tensor_scalar(stat[:, 2:3], stat[:, 0:1], 1.0 / D, None, ALU.mult)
                e.tensor_scalar(stat[:, 3:4], stat[:, 1:2], 1.0 / D, None, ALU.mult)
                e.tensor_mul(stat[:, 4:5], stat[:, 2:3], stat[:, 2:3])
                e.tensor_sub(stat[:, 5:6], stat[:, 3:4], stat[:, 4:5])
                return e.tensor_scalar(stat[:, 5:6], stat[:, 5:6], EPS_LN, None, ALU.add)
            G("dve", ln_stats2)
            G("sc", lambda e: e.activation(stat[:, 6:7], stat[:, 5:6], AF.Ln), waits=W("dve"))
            G("sc", lambda e: e.activation(stat[:, 7:8], stat[:, 6:7], AF.Exp, scale=-0.5))
            G("dve", lambda e: e.tensor_scalar(stat[:, 8:9], stat[:, 2:3], -1.0, None, ALU.mult), waits=W("sc"))
            G("dve", lambda e: e.tensor_scalar(xn_tm[:], x_t[:], stat[:, 8:9], stat[:, 7:8], ALU.add, ALU.mult))

            def xntr(e):
                last = None
                for j in range(8):
                    last = e.transpose(ps_big[:, j * 128:(j + 1) * 128],
                                       xn_tm[:, j * 128:(j + 1) * 128], identf[:])
                return last
            G("pe", xntr, waits=W("dve", "sc"))
            G("dve", lambda e: e.tensor_copy(xn_fm[:], ps_big[:]), waits=W("pe"), stall=True)
            dump("x_t", lambda: x_t[:], p)
            dump("xn_tm", lambda: xn_tm[:], p)
            dump("stat", lambda: stat[:], p)
            dump("xn_fm", lambda: xn_fm[:], p)

            def proj(dst_psum, w, wstride, nlo, nwid, bias_row=None):
                def f(e):
                    last = None
                    for kc in range(8):
                        last = e.matmul(dst_psum,
                                        xn_fm[:, kc * 128:(kc + 1) * 128],
                                        w[:, kc * wstride + nlo: kc * wstride + nlo + nwid],
                                        start=(kc == 0), stop=(kc == 7 and bias_row is None))
                    if bias_row is not None:
                        last = e.matmul(dst_psum, ones1[:], bias_row,
                                        start=False, stop=True)
                    return last
                return f
            def proj1(e):
                for half in range(2):
                    proj(ps_big[:, half * 512:(half + 1) * 512], w_a, 1024, half * 512, 512,
                         ba_r[:, half * 512:(half + 1) * 512])(e)
                proj(ps_a[:, 0:512], w_q, 512, 0, 512)(e)
                return proj(ps_ub[:, 0:512], w_k, 512, 0, 512)(e)
            G("pe", proj1, waits=W("sc", "dve"))
            def evac1(e):
                e.tensor_copy(za[:], ps_big[:])
                e.tensor_copy(qraw_tm[:], ps_a[:, 0:512])
                return e.tensor_copy(kraw_tm[:], ps_ub[:, 0:512])
            G("dve", evac1, waits=W("pe"), stall=True)
            def proj2(e):
                for half in range(2):
                    proj(ps_big[:, half * 512:(half + 1) * 512], w_v, 1024, half * 512, 512)(e)
                return proj(ps_a[:, 0:8], w_b, 8, 0, 8, bb_r[:])(e)
            G("pe", proj2, waits=W("dve"))
            def evac2(e):
                e.tensor_copy(vraw_tm[:], ps_big[:])
                return e.tensor_copy(bz[:], ps_a[:, 0:8])
            G("dve", evac2, waits=W("pe"), stall=True)

            def rawtr(src, dst, carry_t, carry_off, pw):
                def trs(e):
                    last = None
                    for j in range(8):
                        last = e.transpose(
                            ps_tr[0:pw, j * 128:(j + 1) * 128],
                            src[:, j * pw:(j + 1) * pw], ident[:])
                    return last
                G("pe", trs, waits=W("sc", "dve"))
                def evac(e):
                    e.tensor_copy(
                        dst[0:pw, :].rearrange("p (s c) -> p s c", c=131)[:, :, 3:131],
                        ps_tr[0:pw, :].rearrange("p (s c) -> p s c", c=128))
                    return e.tensor_copy(
                        dst[0:pw, :].rearrange("p (s c) -> p s c", c=131)[:, :, 0:3],
                        carry_t[0:pw, carry_off * 3:(carry_off + 8) * 3]
                            .rearrange("p (s c) -> p s c", c=3))
                G("dve", evac, waits=W("pe"), stall=True)
            rawtr(qraw_tm, qr_fm, carry_qk, 0, 64)
            rawtr(kraw_tm, kr_fm, carry_qk, 8, 64)
            rawtr(vraw_tm, vr_fm, carry_v, 0, 128)

            def cscale(srcf, tapt, gbase, pw, t4, reg, cbt):
                def f(e):
                    dst3 = scl[0:pw, reg * 1024:(reg + 1) * 1024].rearrange(
                        "p (s c) -> p s c", c=128)
                    src3 = srcf[0:pw, :].rearrange("p (s c) -> p s c", c=131)[:, :, t4:t4 + 128]
                    tap = tapt[0:pw, gbase * 4 + t4: gbase * 4 + t4 + 29: 4] \
                        .unsqueeze(2).broadcast_to([pw, 8, 128])
                    last = e.tensor_tensor(dst3, src3, tap, ALU.mult)
                    if t4 == 0:
                        cb3 = cbt[0:pw, gbase:gbase + 8].unsqueeze(2).broadcast_to([pw, 8, 128])
                        last = e.tensor_tensor(dst3, dst3, cb3, ALU.add)
                    return last
                return f
            def cmm(psr, pw, t4, reg):
                def f(e):
                    for half in range(2):
                        last = e.matmul(psr[0:pw, half * 512:(half + 1) * 512],
                                        ident[0:pw, 0:pw],
                                        scl[0:pw, reg * 1024 + half * 512: reg * 1024 + (half + 1) * 512],
                                        start=(t4 == 0), stop=(t4 == 3))
                    return last
                return f
            def conv_tensor(srcf, dstf, tapt, gbase, pw, psr, pre_waits, cbt):
                pe_after = []
                for t4 in range(4):
                    wd = ([("pe", pe_after[t4 - 2])] if t4 >= 2 else list(pre_waits))
                    G("dve", cscale(srcf, tapt, gbase, pw, t4, t4 % 2, cbt), waits=wd)
                    G("pe", cmm(psr, pw, t4, t4 % 2),
                      waits=W("dve") + (list(pre_waits) if t4 == 0 else []))
                    pe_after.append(cnt["pe"])
                G("sc", lambda e: e.activation(dstf[0:pw, 0:1024], psr[0:pw, 0:1024], AF.Silu),
                  waits=W("pe"))
            conv_tensor(qr_fm, q_fm, taps_qk, 0, 64, ps_big, [], cb_qk)
            conv_tensor(kr_fm, k_fm, taps_qk, 8, 64, ps_ub, W("pe"), cb_qk)
            conv_tensor(vr_fm, v_fm, taps_v, 0, 128, ps_big, W("pe", "sc"), cb_v)
            G("dve", lambda e: e.tensor_scalar(k_fm[0:64, :], k_fm[0:64, :], KSCALE, None, ALU.mult),
              waits=W("sc"))
            G("dve", lambda e: e.tensor_copy(
                k_slab[:].rearrange("p (t h) -> p t h", h=8),
                k_fm[0:64, :].rearrange("p (h t) -> p t h", t=128)))
            def save_carry(e):
                e.tensor_copy(carry_qk[0:64, 0:24].rearrange("p (s c) -> p s c", c=3),
                              qr_fm[0:64, :].rearrange("p (s c) -> p s c", c=131)[:, :, 128:131])
                e.tensor_copy(carry_qk[0:64, 24:48].rearrange("p (s c) -> p s c", c=3),
                              kr_fm[0:64, :].rearrange("p (s c) -> p s c", c=131)[:, :, 128:131])
                return e.tensor_copy(carry_v[:, 0:24].rearrange("p (s c) -> p s c", c=3),
                              vr_fm[:].rearrange("p (s c) -> p s c", c=131)[:, :, 128:131])
            G("dve", save_carry, waits=W("sc"))
            dump("q_fm", lambda: q_fm[:], p)
            dump("v_fm", lambda: v_fm[:], p)

            for half in range(2):
                G("pe", proj(ps_big[:, half * 512:(half + 1) * 512], w_g, 1024, half * 512, 512),
                  waits=W("dve", "sc"))
            G("sc", lambda e: e.activation(o_tm[:], ps_big[:], AF.Tanh, scale=0.5),
              waits=W("pe", "dve"))
            G("dve", lambda e: e.tensor_scalar(g_tm[:], o_tm[:], 1.0, 0.5, ALU.add, ALU.mult),
              waits=W("sc"))
            G("sc", lambda e: e.activation(bz[:, 0:8], bz[:, 0:8], AF.Tanh, scale=0.5),
              waits=W("dve"))
            G("dve", lambda e: e.tensor_scalar(beta[:], bz[:, 0:8], 1.0, 0.5, ALU.add, ALU.mult),
              waits=W("sc"))

            def ktrs(e):
                last = None
                for j in range(8):
                    last = e.transpose(ps_tr[:, j * 64:(j + 1) * 64],
                                       k_fm[0:64, j * 128:(j + 1) * 128], ident[0:64, 0:64])
                return last
            G("pe", ktrs, waits=W("sc", "dve"))
            G("dve", lambda e: e.tensor_copy(k_tm[:], ps_tr[:, 0:512]), waits=W("pe"), stall=True)
            def vtrs(e):
                last = None
                for j in range(8):
                    last = e.transpose(ps_tr[:, j * 128:(j + 1) * 128],
                                       v_fm[:, j * 128:(j + 1) * 128], ident[:])
                return last
            G("pe", vtrs, waits=W("dve"))
            G("dve", lambda e: e.tensor_copy(v_tm[:], ps_tr[:]), waits=W("pe"), stall=True)

            G("sc", lambda e: e.activation(ew[:], za[:], AF.Exp, scale=-1.0),
              waits=W("dve") + W("st"))
            G("dve", lambda e: e.tensor_scalar(ew[:], ew[:], 1.0, None, ALU.add), waits=W("sc"))
            G("sc", lambda e: e.activation(lp[:], ew[:], AF.Ln), waits=W("dve"))
            G("dve", lambda e: e.tensor_tensor(
                dbw[:].rearrange("p (a b) -> p a b", b=128),
                ew[:].rearrange("p (a b) -> p a b", b=128),
                beta[:].unsqueeze(2).broadcast_to([128, 8, 128]),
                ALU.mult), waits=W("sc"))
            if p > 0:
                def sigmm(e):
                    last = None
                    for half in range(2):
                        last = e.matmul(ps_big[0:64, half * 512:(half + 1) * 512],
                                        w64b[:, 0:64], lp[:, half * 512:(half + 1) * 512],
                                        start=True, stop=True)
                    return last
                G("pe", sigmm, waits=W("sc", "dve"))
                G("sc", lambda e: e.activation(sigm[:], ps_big[0:64, :], AF.Exp), waits=W("pe"))
                G("dve", lambda e: e.tensor_mul(s_st[:], s_st[:], sigm[:]), waits=W("sc"))
                G("dve", lambda e: e.tensor_copy(s_stb[:], s_st[:]))
            def trimm(mat):
                def f(e):
                    last = None
                    for half in range(2):
                        last = e.matmul(ps_big[:, half * 512:(half + 1) * 512],
                                        mat[:], lp[:, half * 512:(half + 1) * 512],
                                        start=True, stop=True)
                    return last
                return f
            G("pe", trimm(trim_in), waits=W("dve", "sc"))
            G("sc", lambda e: e.activation(gin[:], ps_big[:], AF.Exp), waits=W("pe"))
            G("pe", trimm(trim_iex), waits=W("sc"))
            G("sc", lambda e: e.activation(igex[:], ps_big[:], AF.Exp), waits=W("pe"))
            G("dve", lambda e: e.tensor_mul(vt_w[:], v_tm[:], igex[:]), waits=W("sc"))
            dump("gin", lambda: gin[:], p)
            dump("igex", lambda: igex[:], p)


            def hk(hh):
                return k_fm[0:64, hh * 128:(hh + 1) * 128]
            def hq(hh):
                return q_fm[0:64, hh * 128:(hh + 1) * 128]
            def gramms(e):
                last = None
                for hh in range(8):
                    last = e.matmul(ps_big[:, hh * 128:(hh + 1) * 128], hk(hh), hk(hh),
                                    start=True, stop=True)
                return last
            G("pe", gramms, waits=W("dve", "sc"))
            G("dve", lambda e: e.tensor_scalar(gram[:], ps_big[:], -1.0, None, ALU.mult),
              waits=W("pe"), stall=True)
            def atilms(e):
                last = None
                for hh in range(8):
                    last = e.matmul(ps_big[:, hh * 128:(hh + 1) * 128], hk(hh), hq(hh),
                                    start=True, stop=True)
                return last
            G("pe", atilms, waits=W("dve"))
            G("dve", lambda e: e.tensor_mul(
                atil[:].rearrange("p (h b) -> p h b", b=128),
                ps_big[:].rearrange("p (h b) -> p h b", b=128),
                mask_incl[:].unsqueeze(1).broadcast_to([128, 8, 128])),
              waits=W("pe"), stall=True)
            def qs0(e):
                last = None
                for hh in range(8):
                    last = e.matmul(ps_big[:, hh * 128:(hh + 1) * 128], hq(hh),
                                    s_stb[:, hh * 128:(hh + 1) * 128],
                                    start=(hh % 4 == 0), stop=False)
                return last
            G("pe", qs0, waits=W("dve", "sc"))

            DMA("gp", dr["scr_db"], dbw[:], waits=W("dve"))
            for ch in range(2):
                DMA("gp",
                    db_sh[:].rearrange("p (c v) -> p c v", v=128)[:, ch * 4:(ch + 1) * 4, :],
                    dr["scr_db"].rearrange("(c s) (h v) -> (s h) c v", s=16, v=128)
                        [:, ch * 4:(ch + 1) * 4, :],
                    waits=W("ld"))

            G("sc", lambda e: e.square(scr[:], xn_tm[:]), waits=W("dve"))
            G("dve", lambda e: e.reduce_sum(stat[:, 16:17], scr[:], axis=AX.X), waits=W("sc"))
            def rmsst(e):
                e.tensor_scalar(stat[:, 17:18], stat[:, 16:17], 1.0 / D, None, ALU.mult)
                return e.tensor_scalar(stat[:, 17:18], stat[:, 17:18], EPS_RMS, None, ALU.add)
            G("dve", rmsst)
            G("sc", lambda e: e.activation(stat[:, 18:19], stat[:, 17:18], AF.Ln), waits=W("dve"))
            G("sc", lambda e: e.activation(stat[:, 19:20], stat[:, 18:19], AF.Exp, scale=-0.5))
            G("dve", lambda e: e.tensor_scalar(scr[:], xn_tm[:], stat[:, 19:20], None, ALU.mult),
              waits=W("sc"))
            G("dve", lambda e: e.reduce_max(stat[:, 20:21], scr[:], axis=AX.X,
                                            apply_absolute_value=True))
            G("dve", lambda e: e.tensor_scalar(stat[:, 20:21], stat[:, 20:21], EPS_ACT, None, ALU.max))
            G("sc", lambda e: e.activation(stat[:, 21:22], stat[:, 20:21], AF.Ln), waits=W("dve"))
            G("sc", lambda e: e.activation(stat[:, 22:23], stat[:, 21:22], AF.Exp, scale=-1.0))
            G("dve", lambda e: e.tensor_scalar(stat[:, 23:24], stat[:, 22:23], 127.0, None, ALU.mult),
              waits=W("sc"))
            G("dve", lambda e: e.tensor_scalar(scr[:], scr[:], stat[:, 23:24], RND, ALU.mult, ALU.add))
            G("dve", lambda e: e.tensor_scalar(scr[:], scr[:], RND, None, ALU.subtract))
            def oscale(e):
                e.tensor_mul(stat[:, 22:23], stat[:, 20:21], ws_rep[:])
                return e.tensor_scalar(stat[:, 22:23], stat[:, 22:23], 1.0 / 127.0, None, ALU.mult)
            G("dve", oscale)
            G("dve", lambda e: e.tensor_copy(dbw[:], scr[:]), waits=W("ld"))
            def ntrs(e):
                last = None
                for j in range(8):
                    last = e.transpose(ps_tr[:, j * 128:(j + 1) * 128],
                                       dbw[:, j * 128:(j + 1) * 128], ident[:])
                return last
            G("pe", ntrs, waits=W("dve", "sc"))
            G("dve", lambda e: e.tensor_copy(n_fm[:], ps_tr[:]), waits=W("pe"), stall=True)

            for i16 in range(8):
                t0 = i16 * 16
                kS = k_slab[:, t0 * 8:(t0 + 16) * 8]
                G("pe", lambda e, kS=kS: e.matmul(ps_int[:, 0:128], kS, kS,
                                                  start=True, stop=True),
                  waits=W("dve", "sc"))
                G("dve", lambda e: e.tensor_mul(
                    c_steps[:].rearrange("p (r f) -> p r f", f=128),
                    bmask[:].rearrange("p (r f) -> p r f", f=128),
                    ps_int[:, 0:128].unsqueeze(1).broadcast_to([128, 16, 128])),
                  waits=W("pe"), stall=True)
                for half in range(2):
                    hs = range(half * 4, half * 4 + 4)
                    def ksmms(e, hs=hs, i16=i16, t0=t0, half=half):
                        last = None
                        first = True
                        for hh in hs:
                            c0 = (hh % 4) * 128
                            last = e.matmul(ps_ks[0:16, c0:c0 + 128],
                                            hk(hh)[:, t0:t0 + 16],
                                            s_stb[:, hh * 128:(hh + 1) * 128],
                                            start=first, stop=False)
                            first = False
                            if i16 > 0:
                                last = e.matmul(ps_ks[0:16, c0:c0 + 128],
                                                gram[0:t0, hh * 128 + t0: hh * 128 + t0 + 16],
                                                u_big[0:t0, hh * 128:(hh + 1) * 128],
                                                start=False, stop=False)
                        last = e.matmul(ps_ks[0:16, 0:512],
                                        negident[:, t0:t0 + 16],
                                        vt_w[:, half * 512:(half + 1) * 512],
                                        start=False, stop=True)
                        return last
                    G("pe", ksmms, waits=W("dve") + (W("ld") if half == 0 else []))
                    G("dve", lambda e, half=half: e.tensor_copy(
                        stag[:, half * 512:(half + 1) * 512], ps_ks[0:16, 0:512]),
                      waits=W("pe"))
                def permin(e):
                    last = None
                    for hh in range(8):
                        last = e.matmul(ps_int[:, 128:256], pstack[:, hh * 128:(hh + 1) * 128],
                                        stag[:, hh * 128:(hh + 1) * 128],
                                        start=(hh == 0), stop=False)
                    return last
                G("pe", permin, waits=W("dve"))
                def chainTT(i16=i16):
                    def f(e):
                        return e.tensor_mul(
                            u_slab[:], db_sh[:, i16 * 128:(i16 + 1) * 128],
                            ps_int[:, 128:256])
                    return f
                G("dve", chainTT(), waits=W("pe", "ld"), stall=True)
                for r16 in range(15):
                    G("pe", lambda e, r16=r16: e.matmul(
                        ps_int[:, 128:256],
                        c_steps[:, r16 * 128:(r16 + 1) * 128], u_slab[:],
                        start=False, stop=(r16 == 14)), waits=W("dve"))
                    G("dve", chainTT(), waits=W("pe"), stall=True)
                DMA("gp", dr["scr_ub"][:, t0 * 8:t0 * 8 + 128], u_slab[:],
                    waits=W("dve"))
                DMA("sy", u_big[t0:t0 + 16, :].rearrange("s (h v) -> s h v", v=128),
                    dr["scr_ub"][:, t0 * 8:t0 * 8 + 128]
                        .rearrange("(s h) v -> s h v", h=8),
                    waits=W("ld"))

            dump("u_big", lambda: u_big[:], p)
            def ocouple(e):
                last = None
                for hh in range(8):
                    last = e.matmul(ps_big[:, hh * 128:(hh + 1) * 128],
                                    atil[:, hh * 128:(hh + 1) * 128],
                                    u_big[:, hh * 128:(hh + 1) * 128],
                                    start=False, stop=True)
                return last
            G("pe", ocouple, waits=W("dve", "sc", "ld"))
            G("dve", lambda e: e.tensor_mul(o_tm[:], gin[:], ps_big[:]), waits=W("pe"))
            dump("o_tm", lambda: o_tm[:], p)

            def repl127(e):
                last = None
                for half in range(2):
                    last = e.matmul(ps_big[0:64, half * 512:(half + 1) * 512],
                                    oh127[:, 0:64], gin[:, half * 512:(half + 1) * 512],
                                    start=True, stop=True)
                return last
            G("pe", repl127, waits=W("dve"))
            G("dve", lambda e: e.tensor_copy(sigm[:], ps_big[0:64, :]), waits=W("pe"))
            for hp in range(2):
                def wmms(e, hp=hp):
                    last = None
                    for hh in range(4 * hp, 4 * hp + 4):
                        last = e.matmul(ps_a[0:64, (hh % 4) * 128:((hh % 4) + 1) * 128],
                                        k_tm[:, hh * 64:(hh + 1) * 64],
                                        u_big[:, hh * 128:(hh + 1) * 128],
                                        start=True, stop=True)
                    return last
                G("pe", wmms, waits=W("dve", "sc"))
                sl = slice(hp * 512, (hp + 1) * 512)
                G("dve", lambda e, sl=sl: e.tensor_sub(s_st[:, sl], s_st[:, sl],
                                                       ps_a[0:64, 0:512]), waits=W("pe"))
            G("dve", lambda e: e.tensor_mul(s_st[:], s_st[:], sigm[:]))
            G("dve", lambda e: e.tensor_copy(s_stb[:], s_st[:]))
            dump("s_st", lambda: s_st[:], p)

            G("sc", lambda e: e.square(scr[:], o_tm[:]), waits=W("dve"))
            def gstats(e):
                e.reduce_sum(stat[:, 0:8], o_tm[:].rearrange("p (a b) -> p a b", b=128), axis=AX.X)
                return e.reduce_sum(stat[:, 8:16], scr[:].rearrange("p (a b) -> p a b", b=128), axis=AX.X)
            G("dve", gstats, waits=W("sc"))
            def gstats2(e):
                e.tensor_scalar(stat[:, 0:8], stat[:, 0:8], 1.0 / DV, None, ALU.mult)
                e.tensor_scalar(stat[:, 8:16], stat[:, 8:16], 1.0 / DV, None, ALU.mult)
                e.tensor_mul(scr[:, 0:8], stat[:, 0:8], stat[:, 0:8])
                e.tensor_sub(stat[:, 8:16], stat[:, 8:16], scr[:, 0:8])
                return e.tensor_scalar(stat[:, 8:16], stat[:, 8:16], EPS_GN, None, ALU.add)
            G("dve", gstats2)
            G("sc", lambda e: e.activation(stat[:, 8:16], stat[:, 8:16], AF.Ln), waits=W("dve"))
            G("sc", lambda e: e.activation(stat[:, 8:16], stat[:, 8:16], AF.Exp, scale=-0.5))
            def onorm(e):
                o3 = o_tm[:].rearrange("p (a b) -> p a b", b=128)
                mu = stat[:, 0:8].unsqueeze(2).broadcast_to([128, 8, 128])
                rs = stat[:, 8:16].unsqueeze(2).broadcast_to([128, 8, 128])
                e.tensor_sub(o3, o3, mu)
                e.tensor_mul(o3, o3, rs)
                e.tensor_mul(o_tm[:].rearrange("p (a b) -> p a b", b=128),
                             o_tm[:].rearrange("p (a b) -> p a b", b=128),
                             lnw_rep[:].unsqueeze(1).broadcast_to([128, 8, 128]))
                e.tensor_add(o_tm[:].rearrange("p (a b) -> p a b", b=128),
                             o_tm[:].rearrange("p (a b) -> p a b", b=128),
                             lnb_rep[:].unsqueeze(1).broadcast_to([128, 8, 128]))
                return e.tensor_mul(o_tm[:], o_tm[:], g_tm[:])
            G("dve", onorm, waits=W("sc"))
            def ogtrs(e):
                last = None
                for j in range(8):
                    last = e.transpose(ps_big[:, j * 128:(j + 1) * 128],
                                       o_tm[:, j * 128:(j + 1) * 128], identf[:])
                return last
            G("pe", ogtrs, waits=W("dve", "sc"))
            G("dve", lambda e: e.tensor_copy(og_fm[:], ps_big[:]), waits=W("pe"), stall=True)
            def wfmms(e):
                last = None
                for half in range(2):
                    dst = ps_big[:, half * 512:(half + 1) * 512]
                    for kc in range(8):
                        last = e.matmul(dst, og_fm[:, kc * 128:(kc + 1) * 128],
                                        w_o[:, kc * 1024 + half * 512: kc * 1024 + half * 512 + 512],
                                        start=(kc == 0), stop=False)
                    last = e.matmul(dst, ones1[:], fbr[:, half * 512:(half + 1) * 512],
                                    start=False, stop=True)
                    dstf = ps_ub[:, half * 512:(half + 1) * 512]
                    for kc in range(8):
                        last = e.matmul(dstf, n_fm[:, kc * 128:(kc + 1) * 128],
                                        w_f[:, kc * 1024 + half * 512: kc * 1024 + half * 512 + 512],
                                        start=(kc == 0), stop=(kc == 7))
                return last
            G("pe", wfmms, waits=W("sc", "dve"))
            def resfin(e):
                e.tensor_add(res1[:], x_t[:], ps_big[:])
                return e.scalar_tensor_tensor(res1[:], ps_ub[:], stat[:, 22:23],
                                              res1[:], ALU.mult, ALU.add)
            G("dve", resfin, waits=W("pe"), stall=True)
            DMA("sy", dr["out"][r0:r0 + C, :], res1[:], waits=W("dve"), sem="st")
            dump("out_rows", lambda: res1[:], p)

        G("sy", lambda e: e.wait_ge(sems["st"], cnt["st"]) or None, waits=(), sem=None)

    for t in reversed(ctx):
        t.__exit__(None, None, None)
    return nc





_BUILD_CACHE = {}

def kernel(**inputs):
    import numpy as np
    from concourse.bass_utils import run_bass_kernel_spmd
    T = 8192
    x_full = np.asarray(inputs["L0_attn_x"])
    B = x_full.shape[0]
    g = host_prep(inputs)
    if T not in _BUILD_CACHE:
        _BUILD_CACHE[T] = build(T)
    nc = _BUILD_CACHE[T]
    in_maps = []
    for c in range(8):
        im = dict(g)
        im["x"] = np.ascontiguousarray(x_full[c % B])
        in_maps.append(im)
    res = run_bass_kernel_spmd(nc, in_maps, core_ids=list(range(8)))
    out = np.stack([np.asarray(res.results[b]["out"]) for b in range(B)], axis=0)
    return out.astype(np.float32)


# version 10
# speedup vs baseline: 1.7750x; 1.0016x over previous
import numpy as np
import ml_dtypes
import concourse.bass as bass
import concourse.mybir as mybir

F32 = mybir.dt.float32
BF16 = mybir.dt.bfloat16
AF = mybir.ActivationFunctionType
ALU = mybir.AluOpType
AX = mybir.AxisListType

H, DK, DV, D = 8, 64, 128, 1024
C = 128
EPS_LN = 1e-5
EPS_GN = 1e-5
EPS_RMS = 1e-6
EPS_ACT = 1e-5
KSCALE = DK ** (-0.5)
RND = 12582912.0


def host_constants():
    r = np.arange(128)
    tri_incl = (r[:, None] <= r[None, :]).astype(np.float32)
    tri_excl = (r[:, None] < r[None, :]).astype(np.float32)
    mlow = ((r[:, None] < 64).astype(np.float32)) * np.ones((1, 128), np.float32)
    c = {}
    c["ident"] = np.eye(128, dtype=np.float32)
    c["trim_in"] = (mlow - tri_incl)
    c["trim_iex"] = (tri_excl - mlow)
    c["w64b"] = (-(r < 64).astype(np.float32))[:, None] * np.ones((1, 64), np.float32)
    c["oh127"] = (r == 127).astype(np.float32)[:, None] * np.ones((1, 64), np.float32)
    pstack = np.zeros((16, 8 * 128), np.float32)
    for h in range(8):
        for s in range(16):
            pstack[s, h * 128 + s * 8 + h] = 1.0
    c["pstack"] = pstack
    bm = np.zeros((128, 16 * 128), np.float32)
    for h in range(8):
        for rr in range(16):
            for s in range(rr + 1, 16):
                bm[rr * 8 + h, rr * 128 + s * 8 + h] = -1.0
    c["bmask"] = bm
    c["negident"] = -np.eye(128, dtype=np.float32)
    c["mask_incl"] = -tri_incl
    return c


def host_prep(inputs):
    g = {}
    tr = lambda a: np.ascontiguousarray(np.asarray(a).T).astype(np.float32)
    g["wqt"] = tr(inputs["L0_attn_Wq"])
    g["wkt"] = tr(inputs["L0_attn_Wk"])
    g["wvt"] = tr(inputs["L0_attn_Wv"])
    g["wat"] = tr(inputs["L0_attn_Wa"])
    g["wgt"] = tr(inputs["L0_attn_Wg"])
    g["wbt"] = tr(inputs["L0_attn_Wb"])
    g["wot"] = tr(inputs["L0_attn_Wo"])
    g["wft"] = tr(inputs["L0_ffn_w_int8"])
    g["ba"] = np.asarray(inputs["L0_attn_ba"]).reshape(1, D).astype(np.float32)
    g["bb"] = np.asarray(inputs["L0_attn_bb"]).reshape(1, 8).astype(np.float32)
    tqk = np.concatenate([
        np.asarray(inputs["L0_attn_qc_w"]).reshape(512, 4),
        np.asarray(inputs["L0_attn_kc_w"]).reshape(512, 4)], axis=0).astype(np.float32)
    g["taps_qk"] = np.ascontiguousarray(tqk.reshape(16, 64, 4).transpose(1, 0, 2).reshape(64, 64))
    tv = np.asarray(inputs["L0_attn_vc_w"]).reshape(1024, 4).astype(np.float32)
    g["taps_v"] = np.ascontiguousarray(tv.reshape(8, 128, 4).transpose(1, 0, 2).reshape(128, 32))
    cqk = np.concatenate([
        np.asarray(inputs["L0_attn_qc_b"]).reshape(512),
        np.asarray(inputs["L0_attn_kc_b"]).reshape(512)]).astype(np.float32)
    g["cb_qk"] = np.ascontiguousarray(cqk.reshape(16, 64).T)
    g["cb_v"] = np.ascontiguousarray(
        np.asarray(inputs["L0_attn_vc_b"]).reshape(8, 128).T.astype(np.float32))
    g["lnw"] = np.asarray(inputs["L0_attn_ln_w"]).reshape(128).astype(np.float32)
    g["lnb"] = np.asarray(inputs["L0_attn_ln_b"]).reshape(128).astype(np.float32)
    g["fbias"] = np.asarray(inputs["L0_ffn_bias"]).reshape(D).astype(np.float32)
    g["wscale"] = np.asarray(inputs["L0_ffn_weight_scale"]).reshape(1).astype(np.float32)
    for k, v in host_constants().items():
        g[k] = v.astype(np.float32)
    return g


def build(T, dumps=()):
    assert T % C == 0
    NOUT = T // C
    nc = bass.Bass("TRN2", target_bir_lowering=False, debug=False)
    dr = {}
    dr["x"] = nc.dram_tensor("x", [T, D], F32, kind="ExternalInput").ap()
    for nm, shp in [("wqt", [D, 512]), ("wkt", [D, 512]), ("wvt", [D, D]),
                    ("wat", [D, D]), ("wgt", [D, D]), ("wbt", [D, 8]),
                    ("wot", [D, D]), ("wft", [D, D]), ("ba", [1, D]),
                    ("bb", [1, 8]), ("taps_qk", [64, 64]), ("taps_v", [128, 32]),
                    ("cb_qk", [64, 16]), ("cb_v", [128, 8]),
                    ("lnw", [128]), ("lnb", [128]), ("fbias", [D]), ("wscale", [1]),
                    ("ident", [128, 128]), ("trim_in", [128, 128]),
                    ("trim_iex", [128, 128]), ("w64b", [128, 64]),
                    ("oh127", [128, 64]), ("pstack", [16, 1024]),
                    ("bmask", [128, 2048]),
                    ("negident", [128, 128]),
                    ("mask_incl", [128, 128])]:
        dr[nm] = nc.dram_tensor(nm, shp, F32, kind="ExternalInput").ap()
    dr["out"] = nc.dram_tensor("out", [T, D], F32, kind="ExternalOutput").ap()
    dr["scr_db"] = nc.dram_tensor("scr_db", [128, 1024], BF16, kind="Internal").ap()
    dr["scr_ub"] = nc.dram_tensor("scr_ub", [128, 1024], BF16, kind="Internal").ap()
    dump_dr = {}
    for tag, _p, shp in dumps:
        dump_dr[tag] = nc.dram_tensor("dbg_" + tag, list(shp), F32,
                                      kind="ExternalOutput").ap()

    ctx = []
    def sb(name, shape, dt):
        t = nc.sbuf_tensor(name, list(shape), dt); ctx.append(t); return t.__enter__()
    def ps(name, shape, dt):
        t = nc.psum_tensor(name, list(shape), dt); ctx.append(t); return t.__enter__()

    w_q = sb("w_q", [128, 8 * 512], BF16)
    w_k = sb("w_k", [128, 8 * 512], BF16)
    w_v = sb("w_v", [128, 8 * 1024], BF16)
    w_a = sb("w_a", [128, 8 * 1024], BF16)
    w_g = sb("w_g", [128, 8 * 1024], BF16)
    w_b = sb("w_b", [128, 8 * 8], BF16)
    w_o = sb("w_o", [128, 8 * 1024], BF16)
    w_f = sb("w_f", [128, 8 * 1024], BF16)
    ba_r = sb("ba_r", [1, D], BF16)
    bb_r = sb("bb_r", [1, 8], BF16)
    ones1 = sb("ones1", [1, 128], BF16)
    taps_qk = sb("taps_qkt", [64, 64], F32)
    taps_v = sb("taps_vt", [128, 32], F32)
    cb_qk = sb("cb_qkt", [64, 16], F32)
    cb_v = sb("cb_vt", [128, 8], F32)
    ident = sb("identt", [128, 128], BF16)
    identf = sb("identft", [128, 128], F32)
    trim_in = sb("trim_int", [128, 128], F32)
    trim_iex = sb("trim_iext", [128, 128], F32)
    w64b = sb("w64bt", [128, 64], F32)
    oh127 = sb("oh127t", [128, 64], F32)
    pstack = sb("pstackt", [16, 1024], BF16)
    bmask = sb("bmaskt", [128, 2048], BF16)
    negident = sb("negidentt", [128, 128], BF16)
    mask_incl = sb("mask_inclt", [128, 128], F32)
    lnw_rep = sb("lnw_rep", [128, 128], F32)
    lnb_rep = sb("lnb_rep", [128, 128], F32)
    fbr = sb("fbr", [1, 1024], BF16)
    ws_rep = sb("ws_rep", [128, 1], F32)
    scl = sb("scl", [128, 2 * 1024], BF16)

    x_t = sb("x_tt", [128, D], F32)
    scr = sb("scr", [128, D], F32)
    stat = sb("stat", [128, 24], F32)
    xn_tm = sb("xn_tm", [128, D], F32)
    xn_fm = sb("xn_fm", [128, 8 * 128], BF16)
    qraw_tm = sb("qraw_tm", [128, 512], BF16)
    kraw_tm = sb("kraw_tm", [128, 512], BF16)
    vraw_tm = sb("vraw_tm", [128, 1024], BF16)
    bz = sb("bz", [128, 8], F32)
    qr_fm = sb("qr_fm", [64, 8 * 131], BF16)
    kr_fm = sb("kr_fm", [64, 8 * 131], BF16)
    vr_fm = sb("vr_fm", [128, 8 * 131], BF16)
    q_fm = sb("q_fm", [64, 8 * 128], BF16)
    k_fm = sb("k_fm", [64, 8 * 128], BF16)
    v_fm = sb("v_fm", [128, 8 * 128], BF16)
    k_tm = sb("k_tm", [128, 512], BF16)
    v_tm = sb("v_tm", [128, 1024], BF16)
    g_tm = sb("g_tm", [128, 1024], BF16)
    beta = sb("beta", [128, 8], F32)
    gin = sb("gin", [128, 1024], F32)
    igex = sb("igex", [128, 1024], F32)
    vt_w = sb("vt_w", [128, 1024], BF16)
    dbw = sb("dbw", [128, 1024], BF16)
    sigm = sb("sigm", [64, 1024], F32)
    gram = sb("gram", [128, 8 * 128], BF16)
    atil = sb("atil", [128, 8 * 128], BF16)
    s_st = sb("s_st", [64, 8 * 128], F32)
    s_stb = sb("s_stb", [64, 8 * 128], BF16)
    stag = sb("stag", [16, 8 * 128], BF16)
    c_steps = sb("c_steps", [128, 16 * 128], BF16)
    k_slab = sb("k_slab", [64, 1024], BF16)
    u_slab = sb("u_slab", [128, 128], BF16)
    db_sh = sb("db_sh", [128, 8 * 128], BF16)
    u_big = sb("u_big", [128, 8 * 128], BF16)
    o_tm = sb("o_tm", [128, 1024], F32)
    res1 = sb("res1", [128, 1024], F32)
    za = scr
    lp = o_tm
    ew = res1
    og_fm = xn_fm
    n_fm = v_fm
    carry_qk = sb("carry_qk", [64, 16 * 3], BF16)
    carry_v = sb("carry_v", [128, 8 * 3], BF16)

    ps_big = ps("ps_big", [128, 1024], F32)
    ps_a = ps("ps_a", [128, 512], F32)
    ps_ks = ps("ps_ks", [16, 512], F32)
    ps_tr = ps("ps_tr", [128, 1024], BF16)
    ps_int = ps("ps_int", [128, 512], F32)
    ps_ub = ps("ps_ub", [128, 1024], F32)

    sems = {}
    cnt = {}
    for n in ["pe", "dve", "sc", "gp", "ld", "st"]:
        t = nc.semaphore("s_" + n); ctx.append(t)
        sems[n] = t.__enter__(); cnt[n] = 0

    dump_list = list(dumps)

    with nc.Block() as block:
        SEC = {"pe": block.tensor, "dve": block.vector, "sc": block.scalar,
               "gp": block.gpsimd, "sy": block.sync}

        dirty = {"dve": set(), "sc": set(), "gp": set()}

        class HazardGuard:
            def __init__(self, e, eng):
                self._e = e
                self._d = dirty[eng]
            def __getattr__(self, name):
                f = getattr(self._e, name)
                if name in ("wait_ge", "nop", "drain") or not callable(f):
                    return f
                if name == "dma_start":
                    return f
                def g(*a, **k):
                    wr, rd = set(), set()
                    for i, x in enumerate(a):
                        if isinstance(x, bass.AP):
                            (wr if i == 0 else rd).add(x.tensor.name)
                    for kw, x in k.items():
                        if isinstance(x, bass.AP):
                            (wr if kw in ("out", "outs") else rd).add(x.tensor.name)
                    if rd & self._d:
                        self._e.drain()
                        self._d.clear()
                    res = f(*a, **k)
                    self._d |= wr
                    return res
                return g

        def G(eng, fn, waits=(), sem=None, inc=1, stall=False):
            semn = sem or (eng if eng != "sy" else None)
            wl = [(n, v) for (n, v) in waits if v > 0]
            guard = eng in ("dve", "sc", "gp")
            if stall and guard:
                dirty[eng].clear()
            def body(e):
                for (n, v) in wl:
                    e.wait_ge(sems[n], v)
                last = fn(HazardGuard(e, eng) if guard else e)
                if semn is not None and inc and last is not None:
                    last.then_inc(sems[semn], inc)
            SEC[eng](body)
            if semn is not None and inc:
                cnt[semn] += inc

        def W(*names):
            return [(n, cnt[n]) for n in names]

        def WALL():
            return [(n, cnt[n]) for n in ["pe", "dve", "sc", "gp", "ld"]]

        def DMA(eng, out, in_, waits=(), sem="ld"):
            G(eng, lambda e: e.dma_start(out=out, in_=in_), waits, sem=sem, inc=16)

        def dump(tag, ap_fn, cur_p=None):
            for (t, pp, shp) in dump_list:
                if t == tag and (cur_p is None or pp == cur_p):
                    DMA("gp", dump_dr[tag], ap_fn(), waits=WALL(), sem="st")
                    stc = [("st", cnt["st"])]
                    for eng in ["dve", "sc", "pe", "gp"]:
                        G(eng, lambda e: e.nop(), waits=stc, inc=0)

        for nm, dst, cast in [("wqt", w_q, 1), ("wkt", w_k, 1), ("wvt", w_v, 1),
                              ("wat", w_a, 1), ("wgt", w_g, 1), ("wot", w_o, 1),
                              ("wft", w_f, 1)]:
            ncols = dr[nm].shape[1]
            DMA("gp", dst[:].rearrange("p (c n) -> p c n", n=ncols),
                dr[nm].rearrange("(c p) n -> p c n", p=128))
        DMA("gp", w_b[:].rearrange("p (c n) -> p c n", n=8),
            dr["wbt"].rearrange("(c p) n -> p c n", p=128))
        DMA("gp", ba_r[:], dr["ba"])
        DMA("gp", bb_r[:], dr["bb"])
        DMA("gp", ident[:], dr["ident"])
        DMA("sy", identf[:], dr["ident"])
        DMA("gp", pstack[:], dr["pstack"])
        DMA("gp", bmask[:], dr["bmask"])
        DMA("gp", negident[:], dr["negident"])
        DMA("sy", taps_qk[:], dr["taps_qk"])
        DMA("sy", taps_v[:], dr["taps_v"])
        DMA("sy", cb_qk[:], dr["cb_qk"])
        DMA("sy", cb_v[:], dr["cb_v"])
        DMA("sy", trim_in[:], dr["trim_in"])
        DMA("sy", trim_iex[:], dr["trim_iex"])
        DMA("sy", w64b[:], dr["w64b"])
        DMA("sy", oh127[:], dr["oh127"])
        DMA("sy", mask_incl[:], dr["mask_incl"])
        DMA("sy", lnw_rep[:], dr["lnw"][None, :].broadcast_to([128, 128]))
        DMA("sy", lnb_rep[:], dr["lnb"][None, :].broadcast_to([128, 128]))
        DMA("gp", fbr[:], dr["fbias"][None, :])
        DMA("sy", ws_rep[:], dr["wscale"][None, :].broadcast_to([128, 1]))

        G("dve", lambda e: e.memset(ones1[:], 1.0))
        G("dve", lambda e: e.memset(u_slab[:], 0.0))
        G("dve", lambda e: e.memset(s_st[:], 0.0))
        G("dve", lambda e: e.memset(s_stb[:], 0.0))
        G("dve", lambda e: e.memset(carry_qk[:], 0.0))
        G("dve", lambda e: e.memset(carry_v[:], 0.0))

        for p in range(NOUT):
            r0 = p * C
            DMA("sy", x_t[:], dr["x"][r0:r0 + C, :], waits=W("dve"))

            def ln_stats(e):
                e.reduce_sum(stat[:, 0:1], x_t[:], axis=AX.X)
                return e.reduce_sum(stat[:, 1:2], scr[:], axis=AX.X)
            G("sc", lambda e: e.square(scr[:], x_t[:]), waits=W("ld", "dve"))
            G("dve", ln_stats, waits=W("sc", "ld"))
            def ln_stats2(e):
                e.tensor_scalar(stat[:, 2:3], stat[:, 0:1], 1.0 / D, None, ALU.mult)
                e.tensor_scalar(stat[:, 3:4], stat[:, 1:2], 1.0 / D, None, ALU.mult)
                e.tensor_mul(stat[:, 4:5], stat[:, 2:3], stat[:, 2:3])
                e.tensor_sub(stat[:, 5:6], stat[:, 3:4], stat[:, 4:5])
                return e.tensor_scalar(stat[:, 5:6], stat[:, 5:6], EPS_LN, None, ALU.add)
            G("dve", ln_stats2)
            G("sc", lambda e: e.activation(stat[:, 6:7], stat[:, 5:6], AF.Ln), waits=W("dve"))
            G("sc", lambda e: e.activation(stat[:, 7:8], stat[:, 6:7], AF.Exp, scale=-0.5))
            G("dve", lambda e: e.tensor_scalar(stat[:, 8:9], stat[:, 2:3], -1.0, None, ALU.mult), waits=W("sc"))
            G("dve", lambda e: e.tensor_scalar(xn_tm[:], x_t[:], stat[:, 8:9], stat[:, 7:8], ALU.add, ALU.mult))

            def xntr(e):
                last = None
                for j in range(8):
                    last = e.transpose(ps_big[:, j * 128:(j + 1) * 128],
                                       xn_tm[:, j * 128:(j + 1) * 128], identf[:])
                return last
            G("pe", xntr, waits=W("dve", "sc"))
            G("dve", lambda e: e.tensor_copy(xn_fm[:], ps_big[:]), waits=W("pe"), stall=True)
            dump("x_t", lambda: x_t[:], p)
            dump("xn_tm", lambda: xn_tm[:], p)
            dump("stat", lambda: stat[:], p)
            dump("xn_fm", lambda: xn_fm[:], p)

            def proj(dst_psum, w, wstride, nlo, nwid, bias_row=None):
                def f(e):
                    last = None
                    for kc in range(8):
                        last = e.matmul(dst_psum,
                                        xn_fm[:, kc * 128:(kc + 1) * 128],
                                        w[:, kc * wstride + nlo: kc * wstride + nlo + nwid],
                                        start=(kc == 0), stop=(kc == 7 and bias_row is None))
                    if bias_row is not None:
                        last = e.matmul(dst_psum, ones1[:], bias_row,
                                        start=False, stop=True)
                    return last
                return f
            def proj1(e):
                for half in range(2):
                    proj(ps_big[:, half * 512:(half + 1) * 512], w_a, 1024, half * 512, 512,
                         ba_r[:, half * 512:(half + 1) * 512])(e)
                proj(ps_a[:, 0:512], w_q, 512, 0, 512)(e)
                return proj(ps_ub[:, 0:512], w_k, 512, 0, 512)(e)
            G("pe", proj1, waits=W("sc", "dve"))
            def evac1(e):
                e.tensor_copy(za[:], ps_big[:])
                e.tensor_copy(qraw_tm[:], ps_a[:, 0:512])
                return e.tensor_copy(kraw_tm[:], ps_ub[:, 0:512])
            G("dve", evac1, waits=W("pe"), stall=True)
            def proj2(e):
                for half in range(2):
                    proj(ps_big[:, half * 512:(half + 1) * 512], w_v, 1024, half * 512, 512)(e)
                return proj(ps_a[:, 0:8], w_b, 8, 0, 8, bb_r[:])(e)
            G("pe", proj2, waits=W("dve"))
            def evac2(e):
                e.tensor_copy(vraw_tm[:], ps_big[:])
                return e.tensor_copy(bz[:], ps_a[:, 0:8])
            G("dve", evac2, waits=W("pe"), stall=True)

            def rawtr(src, dst, carry_t, carry_off, pw):
                def trs(e):
                    last = None
                    for j in range(8):
                        last = e.transpose(
                            ps_tr[0:pw, j * 128:(j + 1) * 128],
                            src[:, j * pw:(j + 1) * pw], ident[:])
                    return last
                G("pe", trs, waits=W("sc", "dve"))
                def evac(e):
                    e.tensor_copy(
                        dst[0:pw, :].rearrange("p (s c) -> p s c", c=131)[:, :, 3:131],
                        ps_tr[0:pw, :].rearrange("p (s c) -> p s c", c=128))
                    return e.tensor_copy(
                        dst[0:pw, :].rearrange("p (s c) -> p s c", c=131)[:, :, 0:3],
                        carry_t[0:pw, carry_off * 3:(carry_off + 8) * 3]
                            .rearrange("p (s c) -> p s c", c=3))
                G("dve", evac, waits=W("pe"), stall=True)
            rawtr(qraw_tm, qr_fm, carry_qk, 0, 64)
            rawtr(kraw_tm, kr_fm, carry_qk, 8, 64)
            rawtr(vraw_tm, vr_fm, carry_v, 0, 128)

            def cscale(srcf, tapt, gbase, pw, t4, reg, cbt):
                def f(e):
                    dst3 = scl[0:pw, reg * 1024:(reg + 1) * 1024].rearrange(
                        "p (s c) -> p s c", c=128)
                    src3 = srcf[0:pw, :].rearrange("p (s c) -> p s c", c=131)[:, :, t4:t4 + 128]
                    tap = tapt[0:pw, gbase * 4 + t4: gbase * 4 + t4 + 29: 4] \
                        .unsqueeze(2).broadcast_to([pw, 8, 128])
                    last = e.tensor_tensor(dst3, src3, tap, ALU.mult)
                    if t4 == 0:
                        cb3 = cbt[0:pw, gbase:gbase + 8].unsqueeze(2).broadcast_to([pw, 8, 128])
                        last = e.tensor_tensor(dst3, dst3, cb3, ALU.add)
                    return last
                return f
            def cmm(psr, pw, t4, reg):
                def f(e):
                    for half in range(2):
                        last = e.matmul(psr[0:pw, half * 512:(half + 1) * 512],
                                        ident[0:pw, 0:pw],
                                        scl[0:pw, reg * 1024 + half * 512: reg * 1024 + (half + 1) * 512],
                                        start=(t4 == 0), stop=(t4 == 3))
                    return last
                return f
            def conv_tensor(srcf, dstf, tapt, gbase, pw, psr, pre_waits, cbt):
                pe_after = []
                for t4 in range(4):
                    wd = ([("pe", pe_after[t4 - 2])] if t4 >= 2 else list(pre_waits))
                    G("dve", cscale(srcf, tapt, gbase, pw, t4, t4 % 2, cbt), waits=wd)
                    G("pe", cmm(psr, pw, t4, t4 % 2),
                      waits=W("dve") + (list(pre_waits) if t4 == 0 else []))
                    pe_after.append(cnt["pe"])
                G("sc", lambda e: e.activation(dstf[0:pw, 0:1024], psr[0:pw, 0:1024], AF.Silu),
                  waits=W("pe"))
            conv_tensor(qr_fm, q_fm, taps_qk, 0, 64, ps_big, [], cb_qk)
            conv_tensor(kr_fm, k_fm, taps_qk, 8, 64, ps_ub, W("pe"), cb_qk)
            conv_tensor(vr_fm, v_fm, taps_v, 0, 128, ps_big, W("pe", "sc"), cb_v)
            G("dve", lambda e: e.tensor_scalar(k_fm[0:64, :], k_fm[0:64, :], KSCALE, None, ALU.mult),
              waits=W("sc"))
            G("dve", lambda e: e.tensor_copy(
                k_slab[:].rearrange("p (t h) -> p t h", h=8),
                k_fm[0:64, :].rearrange("p (h t) -> p t h", t=128)))
            def save_carry(e):
                e.tensor_copy(carry_qk[0:64, 0:24].rearrange("p (s c) -> p s c", c=3),
                              qr_fm[0:64, :].rearrange("p (s c) -> p s c", c=131)[:, :, 128:131])
                e.tensor_copy(carry_qk[0:64, 24:48].rearrange("p (s c) -> p s c", c=3),
                              kr_fm[0:64, :].rearrange("p (s c) -> p s c", c=131)[:, :, 128:131])
                return e.tensor_copy(carry_v[:, 0:24].rearrange("p (s c) -> p s c", c=3),
                              vr_fm[:].rearrange("p (s c) -> p s c", c=131)[:, :, 128:131])
            G("dve", save_carry, waits=W("sc"))
            dump("q_fm", lambda: q_fm[:], p)
            dump("v_fm", lambda: v_fm[:], p)

            for half in range(2):
                G("pe", proj(ps_big[:, half * 512:(half + 1) * 512], w_g, 1024, half * 512, 512),
                  waits=W("dve", "sc"))
            G("sc", lambda e: e.activation(o_tm[:], ps_big[:], AF.Tanh, scale=0.5),
              waits=W("pe", "dve"))
            G("dve", lambda e: e.tensor_scalar(g_tm[:], o_tm[:], 1.0, 0.5, ALU.add, ALU.mult),
              waits=W("sc"))
            G("sc", lambda e: e.activation(bz[:, 0:8], bz[:, 0:8], AF.Tanh, scale=0.5),
              waits=W("dve"))
            G("dve", lambda e: e.tensor_scalar(beta[:], bz[:, 0:8], 1.0, 0.5, ALU.add, ALU.mult),
              waits=W("sc"))

            def ktrs(e):
                last = None
                for j in range(8):
                    last = e.transpose(ps_tr[:, j * 64:(j + 1) * 64],
                                       k_fm[0:64, j * 128:(j + 1) * 128], ident[0:64, 0:64])
                return last
            G("pe", ktrs, waits=W("sc", "dve"))
            G("dve", lambda e: e.tensor_copy(k_tm[:], ps_tr[:, 0:512]), waits=W("pe"), stall=True)
            def vtrs(e):
                last = None
                for j in range(8):
                    last = e.transpose(ps_tr[:, j * 128:(j + 1) * 128],
                                       v_fm[:, j * 128:(j + 1) * 128], ident[:])
                return last
            G("pe", vtrs, waits=W("dve"))
            G("dve", lambda e: e.tensor_copy(v_tm[:], ps_tr[:]), waits=W("pe"), stall=True)

            G("sc", lambda e: e.activation(ew[:], za[:], AF.Exp, scale=-1.0),
              waits=W("dve") + W("st"))
            G("dve", lambda e: e.tensor_scalar(ew[:], ew[:], 1.0, None, ALU.add), waits=W("sc"))
            G("sc", lambda e: e.activation(lp[:], ew[:], AF.Ln), waits=W("dve"))
            G("dve", lambda e: e.tensor_tensor(
                dbw[:].rearrange("p (a b) -> p a b", b=128),
                ew[:].rearrange("p (a b) -> p a b", b=128),
                beta[:].unsqueeze(2).broadcast_to([128, 8, 128]),
                ALU.mult), waits=W("sc"))
            if p > 0:
                def sigmm(e):
                    last = None
                    for half in range(2):
                        last = e.matmul(ps_big[0:64, half * 512:(half + 1) * 512],
                                        w64b[:, 0:64], lp[:, half * 512:(half + 1) * 512],
                                        start=True, stop=True)
                    return last
                G("pe", sigmm, waits=W("sc", "dve"))
                G("sc", lambda e: e.activation(sigm[:], ps_big[0:64, :], AF.Exp), waits=W("pe"))
                G("dve", lambda e: e.tensor_mul(s_st[:], s_st[:], sigm[:]), waits=W("sc"))
                G("dve", lambda e: e.tensor_copy(s_stb[:], s_st[:]))
            def trimm(mat):
                def f(e):
                    last = None
                    for half in range(2):
                        last = e.matmul(ps_big[:, half * 512:(half + 1) * 512],
                                        mat[:], lp[:, half * 512:(half + 1) * 512],
                                        start=True, stop=True)
                    return last
                return f
            G("pe", trimm(trim_in), waits=W("dve", "sc"))
            G("sc", lambda e: e.activation(gin[:], ps_big[:], AF.Exp), waits=W("pe"))
            G("pe", trimm(trim_iex), waits=W("sc"))
            G("sc", lambda e: e.activation(igex[:], ps_big[:], AF.Exp), waits=W("pe"))
            G("dve", lambda e: e.tensor_mul(vt_w[:], v_tm[:], igex[:]), waits=W("sc"))
            dump("gin", lambda: gin[:], p)
            dump("igex", lambda: igex[:], p)


            def hk(hh):
                return k_fm[0:64, hh * 128:(hh + 1) * 128]
            def hq(hh):
                return q_fm[0:64, hh * 128:(hh + 1) * 128]
            def gramms(e):
                last = None
                for hh in range(8):
                    last = e.matmul(ps_big[:, hh * 128:(hh + 1) * 128], hk(hh), hk(hh),
                                    start=True, stop=True)
                return last
            G("pe", gramms, waits=W("dve", "sc"))
            G("dve", lambda e: e.tensor_scalar(gram[:], ps_big[:], -1.0, None, ALU.mult),
              waits=W("pe"), stall=True)
            def atilms(e):
                last = None
                for hh in range(8):
                    last = e.matmul(ps_big[:, hh * 128:(hh + 1) * 128], hk(hh), hq(hh),
                                    start=True, stop=True)
                return last
            G("pe", atilms, waits=W("dve"))
            G("dve", lambda e: e.tensor_mul(
                atil[:].rearrange("p (h b) -> p h b", b=128),
                ps_big[:].rearrange("p (h b) -> p h b", b=128),
                mask_incl[:].unsqueeze(1).broadcast_to([128, 8, 128])),
              waits=W("pe"), stall=True)
            def qs0(e):
                last = None
                for hh in range(8):
                    last = e.matmul(ps_big[:, hh * 128:(hh + 1) * 128], hq(hh),
                                    s_stb[:, hh * 128:(hh + 1) * 128],
                                    start=(hh % 4 == 0), stop=False)
                return last
            G("pe", qs0, waits=W("dve", "sc"))

            DMA("gp", dr["scr_db"], dbw[:], waits=W("dve"))
            for ch in range(2):
                DMA("gp",
                    db_sh[:].rearrange("p (c v) -> p c v", v=128)[:, ch * 4:(ch + 1) * 4, :],
                    dr["scr_db"].rearrange("(c s) (h v) -> (s h) c v", s=16, v=128)
                        [:, ch * 4:(ch + 1) * 4, :],
                    waits=W("ld"))

            G("sc", lambda e: e.square(scr[:], xn_tm[:]), waits=W("dve"))
            G("dve", lambda e: e.reduce_sum(stat[:, 16:17], scr[:], axis=AX.X), waits=W("sc"))
            def rmsst(e):
                e.tensor_scalar(stat[:, 17:18], stat[:, 16:17], 1.0 / D, None, ALU.mult)
                return e.tensor_scalar(stat[:, 17:18], stat[:, 17:18], EPS_RMS, None, ALU.add)
            G("dve", rmsst)
            G("sc", lambda e: e.activation(stat[:, 18:19], stat[:, 17:18], AF.Ln), waits=W("dve"))
            G("sc", lambda e: e.activation(stat[:, 19:20], stat[:, 18:19], AF.Exp, scale=-0.5))
            G("dve", lambda e: e.tensor_scalar(scr[:], xn_tm[:], stat[:, 19:20], None, ALU.mult),
              waits=W("sc"))
            G("dve", lambda e: e.reduce_max(stat[:, 20:21], scr[:], axis=AX.X,
                                            apply_absolute_value=True))
            G("dve", lambda e: e.tensor_scalar(stat[:, 20:21], stat[:, 20:21], EPS_ACT, None, ALU.max))
            G("sc", lambda e: e.activation(stat[:, 21:22], stat[:, 20:21], AF.Ln), waits=W("dve"))
            G("sc", lambda e: e.activation(stat[:, 22:23], stat[:, 21:22], AF.Exp, scale=-1.0))
            G("dve", lambda e: e.tensor_scalar(stat[:, 23:24], stat[:, 22:23], 127.0, None, ALU.mult),
              waits=W("sc"))
            G("dve", lambda e: e.tensor_scalar(scr[:], scr[:], stat[:, 23:24], RND, ALU.mult, ALU.add))
            G("dve", lambda e: e.tensor_scalar(scr[:], scr[:], RND, None, ALU.subtract))
            def oscale(e):
                e.tensor_mul(stat[:, 22:23], stat[:, 20:21], ws_rep[:])
                return e.tensor_scalar(stat[:, 22:23], stat[:, 22:23], 1.0 / 127.0, None, ALU.mult)
            G("dve", oscale)
            G("dve", lambda e: e.tensor_copy(dbw[:], scr[:]), waits=W("ld"))
            def ntrs(e):
                last = None
                for j in range(8):
                    last = e.transpose(ps_tr[:, j * 128:(j + 1) * 128],
                                       dbw[:, j * 128:(j + 1) * 128], ident[:])
                return last
            G("pe", ntrs, waits=W("dve", "sc"))
            G("dve", lambda e: e.tensor_copy(n_fm[:], ps_tr[:]), waits=W("pe"), stall=True)

            for i16 in range(8):
                t0 = i16 * 16
                kS = k_slab[:, t0 * 8:(t0 + 16) * 8]
                G("pe", lambda e, kS=kS: e.matmul(ps_int[:, 0:128], kS, kS,
                                                  start=True, stop=True),
                  waits=W("dve", "sc"))
                G("dve", lambda e: e.tensor_mul(
                    c_steps[:].rearrange("p (r f) -> p r f", f=128),
                    bmask[:].rearrange("p (r f) -> p r f", f=128),
                    ps_int[:, 0:128].unsqueeze(1).broadcast_to([128, 16, 128])),
                  waits=W("pe"), stall=True)
                for half in range(2):
                    hs = range(half * 4, half * 4 + 4)
                    def ksmms(e, hs=hs, i16=i16, t0=t0, half=half):
                        last = None
                        first = True
                        for hh in hs:
                            c0 = (hh % 4) * 128
                            last = e.matmul(ps_ks[0:16, c0:c0 + 128],
                                            hk(hh)[:, t0:t0 + 16],
                                            s_stb[:, hh * 128:(hh + 1) * 128],
                                            start=first, stop=False)
                            first = False
                            if i16 > 0:
                                last = e.matmul(ps_ks[0:16, c0:c0 + 128],
                                                gram[0:t0, hh * 128 + t0: hh * 128 + t0 + 16],
                                                u_big[0:t0, hh * 128:(hh + 1) * 128],
                                                start=False, stop=False)
                        last = e.matmul(ps_ks[0:16, 0:512],
                                        negident[:, t0:t0 + 16],
                                        vt_w[:, half * 512:(half + 1) * 512],
                                        start=False, stop=True)
                        return last
                    G("pe", ksmms, waits=W("dve") + (W("ld") if half == 0 else []))
                    G("dve", lambda e, half=half: e.tensor_copy(
                        stag[:, half * 512:(half + 1) * 512], ps_ks[0:16, 0:512]),
                      waits=W("pe"))
                def permin(e):
                    last = None
                    for hh in range(8):
                        last = e.matmul(ps_int[:, 128:256], pstack[:, hh * 128:(hh + 1) * 128],
                                        stag[:, hh * 128:(hh + 1) * 128],
                                        start=(hh == 0), stop=False)
                    return last
                G("pe", permin, waits=W("dve"))
                def chainTT(s, i16=i16):
                    pb2 = 96 if s >= 12 else (64 if s >= 8 else (32 if s >= 4 else 0))
                    def f(e):
                        return e.tensor_mul(
                            u_slab[pb2:8 * s + 8, :],
                            db_sh[pb2:8 * s + 8, i16 * 128:(i16 + 1) * 128],
                            ps_int[pb2:8 * s + 8, 128:256])
                    return f
                G("dve", chainTT(0), waits=W("pe", "ld"), stall=True)
                for r16 in range(15):
                    pb = 64 if r16 >= 8 else (32 if r16 >= 4 else 0)
                    G("pe", lambda e, r16=r16, pb=pb: e.matmul(
                        ps_int[:, 128:256],
                        c_steps[pb:8 * r16 + 8, r16 * 128:(r16 + 1) * 128],
                        u_slab[pb:8 * r16 + 8, :],
                        start=False, stop=(r16 == 14)), waits=W("dve"))
                    G("dve", chainTT(r16 + 1), waits=W("pe"), stall=True)
                DMA("gp", dr["scr_ub"][:, t0 * 8:t0 * 8 + 128], u_slab[:],
                    waits=W("dve"))
                DMA("sy", u_big[t0:t0 + 16, :].rearrange("s (h v) -> s h v", v=128),
                    dr["scr_ub"][:, t0 * 8:t0 * 8 + 128]
                        .rearrange("(s h) v -> s h v", h=8),
                    waits=W("ld"))

            dump("u_big", lambda: u_big[:], p)
            def ocouple(e):
                last = None
                for hh in range(8):
                    last = e.matmul(ps_big[:, hh * 128:(hh + 1) * 128],
                                    atil[:, hh * 128:(hh + 1) * 128],
                                    u_big[:, hh * 128:(hh + 1) * 128],
                                    start=False, stop=True)
                return last
            G("pe", ocouple, waits=W("dve", "sc", "ld"))
            G("dve", lambda e: e.tensor_mul(o_tm[:], gin[:], ps_big[:]), waits=W("pe"))
            dump("o_tm", lambda: o_tm[:], p)

            def repl127(e):
                last = None
                for half in range(2):
                    last = e.matmul(ps_big[0:64, half * 512:(half + 1) * 512],
                                    oh127[:, 0:64], gin[:, half * 512:(half + 1) * 512],
                                    start=True, stop=True)
                return last
            G("pe", repl127, waits=W("dve"))
            G("dve", lambda e: e.tensor_copy(sigm[:], ps_big[0:64, :]), waits=W("pe"))
            for hp in range(2):
                def wmms(e, hp=hp):
                    last = None
                    for hh in range(4 * hp, 4 * hp + 4):
                        last = e.matmul(ps_a[0:64, (hh % 4) * 128:((hh % 4) + 1) * 128],
                                        k_tm[:, hh * 64:(hh + 1) * 64],
                                        u_big[:, hh * 128:(hh + 1) * 128],
                                        start=True, stop=True)
                    return last
                G("pe", wmms, waits=W("dve", "sc"))
                sl = slice(hp * 512, (hp + 1) * 512)
                G("dve", lambda e, sl=sl: e.tensor_sub(s_st[:, sl], s_st[:, sl],
                                                       ps_a[0:64, 0:512]), waits=W("pe"))
            G("dve", lambda e: e.tensor_mul(s_st[:], s_st[:], sigm[:]))
            G("dve", lambda e: e.tensor_copy(s_stb[:], s_st[:]))
            dump("s_st", lambda: s_st[:], p)

            G("sc", lambda e: e.square(scr[:], o_tm[:]), waits=W("dve"))
            def gstats(e):
                e.reduce_sum(stat[:, 0:8], o_tm[:].rearrange("p (a b) -> p a b", b=128), axis=AX.X)
                return e.reduce_sum(stat[:, 8:16], scr[:].rearrange("p (a b) -> p a b", b=128), axis=AX.X)
            G("dve", gstats, waits=W("sc"))
            def gstats2(e):
                e.tensor_scalar(stat[:, 0:8], stat[:, 0:8], 1.0 / DV, None, ALU.mult)
                e.tensor_scalar(stat[:, 8:16], stat[:, 8:16], 1.0 / DV, None, ALU.mult)
                e.tensor_mul(scr[:, 0:8], stat[:, 0:8], stat[:, 0:8])
                e.tensor_sub(stat[:, 8:16], stat[:, 8:16], scr[:, 0:8])
                return e.tensor_scalar(stat[:, 8:16], stat[:, 8:16], EPS_GN, None, ALU.add)
            G("dve", gstats2)
            G("sc", lambda e: e.activation(stat[:, 8:16], stat[:, 8:16], AF.Ln), waits=W("dve"))
            G("sc", lambda e: e.activation(stat[:, 8:16], stat[:, 8:16], AF.Exp, scale=-0.5))
            def onorm(e):
                o3 = o_tm[:].rearrange("p (a b) -> p a b", b=128)
                mu = stat[:, 0:8].unsqueeze(2).broadcast_to([128, 8, 128])
                rs = stat[:, 8:16].unsqueeze(2).broadcast_to([128, 8, 128])
                e.tensor_sub(o3, o3, mu)
                e.tensor_mul(o3, o3, rs)
                e.tensor_mul(o_tm[:].rearrange("p (a b) -> p a b", b=128),
                             o_tm[:].rearrange("p (a b) -> p a b", b=128),
                             lnw_rep[:].unsqueeze(1).broadcast_to([128, 8, 128]))
                e.tensor_add(o_tm[:].rearrange("p (a b) -> p a b", b=128),
                             o_tm[:].rearrange("p (a b) -> p a b", b=128),
                             lnb_rep[:].unsqueeze(1).broadcast_to([128, 8, 128]))
                return e.tensor_mul(o_tm[:], o_tm[:], g_tm[:])
            G("dve", onorm, waits=W("sc"))
            def ogtrs(e):
                last = None
                for j in range(8):
                    last = e.transpose(ps_big[:, j * 128:(j + 1) * 128],
                                       o_tm[:, j * 128:(j + 1) * 128], identf[:])
                return last
            G("pe", ogtrs, waits=W("dve", "sc"))
            G("dve", lambda e: e.tensor_copy(og_fm[:], ps_big[:]), waits=W("pe"), stall=True)
            def wfmms(e):
                last = None
                for half in range(2):
                    dst = ps_big[:, half * 512:(half + 1) * 512]
                    for kc in range(8):
                        last = e.matmul(dst, og_fm[:, kc * 128:(kc + 1) * 128],
                                        w_o[:, kc * 1024 + half * 512: kc * 1024 + half * 512 + 512],
                                        start=(kc == 0), stop=False)
                    last = e.matmul(dst, ones1[:], fbr[:, half * 512:(half + 1) * 512],
                                    start=False, stop=True)
                    dstf = ps_ub[:, half * 512:(half + 1) * 512]
                    for kc in range(8):
                        last = e.matmul(dstf, n_fm[:, kc * 128:(kc + 1) * 128],
                                        w_f[:, kc * 1024 + half * 512: kc * 1024 + half * 512 + 512],
                                        start=(kc == 0), stop=(kc == 7))
                return last
            G("pe", wfmms, waits=W("sc", "dve"))
            def resfin(e):
                e.tensor_add(res1[:], x_t[:], ps_big[:])
                return e.scalar_tensor_tensor(res1[:], ps_ub[:], stat[:, 22:23],
                                              res1[:], ALU.mult, ALU.add)
            G("dve", resfin, waits=W("pe"), stall=True)
            DMA("sy", dr["out"][r0:r0 + C, :], res1[:], waits=W("dve"), sem="st")
            dump("out_rows", lambda: res1[:], p)

        G("sy", lambda e: e.wait_ge(sems["st"], cnt["st"]) or None, waits=(), sem=None)

    for t in reversed(ctx):
        t.__exit__(None, None, None)
    return nc





_BUILD_CACHE = {}

def kernel(**inputs):
    import numpy as np
    from concourse.bass_utils import run_bass_kernel_spmd
    T = 8192
    x_full = np.asarray(inputs["L0_attn_x"])
    B = x_full.shape[0]
    g = host_prep(inputs)
    if T not in _BUILD_CACHE:
        _BUILD_CACHE[T] = build(T)
    nc = _BUILD_CACHE[T]
    in_maps = []
    for c in range(8):
        im = dict(g)
        im["x"] = np.ascontiguousarray(x_full[c % B])
        in_maps.append(im)
    res = run_bass_kernel_spmd(nc, in_maps, core_ids=list(range(8)))
    out = np.stack([np.asarray(res.results[b]["out"]) for b in range(B)], axis=0)
    return out.astype(np.float32)


# version 13
# speedup vs baseline: 1.8489x; 1.0416x over previous
import numpy as np
import ml_dtypes
import concourse.bass as bass
import concourse.mybir as mybir

F32 = mybir.dt.float32
BF16 = mybir.dt.bfloat16
AF = mybir.ActivationFunctionType
ALU = mybir.AluOpType
AX = mybir.AxisListType

H, DK, DV, D = 8, 64, 128, 1024
C = 128
EPS_LN = 1e-5
EPS_GN = 1e-5
EPS_RMS = 1e-6
EPS_ACT = 1e-5
KSCALE = DK ** (-0.5)
RND = 12582912.0


def host_constants():
    r = np.arange(128)
    tri_incl = (r[:, None] <= r[None, :]).astype(np.float32)
    tri_excl = (r[:, None] < r[None, :]).astype(np.float32)
    mlow = ((r[:, None] < 64).astype(np.float32)) * np.ones((1, 128), np.float32)
    c = {}
    c["ident"] = np.eye(128, dtype=np.float32)
    c["trim_in"] = (mlow - tri_incl)
    c["trim_iex"] = (tri_excl - mlow)
    c["w64b"] = (-(r < 64).astype(np.float32))[:, None] * np.ones((1, 64), np.float32)
    c["oh127"] = (r == 127).astype(np.float32)[:, None] * np.ones((1, 64), np.float32)
    pstack = np.zeros((16, 8 * 128), np.float32)
    for h in range(8):
        for s in range(16):
            pstack[s, h * 128 + s * 8 + h] = 1.0
    c["pstack"] = pstack
    bm = np.zeros((128, 16 * 128), np.float32)
    for h in range(8):
        for rr in range(16):
            for s in range(rr + 1, 16):
                bm[rr * 8 + h, rr * 128 + s * 8 + h] = -1.0
    c["bmask"] = bm
    am = np.zeros((128, 128), np.float32)
    for h in range(8):
        for rr in range(16):
            for s in range(16):
                am[rr * 8 + h, s * 8 + h] = -1.0
    c["allmask"] = am
    c["negident"] = -np.eye(128, dtype=np.float32)
    c["mask_incl"] = -tri_incl
    return c


def host_prep(inputs):
    g = {}
    tr = lambda a: np.ascontiguousarray(np.asarray(a).T).astype(np.float32)
    g["wqt"] = tr(inputs["L0_attn_Wq"])
    g["wkt"] = tr(inputs["L0_attn_Wk"])
    g["wvt"] = tr(inputs["L0_attn_Wv"])
    g["wat"] = tr(inputs["L0_attn_Wa"])
    g["wgt"] = tr(inputs["L0_attn_Wg"])
    g["wbt"] = tr(inputs["L0_attn_Wb"])
    g["wot"] = tr(inputs["L0_attn_Wo"])
    g["wft"] = tr(inputs["L0_ffn_w_int8"])
    g["ba"] = np.asarray(inputs["L0_attn_ba"]).reshape(1, D).astype(np.float32)
    g["bb"] = np.asarray(inputs["L0_attn_bb"]).reshape(1, 8).astype(np.float32)
    tqk = np.concatenate([
        np.asarray(inputs["L0_attn_qc_w"]).reshape(512, 4),
        np.asarray(inputs["L0_attn_kc_w"]).reshape(512, 4)], axis=0).astype(np.float32)
    g["taps_qk"] = np.ascontiguousarray(tqk.reshape(16, 64, 4).transpose(1, 0, 2).reshape(64, 64))
    tv = np.asarray(inputs["L0_attn_vc_w"]).reshape(1024, 4).astype(np.float32)
    g["taps_v"] = np.ascontiguousarray(tv.reshape(8, 128, 4).transpose(1, 0, 2).reshape(128, 32))
    cqk = np.concatenate([
        np.asarray(inputs["L0_attn_qc_b"]).reshape(512),
        np.asarray(inputs["L0_attn_kc_b"]).reshape(512)]).astype(np.float32)
    g["cb_qk"] = np.ascontiguousarray(cqk.reshape(16, 64).T)
    g["cb_v"] = np.ascontiguousarray(
        np.asarray(inputs["L0_attn_vc_b"]).reshape(8, 128).T.astype(np.float32))
    g["lnw"] = np.asarray(inputs["L0_attn_ln_w"]).reshape(128).astype(np.float32)
    g["lnb"] = np.asarray(inputs["L0_attn_ln_b"]).reshape(128).astype(np.float32)
    g["fbias"] = np.asarray(inputs["L0_ffn_bias"]).reshape(D).astype(np.float32)
    g["wscale"] = np.asarray(inputs["L0_ffn_weight_scale"]).reshape(1).astype(np.float32)
    for k, v in host_constants().items():
        g[k] = v.astype(np.float32)
    return g


def build(T, dumps=()):
    assert T % C == 0
    NOUT = T // C
    nc = bass.Bass("TRN2", target_bir_lowering=False, debug=False)
    dr = {}
    dr["x"] = nc.dram_tensor("x", [T, D], F32, kind="ExternalInput").ap()
    for nm, shp in [("wqt", [D, 512]), ("wkt", [D, 512]), ("wvt", [D, D]),
                    ("wat", [D, D]), ("wgt", [D, D]), ("wbt", [D, 8]),
                    ("wot", [D, D]), ("wft", [D, D]), ("ba", [1, D]),
                    ("bb", [1, 8]), ("taps_qk", [64, 64]), ("taps_v", [128, 32]),
                    ("cb_qk", [64, 16]), ("cb_v", [128, 8]),
                    ("lnw", [128]), ("lnb", [128]), ("fbias", [D]), ("wscale", [1]),
                    ("ident", [128, 128]), ("trim_in", [128, 128]),
                    ("trim_iex", [128, 128]), ("w64b", [128, 64]),
                    ("oh127", [128, 64]), ("pstack", [16, 1024]),
                    ("bmask", [128, 2048]), ("allmask", [128, 128]),
                    ("negident", [128, 128]),
                    ("mask_incl", [128, 128])]:
        dr[nm] = nc.dram_tensor(nm, shp, F32, kind="ExternalInput").ap()
    dr["out"] = nc.dram_tensor("out", [T, D], F32, kind="ExternalOutput").ap()
    dr["scr_db"] = nc.dram_tensor("scr_db", [128, 1024], BF16, kind="Internal").ap()
    dr["scr_ub"] = nc.dram_tensor("scr_ub", [128, 1024], BF16, kind="Internal").ap()
    dump_dr = {}
    for tag, _p, shp in dumps:
        dump_dr[tag] = nc.dram_tensor("dbg_" + tag, list(shp), F32,
                                      kind="ExternalOutput").ap()

    ctx = []
    def sb(name, shape, dt):
        t = nc.sbuf_tensor(name, list(shape), dt); ctx.append(t); return t.__enter__()
    def ps(name, shape, dt):
        t = nc.psum_tensor(name, list(shape), dt); ctx.append(t); return t.__enter__()

    w_q = sb("w_q", [128, 8 * 512], BF16)
    w_k = sb("w_k", [128, 8 * 512], BF16)
    w_v = sb("w_v", [128, 8 * 1024], BF16)
    w_a = sb("w_a", [128, 8 * 1024], BF16)
    w_g = sb("w_g", [128, 8 * 1024], BF16)
    w_b = sb("w_b", [128, 8 * 8], BF16)
    w_o = sb("w_o", [128, 8 * 1024], BF16)
    w_f = sb("w_f", [128, 8 * 1024], BF16)
    ba_r = sb("ba_r", [1, D], BF16)
    bb_r = sb("bb_r", [1, 8], BF16)
    ones1 = sb("ones1", [1, 128], BF16)
    taps_qk = sb("taps_qkt", [64, 64], F32)
    taps_v = sb("taps_vt", [128, 32], F32)
    cb_qk = sb("cb_qkt", [64, 16], F32)
    cb_v = sb("cb_vt", [128, 8], F32)
    ident = sb("identt", [128, 128], BF16)
    identf = sb("identft", [128, 128], F32)
    trim_in = sb("trim_int", [128, 128], F32)
    trim_iex = sb("trim_iext", [128, 128], F32)
    w64b = sb("w64bt", [128, 64], F32)
    oh127 = sb("oh127t", [128, 64], F32)
    pstack = sb("pstackt", [16, 1024], BF16)
    bmask = sb("bmaskt", [128, 2048], BF16)
    allmask = sb("allmaskt", [128, 128], BF16)
    c_cross = sb("c_cross", [128, 2 * 128], BF16)
    negident = sb("negidentt", [128, 128], BF16)
    mask_incl = sb("mask_inclt", [128, 128], F32)
    lnw_rep = sb("lnw_rep", [128, 128], F32)
    lnb_rep = sb("lnb_rep", [128, 128], F32)
    fbr = sb("fbr", [1, 1024], BF16)
    ws_rep = sb("ws_rep", [128, 1], F32)
    scl = sb("scl", [128, 2 * 1024], BF16)

    x_t = sb("x_tt", [128, D], F32)
    scr = sb("scr", [128, D], F32)
    stat = sb("stat", [128, 24], F32)
    xn_tm = sb("xn_tm", [128, D], F32)
    xn_fm = sb("xn_fm", [128, 8 * 128], BF16)
    qraw_tm = sb("qraw_tm", [128, 512], BF16)
    kraw_tm = sb("kraw_tm", [128, 512], BF16)
    vraw_tm = sb("vraw_tm", [128, 1024], BF16)
    bz = sb("bz", [128, 8], F32)
    qr_fm = sb("qr_fm", [64, 8 * 131], BF16)
    kr_fm = sb("kr_fm", [64, 8 * 131], BF16)
    vr_fm = sb("vr_fm", [128, 8 * 131], BF16)
    q_fm = sb("q_fm", [64, 8 * 128], BF16)
    k_fm = sb("k_fm", [64, 8 * 128], BF16)
    v_fm = sb("v_fm", [128, 8 * 128], BF16)
    k_tm = sb("k_tm", [128, 512], BF16)
    v_tm = sb("v_tm", [128, 1024], BF16)
    g_tm = sb("g_tm", [128, 1024], BF16)
    beta = sb("beta", [128, 8], F32)
    gin = sb("gin", [128, 1024], F32)
    igex = sb("igex", [128, 1024], F32)
    vt_w = sb("vt_w", [128, 1024], BF16)
    dbw = sb("dbw", [128, 1024], BF16)
    sigm = sb("sigm", [64, 1024], F32)
    gram = sb("gram", [128, 8 * 128], BF16)
    atil = sb("atil", [128, 8 * 128], BF16)
    s_st = sb("s_st", [64, 8 * 128], F32)
    s_stb = sb("s_stb", [64, 8 * 128], BF16)
    stag = sb("stag", [16, 8 * 128], BF16)
    c_steps = sb("c_steps", [128, 16 * 128], BF16)
    k_slab = sb("k_slab", [64, 1024], BF16)
    u_slab = sb("u_slab", [128, 128], BF16)
    db_sh = sb("db_sh", [128, 8 * 128], BF16)
    u_big = sb("u_big", [128, 8 * 128], BF16)
    o_tm = sb("o_tm", [128, 1024], F32)
    res1 = sb("res1", [128, 1024], F32)
    za = scr
    lp = o_tm
    ew = res1
    og_fm = xn_fm
    n_fm = v_fm
    carry_qk = sb("carry_qk", [64, 16 * 3], BF16)
    carry_v = sb("carry_v", [128, 8 * 3], BF16)

    ps_big = ps("ps_big", [128, 1024], F32)
    ps_a = ps("ps_a", [128, 512], F32)
    ps_ks = ps("ps_ks", [16, 512], F32)
    ps_tr = ps("ps_tr", [128, 1024], BF16)
    ps_int = ps("ps_int", [128, 512], F32)
    ps_ub = ps("ps_ub", [128, 1024], F32)

    sems = {}
    cnt = {}
    for n in ["pe", "dve", "sc", "gp", "ld", "st"]:
        t = nc.semaphore("s_" + n); ctx.append(t)
        sems[n] = t.__enter__(); cnt[n] = 0

    dump_list = list(dumps)

    with nc.Block() as block:
        SEC = {"pe": block.tensor, "dve": block.vector, "sc": block.scalar,
               "gp": block.gpsimd, "sy": block.sync}

        dirty = {"dve": set(), "sc": set(), "gp": set()}

        class HazardGuard:
            def __init__(self, e, eng):
                self._e = e
                self._d = dirty[eng]
            def __getattr__(self, name):
                f = getattr(self._e, name)
                if name in ("wait_ge", "nop", "drain") or not callable(f):
                    return f
                if name == "dma_start":
                    return f
                def g(*a, **k):
                    wr, rd = set(), set()
                    for i, x in enumerate(a):
                        if isinstance(x, bass.AP):
                            (wr if i == 0 else rd).add(x.tensor.name)
                    for kw, x in k.items():
                        if isinstance(x, bass.AP):
                            (wr if kw in ("out", "outs") else rd).add(x.tensor.name)
                    if rd & self._d:
                        self._e.drain()
                        self._d.clear()
                    res = f(*a, **k)
                    self._d |= wr
                    return res
                return g

        def G(eng, fn, waits=(), sem=None, inc=1, stall=False):
            semn = sem or (eng if eng != "sy" else None)
            wl = [(n, v) for (n, v) in waits if v > 0]
            guard = eng in ("dve", "sc", "gp")
            if stall and guard:
                dirty[eng].clear()
            def body(e):
                for (n, v) in wl:
                    e.wait_ge(sems[n], v)
                last = fn(HazardGuard(e, eng) if guard else e)
                if semn is not None and inc and last is not None:
                    last.then_inc(sems[semn], inc)
            SEC[eng](body)
            if semn is not None and inc:
                cnt[semn] += inc

        def W(*names):
            return [(n, cnt[n]) for n in names]

        def WALL():
            return [(n, cnt[n]) for n in ["pe", "dve", "sc", "gp", "ld"]]

        def DMA(eng, out, in_, waits=(), sem="ld"):
            G(eng, lambda e: e.dma_start(out=out, in_=in_), waits, sem=sem, inc=16)

        def dump(tag, ap_fn, cur_p=None):
            for (t, pp, shp) in dump_list:
                if t == tag and (cur_p is None or pp == cur_p):
                    DMA("gp", dump_dr[tag], ap_fn(), waits=WALL(), sem="st")
                    stc = [("st", cnt["st"])]
                    for eng in ["dve", "sc", "pe", "gp"]:
                        G(eng, lambda e: e.nop(), waits=stc, inc=0)

        for nm, dst, cast in [("wqt", w_q, 1), ("wkt", w_k, 1), ("wvt", w_v, 1),
                              ("wat", w_a, 1), ("wgt", w_g, 1), ("wot", w_o, 1),
                              ("wft", w_f, 1)]:
            ncols = dr[nm].shape[1]
            DMA("gp", dst[:].rearrange("p (c n) -> p c n", n=ncols),
                dr[nm].rearrange("(c p) n -> p c n", p=128))
        DMA("gp", w_b[:].rearrange("p (c n) -> p c n", n=8),
            dr["wbt"].rearrange("(c p) n -> p c n", p=128))
        DMA("gp", ba_r[:], dr["ba"])
        DMA("gp", bb_r[:], dr["bb"])
        DMA("gp", ident[:], dr["ident"])
        DMA("sy", identf[:], dr["ident"])
        DMA("gp", pstack[:], dr["pstack"])
        DMA("gp", bmask[:], dr["bmask"])
        DMA("gp", allmask[:], dr["allmask"])
        DMA("gp", negident[:], dr["negident"])
        DMA("sy", taps_qk[:], dr["taps_qk"])
        DMA("sy", taps_v[:], dr["taps_v"])
        DMA("sy", cb_qk[:], dr["cb_qk"])
        DMA("sy", cb_v[:], dr["cb_v"])
        DMA("sy", trim_in[:], dr["trim_in"])
        DMA("sy", trim_iex[:], dr["trim_iex"])
        DMA("sy", w64b[:], dr["w64b"])
        DMA("sy", oh127[:], dr["oh127"])
        DMA("sy", mask_incl[:], dr["mask_incl"])
        DMA("sy", lnw_rep[:], dr["lnw"][None, :].broadcast_to([128, 128]))
        DMA("sy", lnb_rep[:], dr["lnb"][None, :].broadcast_to([128, 128]))
        DMA("gp", fbr[:], dr["fbias"][None, :])
        DMA("sy", ws_rep[:], dr["wscale"][None, :].broadcast_to([128, 1]))

        G("dve", lambda e: e.memset(ones1[:], 1.0))
        G("dve", lambda e: e.memset(u_slab[:], 0.0))
        G("dve", lambda e: e.memset(s_st[:], 0.0))
        G("dve", lambda e: e.memset(s_stb[:], 0.0))
        G("dve", lambda e: e.memset(carry_qk[:], 0.0))
        G("dve", lambda e: e.memset(carry_v[:], 0.0))

        for p in range(NOUT):
            r0 = p * C
            DMA("sy", x_t[:], dr["x"][r0:r0 + C, :], waits=W("dve"))

            def ln_stats(e):
                e.reduce_sum(stat[:, 0:1], x_t[:], axis=AX.X)
                return e.reduce_sum(stat[:, 1:2], scr[:], axis=AX.X)
            G("sc", lambda e: e.square(scr[:], x_t[:]), waits=W("ld", "dve"))
            G("dve", ln_stats, waits=W("sc", "ld"))
            def ln_stats2(e):
                e.tensor_scalar(stat[:, 2:3], stat[:, 0:1], 1.0 / D, None, ALU.mult)
                e.tensor_scalar(stat[:, 3:4], stat[:, 1:2], 1.0 / D, None, ALU.mult)
                e.tensor_mul(stat[:, 4:5], stat[:, 2:3], stat[:, 2:3])
                e.tensor_sub(stat[:, 5:6], stat[:, 3:4], stat[:, 4:5])
                return e.tensor_scalar(stat[:, 5:6], stat[:, 5:6], EPS_LN, None, ALU.add)
            G("dve", ln_stats2)
            G("sc", lambda e: e.activation(stat[:, 6:7], stat[:, 5:6], AF.Ln), waits=W("dve"))
            G("sc", lambda e: e.activation(stat[:, 7:8], stat[:, 6:7], AF.Exp, scale=-0.5))
            G("dve", lambda e: e.tensor_scalar(stat[:, 8:9], stat[:, 2:3], -1.0, None, ALU.mult), waits=W("sc"))
            G("dve", lambda e: e.tensor_scalar(xn_tm[:], x_t[:], stat[:, 8:9], stat[:, 7:8], ALU.add, ALU.mult))

            def xntr(e):
                last = None
                for j in range(8):
                    last = e.transpose(ps_big[:, j * 128:(j + 1) * 128],
                                       xn_tm[:, j * 128:(j + 1) * 128], identf[:])
                return last
            G("pe", xntr, waits=W("dve", "sc"))
            G("dve", lambda e: e.tensor_copy(xn_fm[:], ps_big[:]), waits=W("pe"), stall=True)
            dump("x_t", lambda: x_t[:], p)
            dump("xn_tm", lambda: xn_tm[:], p)
            dump("stat", lambda: stat[:], p)
            dump("xn_fm", lambda: xn_fm[:], p)

            def proj(dst_psum, w, wstride, nlo, nwid, bias_row=None):
                def f(e):
                    last = None
                    for kc in range(8):
                        last = e.matmul(dst_psum,
                                        xn_fm[:, kc * 128:(kc + 1) * 128],
                                        w[:, kc * wstride + nlo: kc * wstride + nlo + nwid],
                                        start=(kc == 0), stop=(kc == 7 and bias_row is None))
                    if bias_row is not None:
                        last = e.matmul(dst_psum, ones1[:], bias_row,
                                        start=False, stop=True)
                    return last
                return f
            def proj1(e):
                for half in range(2):
                    proj(ps_big[:, half * 512:(half + 1) * 512], w_a, 1024, half * 512, 512,
                         ba_r[:, half * 512:(half + 1) * 512])(e)
                proj(ps_a[:, 0:512], w_q, 512, 0, 512)(e)
                return proj(ps_ub[:, 0:512], w_k, 512, 0, 512)(e)
            G("pe", proj1, waits=W("sc", "dve"))
            def evac1(e):
                e.tensor_copy(za[:], ps_big[:])
                e.tensor_copy(qraw_tm[:], ps_a[:, 0:512])
                return e.tensor_copy(kraw_tm[:], ps_ub[:, 0:512])
            G("dve", evac1, waits=W("pe"), stall=True)
            def proj2(e):
                for half in range(2):
                    proj(ps_big[:, half * 512:(half + 1) * 512], w_v, 1024, half * 512, 512)(e)
                return proj(ps_a[:, 0:8], w_b, 8, 0, 8, bb_r[:])(e)
            G("pe", proj2, waits=W("dve"))
            def evac2(e):
                e.tensor_copy(vraw_tm[:], ps_big[:])
                return e.tensor_copy(bz[:], ps_a[:, 0:8])
            G("dve", evac2, waits=W("pe"), stall=True)

            def rawtr(src, dst, carry_t, carry_off, pw):
                def trs(e):
                    last = None
                    for j in range(8):
                        last = e.transpose(
                            ps_tr[0:pw, j * 128:(j + 1) * 128],
                            src[:, j * pw:(j + 1) * pw], ident[:])
                    return last
                G("pe", trs, waits=W("sc", "dve"))
                def evac(e):
                    e.tensor_copy(
                        dst[0:pw, :].rearrange("p (s c) -> p s c", c=131)[:, :, 3:131],
                        ps_tr[0:pw, :].rearrange("p (s c) -> p s c", c=128))
                    return e.tensor_copy(
                        dst[0:pw, :].rearrange("p (s c) -> p s c", c=131)[:, :, 0:3],
                        carry_t[0:pw, carry_off * 3:(carry_off + 8) * 3]
                            .rearrange("p (s c) -> p s c", c=3))
                G("dve", evac, waits=W("pe"), stall=True)
            rawtr(qraw_tm, qr_fm, carry_qk, 0, 64)
            rawtr(kraw_tm, kr_fm, carry_qk, 8, 64)
            rawtr(vraw_tm, vr_fm, carry_v, 0, 128)

            def cscale(srcf, tapt, gbase, pw, t4, reg, cbt):
                def f(e):
                    dst3 = scl[0:pw, reg * 1024:(reg + 1) * 1024].rearrange(
                        "p (s c) -> p s c", c=128)
                    src3 = srcf[0:pw, :].rearrange("p (s c) -> p s c", c=131)[:, :, t4:t4 + 128]
                    tap = tapt[0:pw, gbase * 4 + t4: gbase * 4 + t4 + 29: 4] \
                        .unsqueeze(2).broadcast_to([pw, 8, 128])
                    last = e.tensor_tensor(dst3, src3, tap, ALU.mult)
                    if t4 == 0:
                        cb3 = cbt[0:pw, gbase:gbase + 8].unsqueeze(2).broadcast_to([pw, 8, 128])
                        last = e.tensor_tensor(dst3, dst3, cb3, ALU.add)
                    return last
                return f
            def cmm(psr, pw, t4, reg):
                def f(e):
                    for half in range(2):
                        last = e.matmul(psr[0:pw, half * 512:(half + 1) * 512],
                                        ident[0:pw, 0:pw],
                                        scl[0:pw, reg * 1024 + half * 512: reg * 1024 + (half + 1) * 512],
                                        start=(t4 == 0), stop=(t4 == 3))
                    return last
                return f
            def conv_tensor(srcf, dstf, tapt, gbase, pw, psr, pre_waits, cbt):
                pe_after = []
                for t4 in range(4):
                    wd = ([("pe", pe_after[t4 - 2])] if t4 >= 2 else list(pre_waits))
                    G("dve", cscale(srcf, tapt, gbase, pw, t4, t4 % 2, cbt), waits=wd)
                    G("pe", cmm(psr, pw, t4, t4 % 2),
                      waits=W("dve") + (list(pre_waits) if t4 == 0 else []))
                    pe_after.append(cnt["pe"])
                G("sc", lambda e: e.activation(dstf[0:pw, 0:1024], psr[0:pw, 0:1024], AF.Silu),
                  waits=W("pe"))
            conv_tensor(qr_fm, q_fm, taps_qk, 0, 64, ps_big, [], cb_qk)
            conv_tensor(kr_fm, k_fm, taps_qk, 8, 64, ps_ub, W("pe"), cb_qk)
            conv_tensor(vr_fm, v_fm, taps_v, 0, 128, ps_big, W("pe", "sc"), cb_v)
            G("dve", lambda e: e.tensor_scalar(k_fm[0:64, :], k_fm[0:64, :], KSCALE, None, ALU.mult),
              waits=W("sc"))
            G("dve", lambda e: e.tensor_copy(
                k_slab[:].rearrange("p (t h) -> p t h", h=8),
                k_fm[0:64, :].rearrange("p (h t) -> p t h", t=128)))
            def save_carry(e):
                e.tensor_copy(carry_qk[0:64, 0:24].rearrange("p (s c) -> p s c", c=3),
                              qr_fm[0:64, :].rearrange("p (s c) -> p s c", c=131)[:, :, 128:131])
                e.tensor_copy(carry_qk[0:64, 24:48].rearrange("p (s c) -> p s c", c=3),
                              kr_fm[0:64, :].rearrange("p (s c) -> p s c", c=131)[:, :, 128:131])
                return e.tensor_copy(carry_v[:, 0:24].rearrange("p (s c) -> p s c", c=3),
                              vr_fm[:].rearrange("p (s c) -> p s c", c=131)[:, :, 128:131])
            G("dve", save_carry, waits=W("sc"))
            dump("q_fm", lambda: q_fm[:], p)
            dump("v_fm", lambda: v_fm[:], p)

            for half in range(2):
                G("pe", proj(ps_big[:, half * 512:(half + 1) * 512], w_g, 1024, half * 512, 512),
                  waits=W("dve", "sc"))
            G("sc", lambda e: e.activation(o_tm[:], ps_big[:], AF.Tanh, scale=0.5),
              waits=W("pe", "dve"))
            G("dve", lambda e: e.tensor_scalar(g_tm[:], o_tm[:], 1.0, 0.5, ALU.add, ALU.mult),
              waits=W("sc"))
            G("sc", lambda e: e.activation(bz[:, 0:8], bz[:, 0:8], AF.Tanh, scale=0.5),
              waits=W("dve"))
            G("dve", lambda e: e.tensor_scalar(beta[:], bz[:, 0:8], 1.0, 0.5, ALU.add, ALU.mult),
              waits=W("sc"))

            def ktrs(e):
                last = None
                for j in range(8):
                    last = e.transpose(ps_tr[:, j * 64:(j + 1) * 64],
                                       k_fm[0:64, j * 128:(j + 1) * 128], ident[0:64, 0:64])
                return last
            G("pe", ktrs, waits=W("sc", "dve"))
            G("dve", lambda e: e.tensor_copy(k_tm[:], ps_tr[:, 0:512]), waits=W("pe"), stall=True)
            def vtrs(e):
                last = None
                for j in range(8):
                    last = e.transpose(ps_tr[:, j * 128:(j + 1) * 128],
                                       v_fm[:, j * 128:(j + 1) * 128], ident[:])
                return last
            G("pe", vtrs, waits=W("dve"))
            G("dve", lambda e: e.tensor_copy(v_tm[:], ps_tr[:]), waits=W("pe"), stall=True)

            G("sc", lambda e: e.activation(ew[:], za[:], AF.Exp, scale=-1.0),
              waits=W("dve") + W("st"))
            G("dve", lambda e: e.tensor_scalar(ew[:], ew[:], 1.0, None, ALU.add), waits=W("sc"))
            G("sc", lambda e: e.activation(lp[:], ew[:], AF.Ln), waits=W("dve"))
            G("dve", lambda e: e.tensor_tensor(
                dbw[:].rearrange("p (a b) -> p a b", b=128),
                ew[:].rearrange("p (a b) -> p a b", b=128),
                beta[:].unsqueeze(2).broadcast_to([128, 8, 128]),
                ALU.mult), waits=W("sc"))
            if p > 0:
                def sigmm(e):
                    last = None
                    for half in range(2):
                        last = e.matmul(ps_big[0:64, half * 512:(half + 1) * 512],
                                        w64b[:, 0:64], lp[:, half * 512:(half + 1) * 512],
                                        start=True, stop=True)
                    return last
                G("pe", sigmm, waits=W("sc", "dve"))
                G("sc", lambda e: e.activation(sigm[:], ps_big[0:64, :], AF.Exp), waits=W("pe"))
                G("dve", lambda e: e.tensor_mul(s_st[:], s_st[:], sigm[:]), waits=W("sc"))
                G("dve", lambda e: e.tensor_copy(s_stb[:], s_st[:]))
            def trimm(mat):
                def f(e):
                    last = None
                    for half in range(2):
                        last = e.matmul(ps_big[:, half * 512:(half + 1) * 512],
                                        mat[:], lp[:, half * 512:(half + 1) * 512],
                                        start=True, stop=True)
                    return last
                return f
            G("pe", trimm(trim_in), waits=W("dve", "sc"))
            G("sc", lambda e: e.activation(gin[:], ps_big[:], AF.Exp), waits=W("pe"))
            G("pe", trimm(trim_iex), waits=W("sc"))
            G("sc", lambda e: e.activation(igex[:], ps_big[:], AF.Exp), waits=W("pe"))
            G("dve", lambda e: e.tensor_mul(vt_w[:], v_tm[:], igex[:]), waits=W("sc"))
            dump("gin", lambda: gin[:], p)
            dump("igex", lambda: igex[:], p)


            def hk(hh):
                return k_fm[0:64, hh * 128:(hh + 1) * 128]
            def hq(hh):
                return q_fm[0:64, hh * 128:(hh + 1) * 128]
            def gramms(e):
                last = None
                for hh in range(8):
                    last = e.matmul(ps_big[:, hh * 128:(hh + 1) * 128], hk(hh), hk(hh),
                                    start=True, stop=True)
                return last
            G("pe", gramms, waits=W("dve", "sc"))
            G("dve", lambda e: e.tensor_scalar(gram[:], ps_big[:], -1.0, None, ALU.mult),
              waits=W("pe"), stall=True)
            def atilms(e):
                last = None
                for hh in range(8):
                    last = e.matmul(ps_big[:, hh * 128:(hh + 1) * 128], hk(hh), hq(hh),
                                    start=True, stop=True)
                return last
            G("pe", atilms, waits=W("dve"))
            G("dve", lambda e: e.tensor_mul(
                atil[:].rearrange("p (h b) -> p h b", b=128),
                ps_big[:].rearrange("p (h b) -> p h b", b=128),
                mask_incl[:].unsqueeze(1).broadcast_to([128, 8, 128])),
              waits=W("pe"), stall=True)
            def qs0(e):
                last = None
                for hh in range(8):
                    last = e.matmul(ps_big[:, hh * 128:(hh + 1) * 128], hq(hh),
                                    s_stb[:, hh * 128:(hh + 1) * 128],
                                    start=(hh % 4 == 0), stop=False)
                return last
            G("pe", qs0, waits=W("dve", "sc"))

            DMA("gp", dr["scr_db"], dbw[:], waits=W("dve"))
            for ch in range(2):
                DMA("gp",
                    db_sh[:].rearrange("p (c v) -> p c v", v=128)[:, ch * 4:(ch + 1) * 4, :],
                    dr["scr_db"].rearrange("(c s) (h v) -> (s h) c v", s=16, v=128)
                        [:, ch * 4:(ch + 1) * 4, :],
                    waits=W("ld"))

            G("sc", lambda e: e.square(scr[:], xn_tm[:]), waits=W("dve"))
            G("dve", lambda e: e.reduce_sum(stat[:, 16:17], scr[:], axis=AX.X), waits=W("sc"))
            def rmsst(e):
                e.tensor_scalar(stat[:, 17:18], stat[:, 16:17], 1.0 / D, None, ALU.mult)
                return e.tensor_scalar(stat[:, 17:18], stat[:, 17:18], EPS_RMS, None, ALU.add)
            G("dve", rmsst)
            G("sc", lambda e: e.activation(stat[:, 18:19], stat[:, 17:18], AF.Ln), waits=W("dve"))
            G("sc", lambda e: e.activation(stat[:, 19:20], stat[:, 18:19], AF.Exp, scale=-0.5))
            G("dve", lambda e: e.tensor_scalar(scr[:], xn_tm[:], stat[:, 19:20], None, ALU.mult),
              waits=W("sc"))
            G("dve", lambda e: e.reduce_max(stat[:, 20:21], scr[:], axis=AX.X,
                                            apply_absolute_value=True))
            G("dve", lambda e: e.tensor_scalar(stat[:, 20:21], stat[:, 20:21], EPS_ACT, None, ALU.max))
            G("sc", lambda e: e.activation(stat[:, 21:22], stat[:, 20:21], AF.Ln), waits=W("dve"))
            G("sc", lambda e: e.activation(stat[:, 22:23], stat[:, 21:22], AF.Exp, scale=-1.0))
            G("dve", lambda e: e.tensor_scalar(stat[:, 23:24], stat[:, 22:23], 127.0, None, ALU.mult),
              waits=W("sc"))
            G("dve", lambda e: e.tensor_scalar(scr[:], scr[:], stat[:, 23:24], RND, ALU.mult, ALU.add))
            G("dve", lambda e: e.tensor_scalar(scr[:], scr[:], RND, None, ALU.subtract))
            def oscale(e):
                e.tensor_mul(stat[:, 22:23], stat[:, 20:21], ws_rep[:])
                return e.tensor_scalar(stat[:, 22:23], stat[:, 22:23], 1.0 / 127.0, None, ALU.mult)
            G("dve", oscale)
            G("dve", lambda e: e.tensor_copy(dbw[:], scr[:]), waits=W("ld"))
            def ntrs(e):
                last = None
                for j in range(8):
                    last = e.transpose(ps_tr[:, j * 128:(j + 1) * 128],
                                       dbw[:, j * 128:(j + 1) * 128], ident[:])
                return last
            G("pe", ntrs, waits=W("dve", "sc"))
            G("dve", lambda e: e.tensor_copy(n_fm[:], ps_tr[:]), waits=W("pe"), stall=True)

            ld_base = cnt["ld"]
            ld_ub = {}
            for i16 in range(8):
                t0 = i16 * 16
                ld_ks = ld_ub[i16 - 2] if i16 >= 2 else ld_base
                ld_tt = ld_ub[i16 - 1] if i16 >= 1 else ld_base
                kS = k_slab[:, t0 * 8:(t0 + 16) * 8]
                G("pe", lambda e, kS=kS: e.matmul(ps_int[:, 0:128], kS, kS,
                                                  start=True, stop=True),
                  waits=W("dve", "sc"))
                G("dve", lambda e: e.tensor_mul(
                    c_steps[:].rearrange("p (r f) -> p r f", f=128),
                    bmask[:].rearrange("p (r f) -> p r f", f=128),
                    ps_int[:, 0:128].unsqueeze(1).broadcast_to([128, 16, 128])),
                  waits=W("pe"), stall=True)
                for half in range(2):
                    hs = range(half * 4, half * 4 + 4)
                    def ksmms(e, hs=hs, i16=i16, t0=t0, half=half):
                        last = None
                        first = True
                        for hh in hs:
                            c0 = (hh % 4) * 128
                            last = e.matmul(ps_ks[0:16, c0:c0 + 128],
                                            hk(hh)[:, t0:t0 + 16],
                                            s_stb[:, hh * 128:(hh + 1) * 128],
                                            start=first, stop=False)
                            first = False
                            if i16 > 1:
                                last = e.matmul(ps_ks[0:16, c0:c0 + 128],
                                                gram[0:t0 - 16, hh * 128 + t0: hh * 128 + t0 + 16],
                                                u_big[0:t0 - 16, hh * 128:(hh + 1) * 128],
                                                start=False, stop=False)
                        last = e.matmul(ps_ks[0:16, 0:512],
                                        negident[:, t0:t0 + 16],
                                        vt_w[:, half * 512:(half + 1) * 512],
                                        start=False, stop=True)
                        return last
                    G("pe", ksmms,
                      waits=W("dve") + ([("ld", ld_ks)] if half == 0 else []))
                    G("dve", lambda e, half=half: e.tensor_copy(
                        stag[:, half * 512:(half + 1) * 512], ps_ks[0:16, 0:512]),
                      waits=W("pe"))
                dve_at_stag1 = list(W("dve"))
                def permin(e, i16=i16):
                    last = None
                    for hh in range(8):
                        last = e.matmul(ps_int[:, 128:256], pstack[:, hh * 128:(hh + 1) * 128],
                                        stag[:, hh * 128:(hh + 1) * 128],
                                        start=(hh == 0), stop=False)
                    return last
                G("pe", permin, waits=dve_at_stag1)
                def chainTT(s, i16=i16):
                    pb2 = 96 if s >= 12 else (64 if s >= 8 else (32 if s >= 4 else 0))
                    def f(e):
                        return e.tensor_mul(
                            u_slab[pb2:8 * s + 8, :],
                            db_sh[pb2:8 * s + 8, i16 * 128:(i16 + 1) * 128],
                            ps_int[pb2:8 * s + 8, 128:256])
                    return f
                G("dve", chainTT(0), waits=W("pe") + [("ld", ld_tt)], stall=True)
                for r16 in range(15):
                    pb = 64 if r16 >= 8 else (32 if r16 >= 4 else 0)
                    G("pe", lambda e, r16=r16, pb=pb: e.matmul(
                        ps_int[:, 128:256],
                        c_steps[pb:8 * r16 + 8, r16 * 128:(r16 + 1) * 128],
                        u_slab[pb:8 * r16 + 8, :],
                        start=False, stop=(r16 == 14)), waits=W("dve"))
                    G("dve", chainTT(r16 + 1), waits=W("pe"), stall=True)
                DMA("gp", dr["scr_ub"][:, t0 * 8:t0 * 8 + 128], u_slab[:],
                    waits=W("dve"))
                DMA("sy", u_big[t0:t0 + 16, :].rearrange("s (h v) -> s h v", v=128),
                    dr["scr_ub"][:, t0 * 8:t0 * 8 + 128]
                        .rearrange("(s h) v -> s h v", h=8),
                    waits=W("ld"))
                ld_ub[i16] = cnt["ld"]

            dump("u_big", lambda: u_big[:], p)
            def ocouple(e):
                last = None
                for hh in range(8):
                    last = e.matmul(ps_big[:, hh * 128:(hh + 1) * 128],
                                    atil[:, hh * 128:(hh + 1) * 128],
                                    u_big[:, hh * 128:(hh + 1) * 128],
                                    start=False, stop=True)
                return last
            G("pe", ocouple, waits=W("dve", "sc", "ld"))
            G("dve", lambda e: e.tensor_mul(o_tm[:], gin[:], ps_big[:]), waits=W("pe"))
            dump("o_tm", lambda: o_tm[:], p)

            def repl127(e):
                last = None
                for half in range(2):
                    last = e.matmul(ps_big[0:64, half * 512:(half + 1) * 512],
                                    oh127[:, 0:64], gin[:, half * 512:(half + 1) * 512],
                                    start=True, stop=True)
                return last
            G("pe", repl127, waits=W("dve"))
            G("dve", lambda e: e.tensor_copy(sigm[:], ps_big[0:64, :]), waits=W("pe"))
            for hp in range(2):
                def wmms(e, hp=hp):
                    last = None
                    for hh in range(4 * hp, 4 * hp + 4):
                        last = e.matmul(ps_a[0:64, (hh % 4) * 128:((hh % 4) + 1) * 128],
                                        k_tm[:, hh * 64:(hh + 1) * 64],
                                        u_big[:, hh * 128:(hh + 1) * 128],
                                        start=True, stop=True)
                    return last
                G("pe", wmms, waits=W("dve", "sc"))
                sl = slice(hp * 512, (hp + 1) * 512)
                G("dve", lambda e, sl=sl: e.tensor_sub(s_st[:, sl], s_st[:, sl],
                                                       ps_a[0:64, 0:512]), waits=W("pe"))
            G("dve", lambda e: e.tensor_mul(s_st[:], s_st[:], sigm[:]))
            G("dve", lambda e: e.tensor_copy(s_stb[:], s_st[:]))
            dump("s_st", lambda: s_st[:], p)

            G("sc", lambda e: e.square(scr[:], o_tm[:]), waits=W("dve"))
            def gstats(e):
                e.reduce_sum(stat[:, 0:8], o_tm[:].rearrange("p (a b) -> p a b", b=128), axis=AX.X)
                return e.reduce_sum(stat[:, 8:16], scr[:].rearrange("p (a b) -> p a b", b=128), axis=AX.X)
            G("dve", gstats, waits=W("sc"))
            def gstats2(e):
                e.tensor_scalar(stat[:, 0:8], stat[:, 0:8], 1.0 / DV, None, ALU.mult)
                e.tensor_scalar(stat[:, 8:16], stat[:, 8:16], 1.0 / DV, None, ALU.mult)
                e.tensor_mul(scr[:, 0:8], stat[:, 0:8], stat[:, 0:8])
                e.tensor_sub(stat[:, 8:16], stat[:, 8:16], scr[:, 0:8])
                return e.tensor_scalar(stat[:, 8:16], stat[:, 8:16], EPS_GN, None, ALU.add)
            G("dve", gstats2)
            G("sc", lambda e: e.activation(stat[:, 8:16], stat[:, 8:16], AF.Ln), waits=W("dve"))
            G("sc", lambda e: e.activation(stat[:, 8:16], stat[:, 8:16], AF.Exp, scale=-0.5))
            def onorm(e):
                o3 = o_tm[:].rearrange("p (a b) -> p a b", b=128)
                mu = stat[:, 0:8].unsqueeze(2).broadcast_to([128, 8, 128])
                rs = stat[:, 8:16].unsqueeze(2).broadcast_to([128, 8, 128])
                e.tensor_sub(o3, o3, mu)
                e.tensor_mul(o3, o3, rs)
                e.tensor_mul(o_tm[:].rearrange("p (a b) -> p a b", b=128),
                             o_tm[:].rearrange("p (a b) -> p a b", b=128),
                             lnw_rep[:].unsqueeze(1).broadcast_to([128, 8, 128]))
                e.tensor_add(o_tm[:].rearrange("p (a b) -> p a b", b=128),
                             o_tm[:].rearrange("p (a b) -> p a b", b=128),
                             lnb_rep[:].unsqueeze(1).broadcast_to([128, 8, 128]))
                return e.tensor_mul(o_tm[:], o_tm[:], g_tm[:])
            G("dve", onorm, waits=W("sc"))
            def ogtrs(e):
                last = None
                for j in range(8):
                    last = e.transpose(ps_big[:, j * 128:(j + 1) * 128],
                                       o_tm[:, j * 128:(j + 1) * 128], identf[:])
                return last
            G("pe", ogtrs, waits=W("dve", "sc"))
            G("dve", lambda e: e.tensor_copy(og_fm[:], ps_big[:]), waits=W("pe"), stall=True)
            def wfmms(e):
                last = None
                for half in range(2):
                    dst = ps_big[:, half * 512:(half + 1) * 512]
                    for kc in range(8):
                        last = e.matmul(dst, og_fm[:, kc * 128:(kc + 1) * 128],
                                        w_o[:, kc * 1024 + half * 512: kc * 1024 + half * 512 + 512],
                                        start=(kc == 0), stop=False)
                    last = e.matmul(dst, ones1[:], fbr[:, half * 512:(half + 1) * 512],
                                    start=False, stop=True)
                    dstf = ps_ub[:, half * 512:(half + 1) * 512]
                    for kc in range(8):
                        last = e.matmul(dstf, n_fm[:, kc * 128:(kc + 1) * 128],
                                        w_f[:, kc * 1024 + half * 512: kc * 1024 + half * 512 + 512],
                                        start=(kc == 0), stop=(kc == 7))
                return last
            G("pe", wfmms, waits=W("sc", "dve"))
            def resfin(e):
                e.tensor_add(res1[:], x_t[:], ps_big[:])
                return e.scalar_tensor_tensor(res1[:], ps_ub[:], stat[:, 22:23],
                                              res1[:], ALU.mult, ALU.add)
            G("dve", resfin, waits=W("pe"), stall=True)
            DMA("sy", dr["out"][r0:r0 + C, :], res1[:], waits=W("dve"), sem="st")
            dump("out_rows", lambda: res1[:], p)

        G("sy", lambda e: e.wait_ge(sems["st"], cnt["st"]) or None, waits=(), sem=None)

    for t in reversed(ctx):
        t.__exit__(None, None, None)
    return nc





_BUILD_CACHE = {}

def kernel(**inputs):
    import numpy as np
    from concourse.bass_utils import run_bass_kernel_spmd
    T = 8192
    x_full = np.asarray(inputs["L0_attn_x"])
    B = x_full.shape[0]
    g = host_prep(inputs)
    if T not in _BUILD_CACHE:
        _BUILD_CACHE[T] = build(T)
    nc = _BUILD_CACHE[T]
    in_maps = []
    for c in range(8):
        im = dict(g)
        im["x"] = np.ascontiguousarray(x_full[c % B])
        in_maps.append(im)
    res = run_bass_kernel_spmd(nc, in_maps, core_ids=list(range(8)))
    out = np.stack([np.asarray(res.results[b]["out"]) for b in range(B)], axis=0)
    return out.astype(np.float32)
